# version 05
# baseline (speedup 1.0000x reference)
import numpy as np
import concourse.bass as bass
import concourse.mybir as mybir
from concourse.bass_utils import run_bass_kernel_spmd
from contextlib import ExitStack

F32 = mybir.dt.float32
BF16 = mybir.dt.bfloat16
AF = mybir.ActivationFunctionType
ALU = mybir.AluOpType
AX = mybir.AxisListType

NCORES = 8
D = 1024
TOK = 4096
NT = TOK // 128
LN_EPS = 1e-5

IDENT = np.eye(128, dtype=np.float32)


class PB:

    def __init__(self):
        self.nc = bass.Bass("TRN2", target_bir_lowering=False)
        self.es = ExitStack()
        self._names = {}
        self.block = None
        self.g = None
        self.n = 0

    def _u(self, p):
        i = self._names.get(p, 0)
        self._names[p] = i + 1
        return f"{p}_{i}"

    def dram(self, name, shape, dtype=F32, kind="ExternalInput"):
        return self.nc.dram_tensor(name, shape, dtype, kind=kind).ap()

    def sb(self, prefix, shape, dtype=F32):
        return self.es.enter_context(self.nc.sbuf_tensor(self._u(prefix), shape, dtype))

    def ps(self, prefix, shape, dtype=F32):
        return self.es.enter_context(self.nc.psum_tensor(self._u(prefix), shape, dtype))

    def start(self):
        self.g = self.es.enter_context(self.nc.semaphore("g"))
        self.L = self.es.enter_context(self.nc.semaphore("L"))
        self.S = self.es.enter_context(self.nc.semaphore("S"))
        self.l = 0
        self.s = 0
        self.block = self.es.enter_context(self.nc.Block())

    def op(self, eng, f, inc=1, swait=None):
        need = self.n
        lneed = self.l
        sect = getattr(self.block, eng)

        def body(e):
            e.wait_ge(self.g, need)
            if lneed:
                e.wait_ge(self.L, lneed)
            if swait:
                e.wait_ge(self.S, swait)
            f(e).then_inc(self.g, inc)

        sect(body)
        self.n += inc

    def group(self, eng, fs):
        if not fs:
            return
        need = self.n
        lneed = self.l
        sect = getattr(self.block, eng)

        def body(e):
            e.wait_ge(self.g, need)
            if lneed:
                e.wait_ge(self.L, lneed)
            last = None
            for f in fs:
                last = f(e)
            last.then_inc(self.g, 1)

        sect(body)
        self.n += 1

    def load(self, dst, src, after_n=None):
        need = self.n if after_n is None else after_n
        lneed_prev = 0
        sect = self.block.sync

        def body(e):
            e.wait_ge(self.g, need)
            e.dma_start(out=dst, in_=src).then_inc(self.L, 16)

        sect(body)
        self.l += 16

    def store(self, dst, src):
        self.op("gpsimd", lambda e: e.dma_start(out=dst, in_=src), inc=16)

    def store_async(self, dst, src):
        need = self.n
        lneed = self.l

        def body(e):
            e.wait_ge(self.g, need)
            if lneed:
                e.wait_ge(self.L, lneed)
            e.dma_start(out=dst, in_=src).then_inc(self.S, 16)

        self.block.gpsimd(body)
        self.s += 16

    def drain_stores(self):
        sv = self.s

        def body(e):
            if sv:
                e.wait_ge(self.S, sv)
            e.nop().then_inc(self.g, 1)

        self.block.gpsimd(body)
        self.n += 1

    def mm(self, psum_ap, pairs, first_start=True, last_stop=True):
        fs = []
        keys = []
        for i, pr in enumerate(pairs):
            out = pr[2] if len(pr) > 2 and pr[2] is not None else psum_ap
            keys.append(str(out))
        seen = set()
        last = {}
        for i, k in enumerate(keys):
            last[k] = i
        for i, pr in enumerate(pairs):
            lhsT, rhs = pr[0], pr[1]
            out = pr[2] if len(pr) > 2 and pr[2] is not None else psum_ap
            st = first_start and (keys[i] not in seen)
            seen.add(keys[i])
            sp = last_stop and (last[keys[i]] == i)
            fs.append(lambda e, a=out, l=lhsT, r=rhs, s1=st, s2=sp:
                      e.matmul(a, l, r, start=s1, stop=s2))
        self.group("tensor", fs)

    def cast(self, dst, src):
        self.op("scalar", lambda e: e.copy(out=dst, in_=src))

    def tr(self, ps_tr, src_bf, dst, ident_bf):
        self.op("tensor", lambda e: e.transpose(ps_tr, src_bf, ident_bf))
        self.op("scalar", lambda e: e.copy(out=dst, in_=ps_tr))

    def tr_run(self, pairs, ps8, ident_bf, evac="copy"):
        self.group("tensor", [
            (lambda e, s=s_, i=i_: e.transpose(ps8[:, 128 * i:128 * (i + 1)], s, ident_bf))
            for i_, (s_, d_) in enumerate(pairs)])
        if evac == "copy":
            self.group("scalar", [
                (lambda e, d=d_, i=i_: e.copy(out=d, in_=ps8[:, 128 * i:128 * (i + 1)]))
                for i_, (s_, d_) in enumerate(pairs)])
        else:
            self.group("vector", [
                (lambda e, d=d_, i=i_: e.tensor_tensor(
                    out=d, in0=ps8[:, 128 * i:128 * (i + 1)], in1=d, op=ALU.add))
                for i_, (s_, d_) in enumerate(pairs)])

    def ln(self, x_ap, out_ap, stats, scr, Dd, eps=LN_EPS):
        o = self.op
        o("scalar", lambda e: e.activation(out=scr, in_=x_ap, func=AF.Square,
                                           accum_out=stats[:, 1:2]))
        o("vector", lambda e: e.reduce_sum(stats[:, 0:1], x_ap, axis=AX.X))
        o("scalar", lambda e: e.mul(out=stats[:, 2:3], in_=stats[:, 0:1], mul=1.0 / Dd))
        o("scalar", lambda e: e.mul(out=stats[:, 4:5], in_=stats[:, 1:2], mul=1.0 / Dd))
        o("vector", lambda e: e.tensor_mul(out=stats[:, 3:4], in0=stats[:, 2:3], in1=stats[:, 2:3]))
        o("vector", lambda e: e.tensor_sub(out=stats[:, 5:6], in0=stats[:, 4:5], in1=stats[:, 3:4]))
        o("scalar", lambda e: e.activation(out=stats[:, 6:7], in_=stats[:, 5:6], func=AF.Ln,
                                           bias=self.eps_t[:, 0:1]))
        o("scalar", lambda e: e.activation(out=stats[:, 7:8], in_=stats[:, 6:7], func=AF.Exp, scale=-0.5))
        o("scalar", lambda e: e.mul(out=stats[:, 8:9], in_=stats[:, 2:3], mul=-1.0))
        o("vector", lambda e: e.tensor_scalar(out=out_ap, in0=x_ap, scalar1=stats[:, 8:9],
                                              scalar2=stats[:, 7:8], op0=ALU.add, op1=ALU.mult))

    def vec_to_pcols(self, dram_vec, n, stage, ps_sc, out_tile):
        self.load(stage[0:n, :], dram_vec.rearrange("(j p) -> j p", p=128))
        self.op("tensor", lambda e: e.transpose(ps_sc[:, 0:n], stage[0:n, :], self.ident_f[0:n, 0:n]))
        self.op("scalar", lambda e: e.copy(out=out_tile[:, 0:n], in_=ps_sc[:, 0:n]))

    def bc_row(self, tile, dram_vec, Dd):
        self.load(tile[:, 0:Dd], dram_vec[None, :].broadcast_to([128, Dd]))

    def finish(self):
        nv, lv, sv = self.n, self.l, self.s

        def body(e):
            e.wait_ge(self.g, nv)
            if lv:
                e.wait_ge(self.L, lv)
            if sv:
                e.wait_ge(self.S, sv)
            e.nop()

        self.block.sync(body)
        self.es.close()
        return self.nc


def _setup_common(pb):
    ident_d = pb.dram("ident", [128, 128], F32)
    pb.ident_f = pb.sb("ident_f", [128, 128], F32)
    pb.ident_bf = pb.sb("ident_bf", [128, 128], BF16)
    pb.load(pb.ident_f[:, :], ident_d[:, :])
    pb.cast(pb.ident_bf[:, :], pb.ident_f[:, :])
    pb.eps_t = pb.sb("eps_t", [128, 2], F32)
    pb.op("vector", lambda e: e.memset(pb.eps_t[:, 0:1], LN_EPS))


def build_l1ffn():
    pb = PB()
    x_d = pb.dram("x", [TOK, D], F32)
    w1t_d = pb.dram("w1t", [D, 4096], F32)
    w2t_d = pb.dram("w2t", [D, 4096], F32)
    h1t_d = pb.dram("h1t", [4096, 4096], F32)
    h2t_d = pb.dram("h2t", [4096, 4096], F32)
    w3t_d = pb.dram("w3t", [4096, D], F32)
    s1_d = pb.dram("s1", [4096], F32)
    s2_d = pb.dram("s2", [4096], F32)
    hs1_d = pb.dram("hs1", [4096], F32)
    hs2_d = pb.dram("hs2", [4096], F32)
    s3_d = pb.dram("s3", [D], F32)
    y_d = pb.dram("y", [TOK, D], F32, kind="ExternalOutput")
    H0_d = pb.nc.dram_tensor("H0buf", [4096, TOK], BF16, kind="Internal").ap()
    H1_d = pb.nc.dram_tensor("H1buf", [4096, TOK], BF16, kind="Internal").ap()
    H2_d = pb.nc.dram_tensor("H2buf", [4096, TOK], BF16, kind="Internal").ap()

    BUF_A = pb.sb("bufA", [128, 32768], BF16)
    BUF_B = pb.sb("bufB", [128, 32768], BF16)
    stagw = pb.sb("stagw", [128, 4096], F32)
    stagw2 = pb.sb("stagw2", [128, 4096], F32)
    wc_bf = pb.sb("wc_bf", [128, 4096], BF16)
    wc_bf2 = pb.sb("wc_bf2", [128, 4096], BF16)
    stag2 = pb.sb("stag2", [128, 1024], F32)
    h0_bf2 = pb.sb("h0_bf2", [128, 512], BF16)
    u1_bf2 = pb.sb("u1_bf2", [128, 512], BF16)
    x_t = pb.sb("x_t", [128, D], F32)
    scr = pb.sb("scr", [128, D], F32)
    scr5 = pb.sb("scr5", [128, 512], F32)
    xn_bf = pb.sb("xn_bf", [128, D], BF16)
    u1_bf = pb.sb("u1_bf", [128, 512], BF16)
    h0_bf = pb.sb("h0_bf", [128, 512], BF16)
    stats = pb.sb("stats", [128, 16], F32)
    svst = pb.sb("svst", [32, 128], F32)
    s1_t = pb.sb("s1_t", [128, 32], F32)
    s2_t = pb.sb("s2_t", [128, 32], F32)
    hs1_t = pb.sb("hs1_t", [128, 32], F32)
    hs2_t = pb.sb("hs2_t", [128, 32], F32)
    s3_bc = pb.sb("s3_bc", [128, D], F32)

    psA = pb.ps("psA", [128, 512], F32)
    psB = pb.ps("psB", [128, 512], F32)
    psD = pb.ps("psD", [128, 1024], F32)
    ps_tr = pb.ps("ps_tr", [128, 1024], BF16)
    ps_sc = pb.ps("ps_sc", [128, 32], F32)

    pb.start()
    _setup_common(pb)
    pb.vec_to_pcols(s1_d, 32, svst, ps_sc, s1_t)
    pb.vec_to_pcols(s2_d, 32, svst, ps_sc, s2_t)
    pb.vec_to_pcols(hs1_d, 32, svst, ps_sc, hs1_t)
    pb.vec_to_pcols(hs2_d, 32, svst, ps_sc, hs2_t)
    pb.bc_row(s3_bc, s3_d, D)

    def xnT(c):
        return BUF_A[:, 4096 * c:4096 * (c + 1)]

    for k in range(NT):
        pb.load(x_t[:, :], x_d[128 * k:128 * (k + 1), :])
        pb.ln(x_t[:, :], xn_bf[:, :], stats, scr[:, :], D)
        pb.tr_run([(xn_bf[:, 128 * c:128 * (c + 1)], xnT(c)[:, 128 * k:128 * (k + 1)])
                   for c in range(8)], ps_tr, pb.ident_bf[:, :])

    w1v = w1t_d.rearrange("(c p) f -> p c f", p=128)
    w2v = w2t_d.rearrange("(c p) f -> p c f", p=128)
    STG = [stagw, stagw2]
    WCB = [wc_bf, wc_bf2]
    H0R = [h0_bf, h0_bf2]

    def ph1_loads(j, bi, after_n):
        pb.load(STG[bi][:, 0:1024].rearrange("p (c f) -> p c f", c=8, f=128),
                w1v[:, :, 128 * j:128 * (j + 1)], after_n=after_n)
        pb.load(STG[bi][:, 1024:2048].rearrange("p (c f) -> p c f", c=8, f=128),
                w2v[:, :, 128 * j:128 * (j + 1)], after_n=after_n)

    ph1_loads(0, 0, pb.n)
    for j in range(32):
        bi = j % 2
        pb.cast(WCB[bi][:, 0:2048], STG[bi][:, 0:2048])
        ck = pb.n
        for tb in range(8):
            ts = slice(512 * tb, 512 * (tb + 1))
            pb.mm(psA[:, :], [(WCB[bi][:, 128 * c:128 * (c + 1)], xnT(c)[:, ts]) for c in range(8)])
            pb.mm(psB[:, :], [(WCB[bi][:, 1024 + 128 * c:1024 + 128 * (c + 1)], xnT(c)[:, ts]) for c in range(8)])
            pb.op("scalar", lambda e, j=j: e.activation(out=u1_bf[:, :], in_=psA[:, :],
                                                        func=AF.Silu, scale=s1_t[:, j:j + 1]))
            pb.op("scalar", lambda e, j=j: e.activation(out=scr5[:, :], in_=psB[:, :],
                                                        func=AF.Copy, scale=s2_t[:, j:j + 1]))
            hb = H0R[tb % 2]
            pb.op("vector", lambda e, hb=hb: e.tensor_mul(out=hb[:, :], in0=scr5[:, :], in1=u1_bf[:, :]),
                  swait=max(0, pb.s - 16))
            pb.store_async(H0_d[128 * j:128 * (j + 1), ts], hb[:, :])
        if j + 1 < 32:
            ph1_loads(j + 1, 1 - bi, ck)

    U1R = [u1_bf, u1_bf2]
    for (ht_d, hs_t, src_d, dst_d) in ((h1t_d, hs1_t, H0_d, H1_d),
                                       (h2t_d, hs2_t, H1_d, H2_d)):
        hv = ht_d.rearrange("(c p) f -> p c f", p=128)

        def w_loads(j, bi, after_n, hv=hv):
            for q in range(4):
                pb.load(STG[bi][:, 1024 * q:1024 * (q + 1)].rearrange("p (c f) -> p c f", c=8, f=128),
                        hv[:, 8 * q:8 * (q + 1), 128 * j:128 * (j + 1)], after_n=after_n)

        for tb in range(2):
            tbs = slice(2048 * tb, 2048 * (tb + 1))
            for c in range(32):
                dst = (BUF_A if c < 16 else BUF_B)
                cc = c % 16
                pb.load(dst[:, 2048 * cc:2048 * (cc + 1)], src_d[128 * c:128 * (c + 1), tbs])
            w_loads(0, 0, pb.n)
            for j in range(32):
                bi = j % 2
                pb.cast(WCB[bi][:, :], STG[bi][:, :])
                ck = pb.n
                for h in range(4):
                    hsl = slice(512 * h, 512 * (h + 1))
                    pairs = []
                    for c in range(32):
                        srcb = (BUF_A if c < 16 else BUF_B)
                        cc = c % 16
                        pairs.append((WCB[bi][:, 128 * c:128 * (c + 1)],
                                      srcb[:, 2048 * cc:2048 * cc + 2048][:, hsl]))
                    pb.mm(psA[:, :], pairs)
                    ub = U1R[h % 2]
                    pb.op("scalar", lambda e, j=j, hs_t=hs_t, ub=ub: e.activation(
                        out=ub[:, :], in_=psA[:, :], func=AF.Gelu, scale=hs_t[:, j:j + 1]),
                        swait=max(0, pb.s - 16))
                    pb.store_async(dst_d[128 * j:128 * (j + 1), tbs][:, hsl], ub[:, :])
                if j + 1 < 32:
                    w_loads(j + 1, 1 - bi, ck)

    w3v = w3t_d.rearrange("(c p) f -> p c f", p=128)
    for tb in range(4):
        tbs = slice(1024 * tb, 1024 * (tb + 1))
        for c in range(32):
            pb.load(BUF_A[:, 1024 * c:1024 * (c + 1)], H2_d[128 * c:128 * (c + 1), tbs])
        for c in range(32):
            pb.load(stag2[:, :], w3v[:, c, :])
            pb.cast(BUF_B[:, 1024 * c:1024 * (c + 1)], stag2[:, :])
        for k in range(8):
            ksl = slice(128 * k, 128 * (k + 1))
            pairs = []
            for c in range(32):
                for half in range(2):
                    pairs.append((BUF_A[:, 1024 * c:1024 * (c + 1)][:, ksl],
                                  BUF_B[:, 1024 * c + 512 * half:1024 * c + 512 * (half + 1)],
                                  psD[:, 512 * half:512 * (half + 1)]))
            pb.mm(None, pairs)
            gk = 128 * (8 * tb + k)
            pb.load(x_t[:, :], x_d[gk:gk + 128, :])
            pb.op("vector", lambda e: e.tensor_mul(out=scr[:, :], in0=psD[:, :], in1=s3_bc[:, :]))
            pb.op("vector", lambda e: e.tensor_add(out=scr[:, :], in0=scr[:, :], in1=x_t[:, :]))
            pb.store(y_d[gk:gk + 128, :], scr[:, :])

    pb.finish()
    return pb.nc


def build_l0ffn():
    pb = PB()
    x_d = pb.dram("x", [TOK, D], F32)
    wgt_d = pb.dram("wgt", [D, 8], F32)
    wfct_d = pb.dram("wfct", [8, D, 512], F32)
    wpjt_d = pb.dram("wpjt", [8, 512, D], F32)
    y_d = pb.dram("y", [TOK, D], F32, kind="ExternalOutput")

    WFC = pb.sb("WFC", [128, 32768], BF16)
    WPJ = pb.sb("WPJ", [128, 32768], BF16)
    stag = pb.sb("stag", [128, 4096], F32)
    XT = pb.sb("XT", [128, 4096], BF16)
    HT = pb.sb("HT", [128, 2048], BF16)
    wg_bf = pb.sb("wg_bf", [128, 64], BF16)
    x_t = pb.sb("x_t", [128, D], F32)
    scr = pb.sb("scr", [128, D], F32)
    xn_bf = pb.sb("xn_bf", [128, D], BF16)
    stats = pb.sb("stats", [128, 16], F32)
    Y = [pb.sb("Y", [128, D], F32) for _ in range(4)]
    G4 = pb.sb("G4", [128, 32], F32)
    rt = pb.sb("rt", [128, 8], F32)
    eq1 = pb.sb("eq1", [128, 8], F32)
    eq2 = pb.sb("eq2", [128, 8], F32)
    tmp8 = pb.sb("tmp8", [128, 8], F32)
    g1 = pb.sb("g1", [128, 8], F32)

    psH = pb.ps("psH", [128, 512], F32)
    psY = pb.ps("psY", [128, 1024], F32)
    psG = pb.ps("psG", [128, 8], F32)
    ps_tr = pb.ps("ps_tr", [128, 1024], BF16)

    pb.start()
    _setup_common(pb)
    pb.load(stag[:, 0:64].rearrange("p (c f) -> p c f", c=8, f=8),
            wgt_d.rearrange("(c p) f -> p c f", p=128))
    pb.cast(wg_bf[:, :], stag[:, 0:64])
    for e in range(8):
        pb.load(stag.rearrange("p (c f) -> p c f", c=8, f=512),
                wfct_d[e].rearrange("(c p) f -> p c f", p=128))
        pb.cast(WFC[:, 4096 * e:4096 * (e + 1)], stag[:, :])
        pb.load(stag.rearrange("p (c f) -> p c f", c=4, f=1024),
                wpjt_d[e].rearrange("(c p) f -> p c f", p=128))
        pb.cast(WPJ[:, 4096 * e:4096 * (e + 1)], stag[:, 0:4096])

    for b in range(8):
        for kt in range(4):
            k = 4 * b + kt
            pb.load(x_t[:, :], x_d[128 * k:128 * (k + 1), :])
            pb.ln(x_t[:, :], xn_bf[:, :], stats, scr[:, :], D)
            pb.tr_run([(xn_bf[:, 128 * c:128 * (c + 1)],
                        XT[:, 512 * c + 128 * kt:512 * c + 128 * (kt + 1)])
                       for c in range(8)], ps_tr, pb.ident_bf[:, :])
            pb.mm(psG[:, :], [(XT[:, 512 * c + 128 * kt:512 * c + 128 * (kt + 1)],
                               wg_bf[:, 8 * c:8 * (c + 1)]) for c in range(8)])
            pb.op("vector", lambda e: e.reduce_max(rt[:, 0:1], psG[:, :], axis=AX.X))
            pb.op("vector", lambda e: e.tensor_scalar(out=eq1[:, :], in0=psG[:, :], scalar1=rt[:, 0:1],
                                                      scalar2=None, op0=ALU.is_equal))
            pb.op("scalar", lambda e: e.mul(out=tmp8[:, :], in_=eq1[:, :], mul=-1e30))
            pb.op("vector", lambda e: e.tensor_add(out=tmp8[:, :], in0=psG[:, :], in1=tmp8[:, :]))
            pb.op("vector", lambda e: e.reduce_max(rt[:, 1:2], tmp8[:, :], axis=AX.X))
            pb.op("vector", lambda e: e.tensor_scalar(out=eq2[:, :], in0=tmp8[:, :], scalar1=rt[:, 1:2],
                                                      scalar2=None, op0=ALU.is_equal))
            pb.op("vector", lambda e: e.tensor_sub(out=rt[:, 2:3], in0=rt[:, 0:1], in1=rt[:, 1:2]))
            pb.op("scalar", lambda e: e.activation(out=rt[:, 3:4], in_=rt[:, 2:3], func=AF.Sigmoid))
            pb.op("scalar", lambda e: e.activation(out=rt[:, 4:5], in_=rt[:, 2:3], func=AF.Sigmoid, scale=-1.0))
            pb.op("vector", lambda e: e.tensor_scalar(out=g1[:, :], in0=eq1[:, :], scalar1=rt[:, 3:4],
                                                      scalar2=None, op0=ALU.mult))
            pb.op("vector", lambda e, kt=kt: e.tensor_scalar(out=G4[:, 8 * kt:8 * (kt + 1)], in0=eq2[:, :],
                                                             scalar1=rt[:, 4:5], scalar2=None, op0=ALU.mult))
            pb.op("vector", lambda e, kt=kt: e.tensor_add(out=G4[:, 8 * kt:8 * (kt + 1)],
                                                          in0=G4[:, 8 * kt:8 * (kt + 1)], in1=g1[:, :]))
            pb.cast(Y[kt][:, :], x_t[:, :])
        for e_ in range(8):
            for i in range(4):
                pb.mm(psH[:, :], [(WFC[:, (8 * e_ + c) * 512 + 128 * i:(8 * e_ + c) * 512 + 128 * (i + 1)],
                                   XT[:, 512 * c:512 * (c + 1)]) for c in range(8)])
                pb.op("scalar", lambda e, i=i: e.activation(out=HT[:, 512 * i:512 * (i + 1)],
                                                            in_=psH[:, :], func=AF.Gelu))
            for kt in range(4):
                pairs = []
                for i in range(4):
                    for half in range(2):
                        pairs.append((HT[:, 512 * i + 128 * kt:512 * i + 128 * (kt + 1)],
                                      WPJ[:, (4 * e_ + i) * 1024 + 512 * half:(4 * e_ + i) * 1024 + 512 * (half + 1)],
                                      psY[:, 512 * half:512 * (half + 1)]))
                pb.mm(None, pairs)
                pb.op("vector", lambda e, kt=kt, e_=e_: e.scalar_tensor_tensor(
                    out=Y[kt][:, :], in0=psY[:, :], scalar=G4[:, 8 * kt + e_:8 * kt + e_ + 1],
                    in1=Y[kt][:, :], op0=ALU.mult, op1=ALU.add))
        for kt in range(4):
            k = 4 * b + kt
            pb.store(y_d[128 * k:128 * (k + 1), :], Y[kt][:, :])

    pb.finish()
    return pb.nc


def build_l1ab():
    pb = PB()
    x_d = pb.dram("x", [TOK, D], F32)
    wq_d = pb.dram("wq", [D, D], F32)
    wk_d = pb.dram("wk", [D, D], F32)
    wv_d = pb.dram("wv", [D, D], F32)
    wo_d = pb.dram("wo", [D, D], F32)
    bq_d = pb.dram("bq", [D], F32)
    bk_d = pb.dram("bk", [D], F32)
    bv_d = pb.dram("bv", [D], F32)
    bo_d = pb.dram("bo", [D], F32)
    y_d = pb.dram("y", [TOK, D], F32, kind="ExternalOutput")

    S = 2048
    XT = pb.sb("XT", [128, 8 * S], BF16)
    QT = pb.sb("QT", [128, 8 * S], BF16)
    KT = pb.sb("KT", [128, 8 * S], BF16)
    VA = pb.sb("VA", [128, 16 * 1040], BF16)
    PT = pb.sb("PT", [128, 1024], BF16)
    WB = pb.sb("WB", [128, 8192], BF16)
    stag = pb.sb("stag", [128, 8192], F32)
    x_t = pb.sb("x_t", [128, D], F32)
    scr = pb.sb("scr", [128, D], F32)
    xn_bf = pb.sb("xn_bf", [128, D], BF16)
    stats = pb.sb("stats", [128, 16], F32)
    svst = pb.sb("svst", [32, 128], F32)
    bq_t = pb.sb("bq_t", [128, 8], F32)
    bk_t = pb.sb("bk_t", [128, 8], F32)
    bv_bc = pb.sb("bv_bc", [128, D], F32)
    bo_bc = pb.sb("bo_bc", [128, D], F32)
    ones_bf = pb.sb("ones_bf", [1, 128], BF16)
    dn = pb.sb("dn", [1, 1024], F32)
    dn_bf = pb.sb("dn_bf", [1, 1024], BF16)
    sc64 = pb.sb("sc64", [64, 1024], BF16)

    psS = pb.ps("psS", [128, 1024], F32)
    psN = pb.ps("psN", [65, 1024], F32)
    psP = pb.ps("psP", [128, 1024], F32)
    ps_tr = pb.ps("ps_tr", [128, 1024], BF16)
    ps_sc = pb.ps("ps_sc", [128, 32], F32)

    pb.start()
    _setup_common(pb)
    pb.vec_to_pcols(bq_d, 8, svst, ps_sc, bq_t)
    pb.vec_to_pcols(bk_d, 8, svst, ps_sc, bk_t)
    pb.bc_row(bv_bc, bv_d, D)
    pb.bc_row(bo_bc, bo_d, D)
    pb.op("vector", lambda e: e.memset(ones_bf[:, :], 1.0))

    def load_w(w_d):
        pb.load(stag.rearrange("p (c f) -> p c f", c=8, f=1024),
                w_d.rearrange("(c p) f -> p c f", p=128))
        pb.cast(WB[:, :], stag[:, :])

    for bb in range(2):
        xoff = 2048 * bb
        pb.op("vector", lambda e: e.memset(VA[:, :], 1.0))
        for k in range(16):
            pb.load(x_t[:, :], x_d[xoff + 128 * k:xoff + 128 * (k + 1), :])
            pb.ln(x_t[:, :], xn_bf[:, :], stats, scr[:, :], D)
            pb.tr_run([(xn_bf[:, 128 * c:128 * (c + 1)],
                        XT[:, S * c + 128 * k:S * c + 128 * (k + 1)])
                       for c in range(8)], ps_tr, pb.ident_bf[:, :])
        for (w_d, b_t, DST) in ((wq_d, bq_t, QT), (wk_d, bk_t, KT)):
            load_w(w_d)
            for j in range(8):
                for tq in range(4):
                    pb.mm(psP[:, 0:512],
                          [(WB[:, 1024 * c + 128 * j:1024 * c + 128 * (j + 1)],
                            XT[:, S * c + 512 * tq:S * c + 512 * (tq + 1)]) for c in range(8)])
                    pb.op("vector", lambda e, j=j, tq=tq, DST=DST, b_t=b_t: e.tensor_scalar(
                        out=DST[:, S * j + 512 * tq:S * j + 512 * (tq + 1)], in0=psP[:, 0:512],
                        scalar1=b_t[:, j:j + 1], scalar2=None, op0=ALU.add))
        load_w(wv_d)
        for k in range(16):
            pairs = []
            for c in range(8):
                for half in range(2):
                    pairs.append((XT[:, S * c + 128 * k:S * c + 128 * (k + 1)],
                                  WB[:, 1024 * c + 512 * half:1024 * c + 512 * (half + 1)],
                                  psP[:, 512 * half:512 * (half + 1)]))
            pb.mm(None, pairs)
            vk = VA[:, 1040 * k:1040 * (k + 1)].rearrange("p (h w) -> p h w", h=16, w=65)[:, :, 0:64]
            pb.op("vector", lambda e, vk=vk: e.tensor_tensor(
                out=vk, in0=psP.rearrange("p (h w) -> p h w", h=16, w=64),
                in1=bv_bc.rearrange("p (h w) -> p h w", h=16, w=64), op=ALU.add))
        AT = XT
        for h in range(16):
            hb = (h % 2) * 64
            j = h // 2
            for qh in range(2):
                qo = 1024 * qh
                for kt in range(16):
                    pb.mm(psS[:, :],
                          [(KT[hb:hb + 64, S * j + 128 * kt:S * j + 128 * (kt + 1)],
                            QT[hb:hb + 64, S * j + qo + 512 * tq:S * j + qo + 512 * (tq + 1)],
                            psS[:, 512 * tq:512 * (tq + 1)]) for tq in range(2)])
                    pb.op("scalar", lambda e: e.activation(out=PT[:, :], in_=psS[:, :],
                                                           func=AF.Exp, scale=0.125))
                    pb.mm(None,
                          [(VA[:, 1040 * kt + 65 * h:1040 * kt + 65 * (h + 1)],
                            PT[:, 512 * tq:512 * (tq + 1)],
                            psN[:, 512 * tq:512 * (tq + 1)]) for tq in range(2)],
                          first_start=(kt == 0), last_stop=(kt == 15))
                pb.op("vector", lambda e: e.reciprocal(dn[:, :], psN[64:65, :]))
                pb.cast(dn_bf[:, :], dn[:, :])
                pb.mm(psS[0:64, 0:1024], [(ones_bf[:, 0:64], dn_bf[:, 512 * tq:512 * (tq + 1)],
                                           psS[0:64, 512 * tq:512 * (tq + 1)]) for tq in range(2)])
                pb.cast(sc64[:, :], psN[0:64, :])
                pb.op("vector", lambda e, j=j, hb=hb, qo=qo: e.tensor_tensor(
                    out=AT[hb:hb + 64, S * j + qo:S * j + qo + 1024], in0=sc64[:, :],
                    in1=psS[0:64, 0:1024], op=ALU.mult))
        load_w(wo_d)
        for k in range(16):
            pairs = []
            for c in range(8):
                for half in range(2):
                    pairs.append((AT[:, S * c + 128 * k:S * c + 128 * (k + 1)],
                                  WB[:, 1024 * c + 512 * half:1024 * c + 512 * (half + 1)],
                                  psP[:, 512 * half:512 * (half + 1)]))
            pb.mm(None, pairs)
            pb.load(x_t[:, :], x_d[xoff + 128 * k:xoff + 128 * (k + 1), :])
            pb.op("vector", lambda e: e.tensor_add(out=scr[:, :], in0=psP[:, :], in1=bo_bc[:, :]))
            pb.op("vector", lambda e: e.tensor_add(out=scr[:, :], in0=scr[:, :], in1=x_t[:, :]))
            pb.store(y_d[xoff + 128 * k:xoff + 128 * (k + 1), :], scr[:, :])

    pb.finish()
    return pb.nc


def build_l0aa():
    pb = PB()
    S = 2048
    x_d = pb.dram("x", [TOK, D], F32)
    wdq_d = pb.dram("wdq", [D, 512], F32)
    wuq_d = pb.dram("wuq", [512, D], F32)
    wqr_d = pb.dram("wqr", [512, D], F32)
    wkr_d = pb.dram("wkr", [D, D], F32)
    wdkv_d = pb.dram("wdkv", [D, 512], F32)
    wuk_d = pb.dram("wuk", [512, D], F32)
    wuv_d = pb.dram("wuv", [512, D], F32)
    wo_d = pb.dram("wo", [D, D], F32)
    bdq_d = pb.dram("bdq", [512], F32)
    buq_d = pb.dram("buq", [D], F32)
    bqr_d = pb.dram("bqr", [D], F32)
    bkr_d = pb.dram("bkr", [D], F32)
    bdkv_d = pb.dram("bdkv", [512], F32)
    buk_d = pb.dram("buk", [D], F32)
    buv_d = pb.dram("buv", [D], F32)
    bo_d = pb.dram("bo", [D], F32)
    cos_d = pb.dram("cost", [S, 512], F32)
    sin_d = pb.dram("sint", [S, 512], F32)
    y_d = pb.dram("y", [TOK, D], F32, kind="ExternalOutput")

    XT = pb.sb("XT", [128, 8 * S], BF16)
    CC = pb.sb("CC", [128, 8 * S], BF16)
    QT = pb.sb("QT", [128, 8 * S], BF16)
    KT = pb.sb("KT", [128, 8 * S], BF16)
    VA = pb.sb("VA", [128, 16 * 1040], BF16)
    PT = pb.sb("PT", [128, 1024], BF16)
    WB = pb.sb("WB", [128, 8192], BF16)
    stag = pb.sb("stag", [128, 1024], F32)
    x_t = pb.sb("x_t", [128, D], F32)
    scr = pb.sb("scr", [128, D], F32)
    qro = pb.sb("qro", [128, D], F32)
    cos_tl = pb.sb("cos_tl", [128, 512], F32)
    sin_tl = pb.sb("sin_tl", [128, 512], F32)
    tmp5 = pb.sb("tmp5", [128, 512], F32)
    xn_bf = pb.sb("xn_bf", [128, D], BF16)
    stats = pb.sb("stats", [128, 16], F32)
    svst = stag[0:32, 0:128]
    bdq_t = pb.sb("bdq_t", [128, 4], F32)
    bdkv_t = pb.sb("bdkv_t", [128, 4], F32)
    buq_t = pb.sb("buq_t", [128, 8], F32)
    buk_t = pb.sb("buk_t", [128, 8], F32)
    b_bc = pb.sb("b_bc", [128, D], F32)
    bqr_bc = bkr_bc = buv_bc = bo_bc = b_bc
    ones_bf = pb.sb("ones_bf", [1, 128], BF16)
    dn = stag[0:1, 0:1024]
    dn_bf = XT[0:1, 0:1024]
    sc64 = XT[0:64, 2048:3072]

    psS = pb.ps("psS", [128, 1024], F32)
    psN = pb.ps("psN", [65, 1024], F32)
    psP = pb.ps("psP", [128, 1024], F32)
    ps_tr = pb.ps("ps_tr", [128, 1024], BF16)
    ps_sc = pb.ps("ps_sc", [128, 32], F32)

    pb.start()
    _setup_common(pb)
    pb.vec_to_pcols(bdq_d, 4, svst, ps_sc, bdq_t)
    pb.vec_to_pcols(bdkv_d, 4, svst, ps_sc, bdkv_t)
    pb.vec_to_pcols(buq_d, 8, svst, ps_sc, buq_t)
    pb.vec_to_pcols(buk_d, 8, svst, ps_sc, buk_t)
    pb.op("vector", lambda e: e.memset(ones_bf[:, :], 1.0))

    def load_w(w_d, rows, cols):
        nc_ = rows // 128
        for c in range(nc_):
            pb.load(stag[:, 0:cols], w_d[128 * c:128 * (c + 1), :])
            pb.cast(WB[:, cols * c:cols * (c + 1)], stag[:, 0:cols])

    def rope(ps_src, bias_bc, out_f32):
        pb.op("vector", lambda e: e.tensor_add(out=qro[:, :], in0=ps_src, in1=bias_bc[:, :]))
        o = pb.op
        o("vector", lambda e: e.tensor_mul(out=out_f32[:, 0:D:2], in0=qro[:, 0:D:2], in1=cos_tl[:, :]))
        o("vector", lambda e: e.tensor_mul(out=tmp5[:, :], in0=qro[:, 1:D:2], in1=sin_tl[:, :]))
        o("vector", lambda e: e.tensor_sub(out=out_f32[:, 0:D:2], in0=out_f32[:, 0:D:2], in1=tmp5[:, :]))
        o("vector", lambda e: e.tensor_mul(out=out_f32[:, 1:D:2], in0=qro[:, 1:D:2], in1=cos_tl[:, :]))
        o("vector", lambda e: e.tensor_mul(out=tmp5[:, :], in0=qro[:, 0:D:2], in1=sin_tl[:, :]))
        o("vector", lambda e: e.tensor_add(out=out_f32[:, 1:D:2], in0=out_f32[:, 1:D:2], in1=tmp5[:, :]))

    CQT = CC[:, 0:4 * S]
    CKT = CC[:, 4 * S:8 * S]

    for bb in range(2):
        xoff = 2048 * bb
        pb.op("vector", lambda e: e.memset(VA[:, :], 1.0))
        for k in range(16):
            pb.load(x_t[:, :], x_d[xoff + 128 * k:xoff + 128 * (k + 1), :])
            pb.ln(x_t[:, :], xn_bf[:, :], stats, scr[:, :], D)
            pb.tr_run([(xn_bf[:, 128 * c:128 * (c + 1)],
                        XT[:, S * c + 128 * k:S * c + 128 * (k + 1)])
                       for c in range(8)], ps_tr, pb.ident_bf[:, :])
        for (w_d, b_t, DSTb) in ((wdq_d, bdq_t, CQT), (wdkv_d, bdkv_t, CKT)):
            load_w(w_d, D, 512)
            for j in range(4):
                for tq in range(4):
                    pb.mm(psP[:, 0:512],
                          [(WB[:, 512 * c + 128 * j:512 * c + 128 * (j + 1)],
                            XT[:, S * c + 512 * tq:S * c + 512 * (tq + 1)]) for c in range(8)])
                    pb.op("vector", lambda e, j=j, tq=tq, DSTb=DSTb, b_t=b_t: e.tensor_scalar(
                        out=DSTb[:, S * j + 512 * tq:S * j + 512 * (tq + 1)], in0=psP[:, 0:512],
                        scalar1=b_t[:, j:j + 1], scalar2=None, op0=ALU.add))
        for (w_d, b_t, SRCb, DSTb) in ((wuq_d, buq_t, CQT, QT), (wuk_d, buk_t, CKT, KT)):
            load_w(w_d, 512, D)
            for j in range(8):
                for tq in range(4):
                    pb.mm(psP[:, 0:512],
                          [(WB[:, 1024 * c + 128 * j:1024 * c + 128 * (j + 1)],
                            SRCb[:, S * c + 512 * tq:S * c + 512 * (tq + 1)]) for c in range(4)])
                    pb.op("vector", lambda e, j=j, tq=tq, DSTb=DSTb, b_t=b_t: e.tensor_scalar(
                        out=DSTb[:, S * j + 512 * tq:S * j + 512 * (tq + 1)], in0=psP[:, 0:512],
                        scalar1=b_t[:, j:j + 1], scalar2=None, op0=ALU.add))
        for (w_d, b_d_, SRCb, nsrc, DSTb) in ((wqr_d, bqr_d, CQT, 4, QT),
                                              (wkr_d, bkr_d, XT, 8, KT)):
            pb.bc_row(b_bc, b_d_, D)
            load_w(w_d, 128 * nsrc, D)
            for k in range(16):
                pb.load(cos_tl[:, :], cos_d[128 * k:128 * (k + 1), :])
                pb.load(sin_tl[:, :], sin_d[128 * k:128 * (k + 1), :])
                pairs = []
                for c in range(nsrc):
                    for half in range(2):
                        pairs.append((SRCb[:, S * c + 128 * k:S * c + 128 * (k + 1)],
                                      WB[:, 1024 * c + 512 * half:1024 * c + 512 * (half + 1)],
                                      psP[:, 512 * half:512 * (half + 1)]))
                pb.mm(None, pairs)
                rope(psP[:, :], b_bc, scr)
                pb.cast(xn_bf[:, :], scr[:, :])
                pb.tr_run([(xn_bf[:, 128 * c:128 * (c + 1)],
                            DSTb[:, S * c + 128 * k:S * c + 128 * (k + 1)])
                           for c in range(8)], ps_tr, pb.ident_bf[:, :], evac="add")
        pb.bc_row(b_bc, buv_d, D)
        load_w(wuv_d, 512, D)
        for k in range(16):
            pairs = []
            for c in range(4):
                for half in range(2):
                    pairs.append((CKT[:, S * c + 128 * k:S * c + 128 * (k + 1)],
                                  WB[:, 1024 * c + 512 * half:1024 * c + 512 * (half + 1)],
                                  psP[:, 512 * half:512 * (half + 1)]))
            pb.mm(None, pairs)
            vk = VA[:, 1040 * k:1040 * (k + 1)].rearrange("p (h w) -> p h w", h=16, w=65)[:, :, 0:64]
            pb.op("vector", lambda e, vk=vk: e.tensor_tensor(
                out=vk, in0=psP.rearrange("p (h w) -> p h w", h=16, w=64),
                in1=buv_bc.rearrange("p (h w) -> p h w", h=16, w=64), op=ALU.add))
        AT = CC
        esc = 1.0 / float(np.sqrt(np.float32(16 + 512 + 512)))
        for h in range(16):
            hb = (h % 2) * 64
            j = h // 2
            for qh in range(2):
                qo = 1024 * qh
                for kt in range(16):
                    pb.mm(psS[:, :],
                          [(KT[hb:hb + 64, S * j + 128 * kt:S * j + 128 * (kt + 1)],
                            QT[hb:hb + 64, S * j + qo + 512 * tq:S * j + qo + 512 * (tq + 1)],
                            psS[:, 512 * tq:512 * (tq + 1)]) for tq in range(2)])
                    pb.op("scalar", lambda e: e.activation(out=PT[:, :], in_=psS[:, :],
                                                           func=AF.Exp, scale=esc))
                    pb.mm(None,
                          [(VA[:, 1040 * kt + 65 * h:1040 * kt + 65 * (h + 1)],
                            PT[:, 512 * tq:512 * (tq + 1)],
                            psN[:, 512 * tq:512 * (tq + 1)]) for tq in range(2)],
                          first_start=(kt == 0), last_stop=(kt == 15))
                pb.op("vector", lambda e: e.reciprocal(dn, psN[64:65, :]))
                pb.cast(dn_bf, dn)
                pb.mm(psS[0:64, 0:1024], [(ones_bf[:, 0:64], dn_bf[0:1, 512 * tq:512 * (tq + 1)],
                                           psS[0:64, 512 * tq:512 * (tq + 1)]) for tq in range(2)])
                pb.cast(sc64, psN[0:64, :])
                pb.op("vector", lambda e, j=j, hb=hb, qo=qo: e.tensor_tensor(
                    out=AT[hb:hb + 64, S * j + qo:S * j + qo + 1024], in0=sc64,
                    in1=psS[0:64, 0:1024], op=ALU.mult))
        pb.bc_row(b_bc, bo_d, D)
        load_w(wo_d, D, D)
        for k in range(16):
            pairs = []
            for c in range(8):
                for half in range(2):
                    pairs.append((AT[:, S * c + 128 * k:S * c + 128 * (k + 1)],
                                  WB[:, 1024 * c + 512 * half:1024 * c + 512 * (half + 1)],
                                  psP[:, 512 * half:512 * (half + 1)]))
            pb.mm(None, pairs)
            pb.load(x_t[:, :], x_d[xoff + 128 * k:xoff + 128 * (k + 1), :])
            pb.op("vector", lambda e: e.tensor_add(out=scr[:, :], in0=psP[:, :], in1=bo_bc[:, :]))
            pb.op("vector", lambda e: e.tensor_add(out=scr[:, :], in0=scr[:, :], in1=x_t[:, :]))
            pb.store(y_d[xoff + 128 * k:xoff + 128 * (k + 1), :], scr[:, :])

    pb.finish()
    return pb.nc


def build_l0ab(debug_u2=False):
    pb = PB()
    x_d = pb.dram("x", [TOK, D], F32)
    qwT_d = pb.dram("qwT", [D, 64], F32)
    kwT_d = pb.dram("kwT", [D, 64], F32)
    vwT_d = pb.dram("vwT", [D, D], F32)
    owT_d = pb.dram("owT", [D, D], F32)
    f1wT_d = pb.dram("f1wT", [D, 4096], F32)
    f2wT_d = pb.dram("f2wT", [4096, D], F32)
    ln1w_d = pb.dram("ln1w", [D], F32)
    ln1b_d = pb.dram("ln1b", [D], F32)
    ln2w_d = pb.dram("ln2w", [D], F32)
    ln2b_d = pb.dram("ln2b", [D], F32)
    bq_d = pb.dram("bq2", [64], F32)
    bk_d = pb.dram("bk2", [64], F32)
    bv_d = pb.dram("bv2", [D], F32)
    bo_d = pb.dram("bo2", [D], F32)
    f1b_d = pb.dram("f1b", [4096], F32)
    f2b_d = pb.dram("f2b", [D], F32)
    tril_d = pb.dram("tril", [128, 128], F32)
    y_d = pb.dram("y", [TOK, D], F32, kind="ExternalOutput")
    u2_d = pb.nc.dram_tensor("u2buf", [TOK, D], F32, kind="Internal").ap()

    AR = pb.sb("AR", [128, 49152], BF16)
    VW = AR[:, 0:8192]
    OW = AR[:, 8192:16384]
    XTc = AR[:, 16384:17408]
    Vbf = AR[:, 17408:18432]
    ATc = AR[:, 18432:19456]
    KVb = AR[:, 19456:20480]
    QW = AR[:, 20480:20992]
    KW = AR[:, 20992:21504]
    MISC = AR[:, 21504:22528]
    HT2 = AR[:, 0:16384]
    F2W = AR[:, 16384:49152]
    XN2T = pb.sb("XN2T", [128, 4096], BF16)

    x_t = pb.sb("x_t", [128, D], F32)
    u_t = pb.sb("u_t", [128, D], F32)
    scr = pb.sb("scr", [128, D], F32)
    scr2 = pb.sb("scr2", [128, D], F32)
    xn_bf = pb.sb("xn_bf", [128, D], BF16)
    stats = pb.sb("stats", [128, 16], F32)
    svst = pb.sb("svst", [32, 128], F32)
    stag = pb.sb("stag", [128, 1024], F32)
    ln1w_bc = pb.sb("ln1w_bc", [128, D], F32)
    ln1b_bc = pb.sb("ln1b_bc", [128, D], F32)
    ln2w_bc = pb.sb("ln2w_bc", [128, D], F32)
    ln2b_bc = pb.sb("ln2b_bc", [128, D], F32)
    bv_bc = pb.sb("bv_bc", [128, D], F32)
    bo_bc = pb.sb("bo_bc", [128, D], F32)
    f2b_bc = pb.sb("f2b_bc", [128, D], F32)
    tril_f = pb.sb("tril_f", [128, 128], F32)
    bq_t = pb.sb("bq_t", [64, 1], F32)
    bk_t = pb.sb("bk_t", [64, 1], F32)
    f1b_t = pb.sb("f1b_t", [128, 32], F32)
    qf = pb.sb("qf", [64, 128], F32)
    kf = pb.sb("kf", [64, 128], F32)
    ef = pb.sb("ef", [64, 128], F32)
    amf = pb.sb("amf", [128, 128], F32)
    dni = pb.sb("dni", [128, 4], F32)
    KVs = pb.sb("KVs", [64, 1024], F32)
    Ksf = pb.sb("Ksf", [64, 4], F32)
    Ksb = pb.sb("Ksb", [64, 1], BF16)
    onesc = pb.sb("onesc", [128, 1], BF16)
    eps6 = pb.sb("eps6", [128, 1], F32)
    wjc = pb.sb("wjc", [128, 1024], BF16)

    psV = pb.ps("psV", [128, 1024], F32)
    psN2 = pb.ps("psN2", [128, 1024], F32)
    KVps = pb.ps("KVps", [64, 1024], F32)
    psM = pb.ps("psM", [128, 512], F32)
    ps_trb = pb.ps("ps_trb", [128, 1024], BF16)

    pb.start()
    _setup_common(pb)
    for (d, t) in ((ln1w_d, ln1w_bc), (ln1b_d, ln1b_bc), (ln2w_d, ln2w_bc),
                   (ln2b_d, ln2b_bc), (bv_d, bv_bc), (bo_d, bo_bc), (f2b_d, f2b_bc)):
        pb.bc_row(t, d, D)
    pb.load(tril_f[:, :], tril_d[:, :])
    pb.load(bq_t[:, :], bq_d[:, None])
    pb.load(bk_t[:, :], bk_d[:, None])
    pb.vec_to_pcols(f1b_d, 32, svst, psM[:, 0:512], f1b_t)
    pb.op("vector", lambda e: e.memset(onesc[:, :], 1.0))
    pb.op("vector", lambda e: e.memset(eps6[:, :], 1e-6))
    for c in range(8):
        pb.load(stag[:, 0:64], qwT_d[128 * c:128 * (c + 1), :])
        pb.cast(QW[:, 64 * c:64 * (c + 1)], stag[:, 0:64])
        pb.load(stag[:, 0:64], kwT_d[128 * c:128 * (c + 1), :])
        pb.cast(KW[:, 64 * c:64 * (c + 1)], stag[:, 0:64])
        pb.load(stag[:, :], vwT_d[128 * c:128 * (c + 1), :])
        pb.cast(VW[:, 1024 * c:1024 * (c + 1)], stag[:, :])
        pb.load(stag[:, :], owT_d[128 * c:128 * (c + 1), :])
        pb.cast(OW[:, 1024 * c:1024 * (c + 1)], stag[:, :])

    Qbf = MISC[0:64, 0:128]
    Kbf = MISC[0:64, 128:256]
    Amb = MISC[:, 256:384]
    AmT = MISC[:, 384:512]

    for bb in range(2):
        xoff = 2048 * bb
        pb.op("vector", lambda e: e.memset(KVs[:, :], 0.0))
        pb.op("vector", lambda e: e.memset(Ksf[:, :], 0.0))
        for k in range(16):
            pb.load(x_t[:, :], x_d[xoff + 128 * k:xoff + 128 * (k + 1), :])
            pb.ln(x_t[:, :], u_t[:, :], stats, scr[:, :], D)
            pb.ln(u_t[:, :], scr2[:, :], stats, scr[:, :], D)
            pb.op("vector", lambda e: e.tensor_mul(out=scr2[:, :], in0=scr2[:, :], in1=ln1w_bc[:, :]))
            pb.op("vector", lambda e: e.tensor_add(out=scr2[:, :], in0=scr2[:, :], in1=ln1b_bc[:, :]))
            pb.cast(xn_bf[:, :], scr2[:, :])
            pb.tr_run([(xn_bf[:, 128 * c:128 * (c + 1)], XTc[:, 128 * c:128 * (c + 1)])
                       for c in range(8)], ps_trb, pb.ident_bf[:, :])
            for (WW, b_t, dst) in ((QW, bq_t, qf), (KW, bk_t, kf)):
                pb.mm(psM[0:64, 0:128],
                      [(WW[:, 64 * c:64 * (c + 1)], XTc[:, 128 * c:128 * (c + 1)],
                        psM[0:64, 0:128]) for c in range(8)])
                pb.op("vector", lambda e, b_t=b_t, dst=dst: e.tensor_scalar(
                    out=dst[:, :], in0=psM[0:64, 0:128], scalar1=b_t[:, 0:1],
                    scalar2=None, op0=ALU.add))
                pb.op("vector", lambda e, dst=dst: e.tensor_scalar(
                    out=ef[:, :], in0=dst[:, :], scalar1=0.0, scalar2=None, op0=ALU.min))
                pb.op("scalar", lambda e: e.activation(out=ef[:, :], in_=ef[:, :], func=AF.Exp))
                pb.op("scalar", lambda e, dst=dst: e.activation(out=dst[:, :], in_=dst[:, :], func=AF.Relu))
                pb.op("vector", lambda e, dst=dst: e.tensor_add(out=dst[:, :], in0=dst[:, :], in1=ef[:, :]))
            pb.cast(Qbf, qf[:, :])
            pb.cast(Kbf, kf[:, :])
            pairs = []
            for c in range(8):
                for half in range(2):
                    pairs.append((XTc[:, 128 * c:128 * (c + 1)],
                                  VW[:, 1024 * c + 512 * half:1024 * c + 512 * (half + 1)],
                                  psV[:, 512 * half:512 * (half + 1)]))
            pb.mm(None, pairs)
            pb.op("vector", lambda e: e.tensor_add(out=scr2[:, :], in0=psV[:, :], in1=bv_bc[:, :]))
            pb.cast(Vbf[:, :], scr2[:, :])
            pb.mm(psM[:, 0:128], [(Qbf, Kbf, psM[:, 0:128])])
            pb.op("vector", lambda e: e.tensor_mul(out=amf[:, :], in0=psM[:, 0:128], in1=tril_f[:, :]))
            pb.op("vector", lambda e: e.reduce_sum(dni[:, 0:1], amf[:, :], axis=AX.X))
            pb.cast(Amb, amf[:, :])
            pb.tr(ps_trb[:, 0:128], Amb, AmT, pb.ident_bf[:, :])
            pairs = [(AmT, Vbf[:, 0:512], psN2[:, 0:512]),
                     (AmT, Vbf[:, 512:1024], psN2[:, 512:1024])]
            if k > 0:
                pairs += [(Qbf, KVb[0:64, 0:512], psN2[:, 0:512]),
                          (Qbf, KVb[0:64, 512:1024], psN2[:, 512:1024]),
                          (Qbf, Ksb[:, 0:1], psM[:, 128:129])]
            pb.mm(None, pairs)
            KiT_ap = MISC[:, 512:576]
            pb.op("tensor", lambda e: e.transpose(ps_trb[:, 0:64], Kbf, pb.ident_bf[0:64, 0:64]))
            pb.op("scalar", lambda e, KiT_ap=KiT_ap: e.copy(out=KiT_ap, in_=ps_trb[:, 0:64]))
            pb.mm(None, [(KiT_ap, Vbf[:, 0:512], KVps[:, 0:512]),
                         (KiT_ap, Vbf[:, 512:1024], KVps[:, 512:1024]),
                         (KiT_ap, onesc[:, 0:1], psM[0:64, 256:257])])
            pb.op("vector", lambda e: e.tensor_add(out=KVs[:, :], in0=KVs[:, :], in1=KVps[:, :]))
            pb.op("vector", lambda e: e.tensor_add(out=Ksf[:, 0:1], in0=Ksf[:, 0:1], in1=psM[0:64, 256:257]))
            pb.cast(KVb[0:64, :], KVs[:, :])
            pb.cast(Ksb[:, :], Ksf[:, 0:1])
            if k > 0:
                pb.op("vector", lambda e: e.tensor_add(out=dni[:, 1:2], in0=psM[:, 128:129], in1=dni[:, 0:1]))
            else:
                pb.op("scalar", lambda e: e.copy(out=dni[:, 1:2], in_=dni[:, 0:1]))
            pb.op("vector", lambda e: e.tensor_add(out=dni[:, 1:2], in0=dni[:, 1:2], in1=eps6[:, 0:1]))
            pb.op("vector", lambda e: e.reciprocal(dni[:, 2:3], dni[:, 1:2]))
            pb.op("vector", lambda e: e.tensor_scalar(out=scr2[:, :], in0=psN2[:, :],
                                                      scalar1=dni[:, 2:3], scalar2=None, op0=ALU.mult))
            pb.cast(xn_bf[:, :], scr2[:, :])
            pb.tr_run([(xn_bf[:, 128 * c:128 * (c + 1)], ATc[:, 128 * c:128 * (c + 1)])
                       for c in range(8)], ps_trb, pb.ident_bf[:, :])
            pairs = []
            for c in range(8):
                for half in range(2):
                    pairs.append((ATc[:, 128 * c:128 * (c + 1)],
                                  OW[:, 1024 * c + 512 * half:1024 * c + 512 * (half + 1)],
                                  psV[:, 512 * half:512 * (half + 1)]))
            pb.mm(None, pairs)
            pb.op("vector", lambda e: e.tensor_add(out=scr2[:, :], in0=psV[:, :], in1=bo_bc[:, :]))
            pb.op("vector", lambda e: e.tensor_add(out=scr2[:, :], in0=scr2[:, :], in1=u_t[:, :]))
            pb.store(u2_d[xoff + 128 * k:xoff + 128 * (k + 1), :], scr2[:, :])
            if debug_u2:
                pb.store(y_d[xoff + 128 * k:xoff + 128 * (k + 1), :], scr2[:, :])

    if debug_u2:
        pb.finish()
        return pb.nc
    f1v = f1wT_d.rearrange("(c p) f -> p c f", p=128)
    for c in range(32):
        pb.load(stag[:, :], f2wT_d[128 * c:128 * (c + 1), :])
        pb.cast(F2W[:, 1024 * c:1024 * (c + 1)], stag[:, :])
    for blk in range(8):
        for kt in range(4):
            k = 4 * blk + kt
            pb.load(u_t[:, :], u2_d[128 * k:128 * (k + 1), :])
            pb.ln(u_t[:, :], scr2[:, :], stats, scr[:, :], D)
            pb.op("vector", lambda e: e.tensor_mul(out=scr2[:, :], in0=scr2[:, :], in1=ln2w_bc[:, :]))
            pb.op("vector", lambda e: e.tensor_add(out=scr2[:, :], in0=scr2[:, :], in1=ln2b_bc[:, :]))
            pb.cast(xn_bf[:, :], scr2[:, :])
            pb.tr_run([(xn_bf[:, 128 * c:128 * (c + 1)],
                        XN2T[:, 512 * c + 128 * kt:512 * c + 128 * (kt + 1)])
                       for c in range(8)], ps_trb, pb.ident_bf[:, :])
        for j in range(32):
            pb.load(stag[:, :].rearrange("p (c f) -> p c f", c=8, f=128),
                    f1v[:, :, 128 * j:128 * (j + 1)])
            pb.cast(wjc[:, :], stag[:, :])
            pb.mm(psM[:, 0:512],
                  [(wjc[:, 128 * c:128 * (c + 1)], XN2T[:, 512 * c:512 * (c + 1)])
                   for c in range(8)])
            pb.op("scalar", lambda e, j=j: e.activation(
                out=HT2[:, 512 * j:512 * (j + 1)], in_=psM[:, 0:512], func=AF.Gelu,
                bias=f1b_t[:, j:j + 1]))
        for kt in range(4):
            k = 4 * blk + kt
            pairs = []
            for c in range(32):
                for half in range(2):
                    pairs.append((HT2[:, 512 * c + 128 * kt:512 * c + 128 * (kt + 1)],
                                  F2W[:, 1024 * c + 512 * half:1024 * c + 512 * (half + 1)],
                                  psV[:, 512 * half:512 * (half + 1)]))
            pb.mm(None, pairs)
            pb.load(u_t[:, :], u2_d[128 * k:128 * (k + 1), :])
            pb.load(x_t[:, :], x_d[128 * k:128 * (k + 1), :])
            pb.op("vector", lambda e: e.tensor_add(out=scr[:, :], in0=psV[:, :], in1=f2b_bc[:, :]))
            pb.op("vector", lambda e: e.tensor_add(out=scr[:, :], in0=scr[:, :], in1=u_t[:, :]))
            pb.op("vector", lambda e: e.tensor_add(out=scr[:, :], in0=scr[:, :], in1=x_t[:, :]))
            pb.store(y_d[128 * k:128 * (k + 1), :], scr[:, :])

    pb.finish()
    return pb.nc


def build_l1aa():
    pb = PB()
    x_d = pb.dram("x", [TOK, D], F32)
    winT_d = pb.dram("winT", [D, 3072], F32)
    binp_d = pb.dram("binp", [3072], F32)
    woutT_d = pb.dram("woutT", [D, D], F32)
    bout_d = pb.dram("bout", [D], F32)
    y_d = pb.dram("y", [TOK, D], F32, kind="ExternalOutput")

    WIN = pb.sb("WIN", [128, 24576], BF16)
    OWT = pb.sb("OWT", [128, 8192], BF16)
    QL = pb.sb("QL", [128, 16384], BF16)
    KL = pb.sb("KL", [128, 16384], BF16)
    VL = pb.sb("VL", [128, 16384], BF16)
    binp_bc = pb.sb("binp_bc", [128, 3072], F32)
    bout_bc = pb.sb("bout_bc", [128, D], F32)
    stag = pb.sb("stag", [128, 1024], F32)
    XTc = pb.sb("XTc", [128, 1024], BF16)
    CXT = pb.sb("CXT", [128, 1024], BF16)
    x_t = pb.sb("x_t", [128, D], F32)
    scr = pb.sb("scr", [128, D], F32)
    tmp = pb.sb("tmp", [128, D], F32)
    CX = pb.sb("CX", [128, D], F32)
    xn_bf = pb.sb("xn_bf", [128, D], BF16)
    SC = pb.sb("SC", [128, 256], F32)
    P = pb.sb("P", [128, 256], F32)
    lden = pb.sb("lden", [128, 16], F32)
    rl = pb.sb("rl", [128, 16], F32)
    stats = pb.sb("stats", [128, 16], F32)

    psL = pb.ps("psL", [128, 1536], F32)
    psO = pb.ps("psO", [128, 1024], F32)
    ps_tr = pb.ps("ps_tr", [128, 1024], BF16)

    pb.start()
    _setup_common(pb)
    pb.bc_row(binp_bc, binp_d, 3072)
    pb.bc_row(bout_bc, bout_d, D)
    for c in range(8):
        for q in range(3):
            pb.load(stag[:, :], winT_d[128 * c:128 * (c + 1), 1024 * q:1024 * (q + 1)])
            pb.cast(WIN[:, 3072 * c + 1024 * q:3072 * c + 1024 * (q + 1)], stag[:, :])
        pb.load(stag[:, :], woutT_d[128 * c:128 * (c + 1), :])
        pb.cast(OWT[:, 1024 * c:1024 * (c + 1)], stag[:, :])

    qpat = lambda ap, off: ap.rearrange("p (h z j) -> p h z j", h=8, z=3, j=64)[:, :, off, :]

    for hf in range(2):
        for s in range(16):
            i = 2 * s + hf
            pb.load(x_t[:, :], x_d[128 * i:128 * (i + 1), :])
            pb.ln(x_t[:, :], xn_bf[:, :], stats, scr[:, :], D)
            pb.tr_run([(xn_bf[:, 128 * c:128 * (c + 1)], XTc[:, 128 * c:128 * (c + 1)])
                       for c in range(8)], ps_tr, pb.ident_bf[:, :])
            for fh in range(2):
                pairs = []
                for c in range(8):
                    for q in range(3):
                        pairs.append((XTc[:, 128 * c:128 * (c + 1)],
                                      WIN[:, 3072 * c + 1536 * fh + 512 * q:3072 * c + 1536 * fh + 512 * (q + 1)],
                                      psL[:, 512 * q:512 * (q + 1)]))
                pb.mm(None, pairs)
                for (off, DST) in ((0, QL), (1, KL), (2, VL)):
                    pb.op("vector", lambda e, off=off, DST=DST, s=s, fh=fh: e.tensor_tensor(
                        out=DST[:, 1024 * s + 512 * fh:1024 * s + 512 * fh + 512].rearrange(
                            "p (h j) -> p h j", h=8, j=64),
                        in0=qpat(psL, off),
                        in1=qpat(binp_bc[:, 1536 * fh:1536 * (fh + 1)], off), op=ALU.add))
        for s in range(16):
            i = 2 * s + hf
            for t in range(16):
                pb.op("vector", lambda e, s=s, t=t: e.tensor_mul(
                    out=tmp[:, :], in0=QL[:, 1024 * s:1024 * (s + 1)],
                    in1=KL[:, 1024 * t:1024 * (t + 1)]))
                pb.op("vector", lambda e, t=t: e.reduce_sum(
                    SC[:, t:256:16], tmp.rearrange("p (h o) -> p h o", h=16, o=64), axis=AX.X))
            pb.op("scalar", lambda e: e.activation(out=P[:, :], in_=SC[:, :], func=AF.Exp, scale=0.125))
            pb.op("vector", lambda e: e.reduce_sum(
                lden[:, :], P.rearrange("p (h t) -> p h t", h=16, t=16), axis=AX.X))
            pb.op("vector", lambda e: e.reciprocal(rl[:, :], lden[:, :]))
            for t in range(16):
                pbc = P[:, t:256:16].rearrange("p (h o) -> p h o", h=16, o=1).broadcast_to([128, 16, 64])
                dst = CX if t == 0 else tmp
                pb.op("vector", lambda e, t=t, pbc=pbc, dst=dst: e.tensor_tensor(
                    out=dst.rearrange("p (h o) -> p h o", h=16, o=64), in0=pbc,
                    in1=VL[:, 1024 * t:1024 * (t + 1)].rearrange("p (h o) -> p h o", h=16, o=64),
                    op=ALU.mult))
                if t > 0:
                    pb.op("vector", lambda e: e.tensor_add(out=CX[:, :], in0=CX[:, :], in1=tmp[:, :]))
            rbc = rl.rearrange("p (h o) -> p h o", h=16, o=1).broadcast_to([128, 16, 64])
            pb.op("vector", lambda e, rbc=rbc: e.tensor_tensor(
                out=CX.rearrange("p (h o) -> p h o", h=16, o=64),
                in0=CX.rearrange("p (h o) -> p h o", h=16, o=64), in1=rbc, op=ALU.mult))
            pb.cast(xn_bf[:, :], CX[:, :])
            pb.tr_run([(xn_bf[:, 128 * c:128 * (c + 1)], CXT[:, 128 * c:128 * (c + 1)])
                       for c in range(8)], ps_tr, pb.ident_bf[:, :])
            pairs = []
            for c in range(8):
                for half in range(2):
                    pairs.append((CXT[:, 128 * c:128 * (c + 1)],
                                  OWT[:, 1024 * c + 512 * half:1024 * c + 512 * (half + 1)],
                                  psO[:, 512 * half:512 * (half + 1)]))
            pb.mm(None, pairs)
            pb.load(x_t[:, :], x_d[128 * i:128 * (i + 1), :])
            pb.op("vector", lambda e: e.tensor_add(out=scr[:, :], in0=psO[:, :], in1=bout_bc[:, :]))
            pb.op("vector", lambda e: e.tensor_add(out=scr[:, :], in0=scr[:, :], in1=x_t[:, :]))
            pb.store(y_d[128 * i:128 * (i + 1), :], scr[:, :])

    pb.finish()
    return pb.nc


def run_l1aa(x, inp):
    nc = _get("l1aa", build_l1aa)
    shared = {
        "winT": _t(inp["L1_attn_a_in_proj_weight"]),
        "binp": np.asarray(inp["L1_attn_a_in_proj_bias"], np.float32),
        "woutT": _t(inp["L1_attn_a_out_proj_weight"]),
        "bout": np.asarray(inp["L1_attn_a_out_proj_bias"], np.float32),
        "ident": IDENT,
    }
    shards = _shard_seq(x)
    in_maps = [dict(shared, x=s) for s in shards]
    return _unshard_seq(_run(nc, in_maps))


def run_l0aa(x, inp):
    nc = _get("l0aa", build_l0aa)
    pos = np.arange(2048, dtype=np.float32)[:, None]
    ids = np.arange(512, dtype=np.float32)
    theta = (10000.0 ** (-2.0 * ids)) / 512.0
    r = pos * theta[None, :]
    shared = {
        "wdq": np.asarray(inp["L0_attn_a_Wd_q_w"], np.float32),
        "wuq": np.asarray(inp["L0_attn_a_Wu_q_w"], np.float32),
        "wqr": np.asarray(inp["L0_attn_a_Wq_r_w"], np.float32),
        "wkr": np.asarray(inp["L0_attn_a_Wk_r_w"], np.float32),
        "wdkv": np.asarray(inp["L0_attn_a_Wd_kv_w"], np.float32),
        "wuk": np.asarray(inp["L0_attn_a_Wu_k_w"], np.float32),
        "wuv": np.asarray(inp["L0_attn_a_Wu_v_w"], np.float32),
        "wo": np.asarray(inp["L0_attn_a_Wo_w"], np.float32),
        "bdq": np.asarray(inp["L0_attn_a_Wd_q_b"], np.float32),
        "buq": np.asarray(inp["L0_attn_a_Wu_q_b"], np.float32),
        "bqr": np.asarray(inp["L0_attn_a_Wq_r_b"], np.float32),
        "bkr": np.asarray(inp["L0_attn_a_Wk_r_b"], np.float32),
        "bdkv": np.asarray(inp["L0_attn_a_Wd_kv_b"], np.float32),
        "buk": np.asarray(inp["L0_attn_a_Wu_k_b"], np.float32),
        "buv": np.asarray(inp["L0_attn_a_Wu_v_b"], np.float32),
        "bo": np.asarray(inp["L0_attn_a_Wo_b"], np.float32),
        "cost": np.cos(r).astype(np.float32),
        "sint": np.sin(r).astype(np.float32),
        "ident": IDENT,
    }
    shards = _shard_batch(x)
    in_maps = [dict(shared, x=s) for s in shards]
    return _unshard_batch(_run(nc, in_maps))


def run_l0ab(x, inp):
    nc = _get("l0ab", build_l0ab)
    shared = {
        "qwT": _t(inp["L0_attn_b_q_w"]),
        "kwT": _t(inp["L0_attn_b_k_w"]),
        "vwT": _t(inp["L0_attn_b_v_w"]),
        "owT": _t(inp["L0_attn_b_o_w"]),
        "f1wT": _t(inp["L0_attn_b_f1_w"]),
        "f2wT": _t(inp["L0_attn_b_f2_w"]),
        "ln1w": np.asarray(inp["L0_attn_b_ln1_w"], np.float32),
        "ln1b": np.asarray(inp["L0_attn_b_ln1_b"], np.float32),
        "ln2w": np.asarray(inp["L0_attn_b_ln2_w"], np.float32),
        "ln2b": np.asarray(inp["L0_attn_b_ln2_b"], np.float32),
        "bq2": np.asarray(inp["L0_attn_b_q_b"], np.float32),
        "bk2": np.asarray(inp["L0_attn_b_k_b"], np.float32),
        "bv2": np.asarray(inp["L0_attn_b_v_b"], np.float32),
        "bo2": np.asarray(inp["L0_attn_b_o_b"], np.float32),
        "f1b": np.asarray(inp["L0_attn_b_f1_b"], np.float32),
        "f2b": np.asarray(inp["L0_attn_b_f2_b"], np.float32),
        "tril": np.tril(np.ones((128, 128), np.float32)),
        "ident": IDENT,
    }
    shards = _shard_batch(x)
    in_maps = [dict(shared, x=s) for s in shards]
    return _unshard_batch(_run(nc, in_maps))


def run_l0ffn(x, inp):
    nc = _get("l0ffn", build_l0ffn)
    wfc = np.asarray(inp["L0_ffn_w_fc"], np.float32)
    wpj = np.asarray(inp["L0_ffn_w_proj"], np.float32)
    shared = {
        "wgt": _t(inp["L0_ffn_w_gate"]),
        "wfct": np.ascontiguousarray(wfc.transpose(0, 2, 1)),
        "wpjt": np.ascontiguousarray(wpj.transpose(0, 2, 1)),
        "ident": IDENT,
    }
    shards = _shard_batch(x)
    in_maps = [dict(shared, x=s) for s in shards]
    return _unshard_batch(_run(nc, in_maps))


def run_l1ab(x, inp):
    nc = _get("l1ab", build_l1ab)
    shared = {
        "wq": np.asarray(inp["L1_attn_b_Wq"], np.float32),
        "wk": np.asarray(inp["L1_attn_b_Wk"], np.float32),
        "wv": np.asarray(inp["L1_attn_b_Wv"], np.float32),
        "wo": np.asarray(inp["L1_attn_b_Wo"], np.float32),
        "bq": np.asarray(inp["L1_attn_b_bq"], np.float32),
        "bk": np.asarray(inp["L1_attn_b_bk"], np.float32),
        "bv": np.asarray(inp["L1_attn_b_bv"], np.float32),
        "bo": np.asarray(inp["L1_attn_b_bo"], np.float32),
        "ident": IDENT,
    }
    shards = _shard_batch(x)
    in_maps = [dict(shared, x=s) for s in shards]
    return _unshard_batch(_run(nc, in_maps))


_BUILT = {}


def _get(name, builder):
    if name not in _BUILT:
        _BUILT[name] = builder()
    return _BUILT[name]


def _shard_batch(x):
    return [np.ascontiguousarray(s.reshape(TOK, D)) for s in np.split(x, NCORES, axis=0)]


def _unshard_batch(ys):
    return np.concatenate([y.reshape(2, 2048, D) for y in ys], axis=0)


def _shard_seq(x):
    return [np.ascontiguousarray(s.reshape(TOK, D)) for s in np.split(x, NCORES, axis=1)]


def _unshard_seq(ys):
    return np.concatenate([y.reshape(16, 256, D) for y in ys], axis=1)


def _run(nc, in_maps):
    res = run_bass_kernel_spmd(nc, in_maps, core_ids=list(range(NCORES)))
    return [r["y"] for r in res.results]


def _t(a):
    return np.ascontiguousarray(np.asarray(a, np.float32).T)


def run_l1ffn(x, inp):
    nc = _get("l1ffn", build_l1ffn)
    shared = {
        "w1t": _t(inp["L1_ffn_w1_w"]), "w2t": _t(inp["L1_ffn_w2_w"]),
        "h1t": _t(inp["L1_ffn_h1_w"]), "h2t": _t(inp["L1_ffn_h2_w"]),
        "w3t": _t(inp["L1_ffn_w3_w"]),
        "s1": np.asarray(inp["L1_ffn_w1_s"], np.float32),
        "s2": np.asarray(inp["L1_ffn_w2_s"], np.float32),
        "hs1": np.asarray(inp["L1_ffn_h1_s"], np.float32),
        "hs2": np.asarray(inp["L1_ffn_h2_s"], np.float32),
        "s3": np.asarray(inp["L1_ffn_w3_s"], np.float32),
        "ident": IDENT,
    }
    shards = _shard_batch(x)
    in_maps = [dict(shared, x=s) for s in shards]
    return _unshard_batch(_run(nc, in_maps))


def kernel(**inp):
    x = np.asarray(inp["L0_attn_a_x"], np.float32)
    x = run_l0aa(x, inp)
    x = run_l0ffn(x, inp)
    x = run_l0ab(x, inp)
    x = run_l0ffn(x, inp)
    x = run_l1aa(x, inp)
    x = run_l1ffn(x, inp)
    x = run_l1ab(x, inp)
    x = run_l1ffn(x, inp)
    return x


# version 06
# speedup vs baseline: 1.0973x; 1.0973x over previous
import numpy as np
import concourse.bass as bass
import concourse.mybir as mybir
from concourse.bass_utils import run_bass_kernel_spmd
from contextlib import ExitStack

F32 = mybir.dt.float32
BF16 = mybir.dt.bfloat16
AF = mybir.ActivationFunctionType
ALU = mybir.AluOpType
AX = mybir.AxisListType

NCORES = 8
D = 1024
TOK = 4096
NT = TOK // 128
LN_EPS = 1e-5

IDENT = np.eye(128, dtype=np.float32)


class PB:

    def __init__(self):
        self.nc = bass.Bass("TRN2", target_bir_lowering=False)
        self.es = ExitStack()
        self._names = {}
        self.block = None
        self.g = None
        self.n = 0

    def _u(self, p):
        i = self._names.get(p, 0)
        self._names[p] = i + 1
        return f"{p}_{i}"

    def dram(self, name, shape, dtype=F32, kind="ExternalInput"):
        return self.nc.dram_tensor(name, shape, dtype, kind=kind).ap()

    def sb(self, prefix, shape, dtype=F32):
        return self.es.enter_context(self.nc.sbuf_tensor(self._u(prefix), shape, dtype))

    def ps(self, prefix, shape, dtype=F32):
        return self.es.enter_context(self.nc.psum_tensor(self._u(prefix), shape, dtype))

    def start(self):
        self.g = self.es.enter_context(self.nc.semaphore("g"))
        self.L = self.es.enter_context(self.nc.semaphore("L"))
        self.S = self.es.enter_context(self.nc.semaphore("S"))
        self.l = 0
        self.s = 0
        self.block = self.es.enter_context(self.nc.Block())

    def op(self, eng, f, inc=1, swait=None):
        need = self.n
        lneed = self.l
        sect = getattr(self.block, eng)

        def body(e):
            e.wait_ge(self.g, need)
            if lneed:
                e.wait_ge(self.L, lneed)
            if swait:
                e.wait_ge(self.S, swait)
            f(e).then_inc(self.g, inc)

        sect(body)
        self.n += inc

    def group(self, eng, fs):
        if not fs:
            return
        need = self.n
        lneed = self.l
        sect = getattr(self.block, eng)

        def body(e):
            e.wait_ge(self.g, need)
            if lneed:
                e.wait_ge(self.L, lneed)
            last = None
            for f in fs:
                last = f(e)
            last.then_inc(self.g, 1)

        sect(body)
        self.n += 1

    def load(self, dst, src, after_n=None):
        need = self.n if after_n is None else after_n
        lneed_prev = 0
        sect = self.block.sync

        def body(e):
            e.wait_ge(self.g, need)
            e.dma_start(out=dst, in_=src).then_inc(self.L, 16)

        sect(body)
        self.l += 16

    def store(self, dst, src):
        self.op("gpsimd", lambda e: e.dma_start(out=dst, in_=src), inc=16)

    def store_async(self, dst, src):
        need = self.n
        lneed = self.l

        def body(e):
            e.wait_ge(self.g, need)
            if lneed:
                e.wait_ge(self.L, lneed)
            e.dma_start(out=dst, in_=src).then_inc(self.S, 16)

        self.block.gpsimd(body)
        self.s += 16

    def drain_stores(self):
        sv = self.s

        def body(e):
            if sv:
                e.wait_ge(self.S, sv)
            e.nop().then_inc(self.g, 1)

        self.block.gpsimd(body)
        self.n += 1

    def mm(self, psum_ap, pairs, first_start=True, last_stop=True):
        fs = []
        keys = []
        for i, pr in enumerate(pairs):
            out = pr[2] if len(pr) > 2 and pr[2] is not None else psum_ap
            keys.append(str(out))
        seen = set()
        last = {}
        for i, k in enumerate(keys):
            last[k] = i
        for i, pr in enumerate(pairs):
            lhsT, rhs = pr[0], pr[1]
            out = pr[2] if len(pr) > 2 and pr[2] is not None else psum_ap
            st = first_start and (keys[i] not in seen)
            seen.add(keys[i])
            sp = last_stop and (last[keys[i]] == i)
            fs.append(lambda e, a=out, l=lhsT, r=rhs, s1=st, s2=sp:
                      e.matmul(a, l, r, start=s1, stop=s2))
        self.group("tensor", fs)

    def cast(self, dst, src):
        self.op("scalar", lambda e: e.copy(out=dst, in_=src))

    def tr(self, ps_tr, src_bf, dst, ident_bf):
        self.op("tensor", lambda e: e.transpose(ps_tr, src_bf, ident_bf))
        self.op("scalar", lambda e: e.copy(out=dst, in_=ps_tr))

    def tr_run(self, pairs, ps8, ident_bf, evac="copy"):
        self.group("tensor", [
            (lambda e, s=s_, i=i_: e.transpose(ps8[:, 128 * i:128 * (i + 1)], s, ident_bf))
            for i_, (s_, d_) in enumerate(pairs)])
        if evac == "copy":
            self.group("scalar", [
                (lambda e, d=d_, i=i_: e.copy(out=d, in_=ps8[:, 128 * i:128 * (i + 1)]))
                for i_, (s_, d_) in enumerate(pairs)])
        else:
            self.group("vector", [
                (lambda e, d=d_, i=i_: e.tensor_tensor(
                    out=d, in0=ps8[:, 128 * i:128 * (i + 1)], in1=d, op=ALU.add))
                for i_, (s_, d_) in enumerate(pairs)])

    def ln(self, x_ap, out_ap, stats, scr, Dd, eps=LN_EPS):
        o = self.op
        o("scalar", lambda e: e.activation(out=scr, in_=x_ap, func=AF.Square,
                                           accum_out=stats[:, 1:2]))
        o("vector", lambda e: e.reduce_sum(stats[:, 0:1], x_ap, axis=AX.X))
        o("scalar", lambda e: e.mul(out=stats[:, 2:3], in_=stats[:, 0:1], mul=1.0 / Dd))
        o("scalar", lambda e: e.mul(out=stats[:, 4:5], in_=stats[:, 1:2], mul=1.0 / Dd))
        o("vector", lambda e: e.tensor_mul(out=stats[:, 3:4], in0=stats[:, 2:3], in1=stats[:, 2:3]))
        o("vector", lambda e: e.tensor_sub(out=stats[:, 5:6], in0=stats[:, 4:5], in1=stats[:, 3:4]))
        o("scalar", lambda e: e.activation(out=stats[:, 6:7], in_=stats[:, 5:6], func=AF.Ln,
                                           bias=self.eps_t[:, 0:1]))
        o("scalar", lambda e: e.activation(out=stats[:, 7:8], in_=stats[:, 6:7], func=AF.Exp, scale=-0.5))
        o("scalar", lambda e: e.mul(out=stats[:, 8:9], in_=stats[:, 2:3], mul=-1.0))
        o("vector", lambda e: e.tensor_scalar(out=out_ap, in0=x_ap, scalar1=stats[:, 8:9],
                                              scalar2=stats[:, 7:8], op0=ALU.add, op1=ALU.mult))

    def vec_to_pcols(self, dram_vec, n, stage, ps_sc, out_tile):
        self.load(stage[0:n, :], dram_vec.rearrange("(j p) -> j p", p=128))
        self.op("tensor", lambda e: e.transpose(ps_sc[:, 0:n], stage[0:n, :], self.ident_f[0:n, 0:n]))
        self.op("scalar", lambda e: e.copy(out=out_tile[:, 0:n], in_=ps_sc[:, 0:n]))

    def bc_row(self, tile, dram_vec, Dd):
        self.load(tile[:, 0:Dd], dram_vec[None, :].broadcast_to([128, Dd]))

    def finish(self):
        nv, lv, sv = self.n, self.l, self.s

        def body(e):
            e.wait_ge(self.g, nv)
            if lv:
                e.wait_ge(self.L, lv)
            if sv:
                e.wait_ge(self.S, sv)
            e.nop()

        self.block.sync(body)
        self.es.close()
        return self.nc


def _setup_common(pb):
    ident_d = pb.dram("ident", [128, 128], F32)
    pb.ident_f = pb.sb("ident_f", [128, 128], F32)
    pb.ident_bf = pb.sb("ident_bf", [128, 128], BF16)
    pb.load(pb.ident_f[:, :], ident_d[:, :])
    pb.cast(pb.ident_bf[:, :], pb.ident_f[:, :])
    pb.eps_t = pb.sb("eps_t", [128, 2], F32)
    pb.op("vector", lambda e: e.memset(pb.eps_t[:, 0:1], LN_EPS))


def build_l1ffn():
    pb = PB()
    x_d = pb.dram("x", [TOK, D], F32)
    w1t_d = pb.dram("w1t", [D, 4096], F32)
    w2t_d = pb.dram("w2t", [D, 4096], F32)
    h1t_d = pb.dram("h1t", [4096, 4096], F32)
    h2t_d = pb.dram("h2t", [4096, 4096], F32)
    w3t_d = pb.dram("w3t", [4096, D], F32)
    s1_d = pb.dram("s1", [4096], F32)
    s2_d = pb.dram("s2", [4096], F32)
    hs1_d = pb.dram("hs1", [4096], F32)
    hs2_d = pb.dram("hs2", [4096], F32)
    s3_d = pb.dram("s3", [D], F32)
    y_d = pb.dram("y", [TOK, D], F32, kind="ExternalOutput")
    H0_d = pb.nc.dram_tensor("H0buf", [4096, TOK], BF16, kind="Internal").ap()
    H1_d = pb.nc.dram_tensor("H1buf", [4096, TOK], BF16, kind="Internal").ap()
    H2_d = pb.nc.dram_tensor("H2buf", [4096, TOK], BF16, kind="Internal").ap()

    BUF_A = pb.sb("bufA", [128, 32768], BF16)
    BUF_B = pb.sb("bufB", [128, 32768], BF16)
    stagw = pb.sb("stagw", [128, 4096], F32)
    stagw2 = pb.sb("stagw2", [128, 4096], F32)
    wc_bf = pb.sb("wc_bf", [128, 4096], BF16)
    wc_bf2 = pb.sb("wc_bf2", [128, 4096], BF16)
    stag2 = pb.sb("stag2", [128, 1024], F32)
    h0_bf2 = pb.sb("h0_bf2", [128, 512], BF16)
    u1_bf2 = pb.sb("u1_bf2", [128, 512], BF16)
    x_t = pb.sb("x_t", [128, D], F32)
    scr = pb.sb("scr", [128, D], F32)
    scr5 = pb.sb("scr5", [128, 512], F32)
    xn_bf = pb.sb("xn_bf", [128, D], BF16)
    u1_bf = pb.sb("u1_bf", [128, 512], BF16)
    h0_bf = pb.sb("h0_bf", [128, 512], BF16)
    stats = pb.sb("stats", [128, 16], F32)
    svst = pb.sb("svst", [32, 128], F32)
    s1_t = pb.sb("s1_t", [128, 32], F32)
    s2_t = pb.sb("s2_t", [128, 32], F32)
    hs1_t = pb.sb("hs1_t", [128, 32], F32)
    hs2_t = pb.sb("hs2_t", [128, 32], F32)
    s3_bc = pb.sb("s3_bc", [128, D], F32)

    psA = pb.ps("psA", [128, 512], F32)
    psB = pb.ps("psB", [128, 512], F32)
    psD = pb.ps("psD", [128, 1024], F32)
    ps_tr = pb.ps("ps_tr", [128, 1024], BF16)
    ps_sc = pb.ps("ps_sc", [128, 32], F32)

    pb.start()
    _setup_common(pb)
    pb.vec_to_pcols(s1_d, 32, svst, ps_sc, s1_t)
    pb.vec_to_pcols(s2_d, 32, svst, ps_sc, s2_t)
    pb.vec_to_pcols(hs1_d, 32, svst, ps_sc, hs1_t)
    pb.vec_to_pcols(hs2_d, 32, svst, ps_sc, hs2_t)
    pb.bc_row(s3_bc, s3_d, D)

    def xnT(c):
        return BUF_A[:, 4096 * c:4096 * (c + 1)]

    for k in range(NT):
        pb.load(x_t[:, :], x_d[128 * k:128 * (k + 1), :])
        pb.ln(x_t[:, :], xn_bf[:, :], stats, scr[:, :], D)
        pb.tr_run([(xn_bf[:, 128 * c:128 * (c + 1)], xnT(c)[:, 128 * k:128 * (k + 1)])
                   for c in range(8)], ps_tr, pb.ident_bf[:, :])

    w1v = w1t_d.rearrange("(c p) f -> p c f", p=128)
    w2v = w2t_d.rearrange("(c p) f -> p c f", p=128)
    STG = [stagw, stagw2]
    WCB = [wc_bf, wc_bf2]
    H0R = [h0_bf, h0_bf2]

    def ph1_loads(j, bi, after_n):
        pb.load(STG[bi][:, 0:1024].rearrange("p (c f) -> p c f", c=8, f=128),
                w1v[:, :, 128 * j:128 * (j + 1)], after_n=after_n)
        pb.load(STG[bi][:, 1024:2048].rearrange("p (c f) -> p c f", c=8, f=128),
                w2v[:, :, 128 * j:128 * (j + 1)], after_n=after_n)

    ph1_loads(0, 0, pb.n)
    for j in range(32):
        bi = j % 2
        pb.cast(WCB[bi][:, 0:2048], STG[bi][:, 0:2048])
        ck = pb.n
        for tb in range(8):
            ts = slice(512 * tb, 512 * (tb + 1))
            pb.mm(psA[:, :], [(WCB[bi][:, 128 * c:128 * (c + 1)], xnT(c)[:, ts]) for c in range(8)])
            pb.mm(psB[:, :], [(WCB[bi][:, 1024 + 128 * c:1024 + 128 * (c + 1)], xnT(c)[:, ts]) for c in range(8)])
            pb.op("scalar", lambda e, j=j: e.activation(out=u1_bf[:, :], in_=psA[:, :],
                                                        func=AF.Silu, scale=s1_t[:, j:j + 1]))
            pb.op("scalar", lambda e, j=j: e.activation(out=scr5[:, :], in_=psB[:, :],
                                                        func=AF.Copy, scale=s2_t[:, j:j + 1]))
            hb = H0R[tb % 2]
            pb.op("vector", lambda e, hb=hb: e.tensor_mul(out=hb[:, :], in0=scr5[:, :], in1=u1_bf[:, :]),
                  swait=max(0, pb.s - 16))
            pb.store_async(H0_d[128 * j:128 * (j + 1), ts], hb[:, :])
        if j + 1 < 32:
            ph1_loads(j + 1, 1 - bi, ck)

    U1R = [u1_bf, u1_bf2]
    for (ht_d, hs_t, src_d, dst_d) in ((h1t_d, hs1_t, H0_d, H1_d),
                                       (h2t_d, hs2_t, H1_d, H2_d)):
        hv = ht_d.rearrange("(c p) f -> p c f", p=128)

        def w_loads(j, bi, after_n, hv=hv):
            for q in range(4):
                pb.load(STG[bi][:, 1024 * q:1024 * (q + 1)].rearrange("p (c f) -> p c f", c=8, f=128),
                        hv[:, 8 * q:8 * (q + 1), 128 * j:128 * (j + 1)], after_n=after_n)

        for tb in range(2):
            tbs = slice(2048 * tb, 2048 * (tb + 1))
            for c in range(32):
                dst = (BUF_A if c < 16 else BUF_B)
                cc = c % 16
                pb.load(dst[:, 2048 * cc:2048 * (cc + 1)], src_d[128 * c:128 * (c + 1), tbs])
            w_loads(0, 0, pb.n)
            for j in range(32):
                bi = j % 2
                pb.cast(WCB[bi][:, :], STG[bi][:, :])
                ck = pb.n
                for h in range(4):
                    hsl = slice(512 * h, 512 * (h + 1))
                    pairs = []
                    for c in range(32):
                        srcb = (BUF_A if c < 16 else BUF_B)
                        cc = c % 16
                        pairs.append((WCB[bi][:, 128 * c:128 * (c + 1)],
                                      srcb[:, 2048 * cc:2048 * cc + 2048][:, hsl]))
                    pb.mm(psA[:, :], pairs)
                    ub = U1R[h % 2]
                    pb.op("scalar", lambda e, j=j, hs_t=hs_t, ub=ub: e.activation(
                        out=ub[:, :], in_=psA[:, :], func=AF.Gelu, scale=hs_t[:, j:j + 1]),
                        swait=max(0, pb.s - 16))
                    pb.store_async(dst_d[128 * j:128 * (j + 1), tbs][:, hsl], ub[:, :])
                if j + 1 < 32:
                    w_loads(j + 1, 1 - bi, ck)

    w3v = w3t_d.rearrange("(c p) f -> p c f", p=128)
    for tb in range(4):
        tbs = slice(1024 * tb, 1024 * (tb + 1))
        for c in range(32):
            pb.load(BUF_A[:, 1024 * c:1024 * (c + 1)], H2_d[128 * c:128 * (c + 1), tbs])
        for c in range(32):
            pb.load(stag2[:, :], w3v[:, c, :])
            pb.cast(BUF_B[:, 1024 * c:1024 * (c + 1)], stag2[:, :])
        for k in range(8):
            ksl = slice(128 * k, 128 * (k + 1))
            pairs = []
            for c in range(32):
                for half in range(2):
                    pairs.append((BUF_A[:, 1024 * c:1024 * (c + 1)][:, ksl],
                                  BUF_B[:, 1024 * c + 512 * half:1024 * c + 512 * (half + 1)],
                                  psD[:, 512 * half:512 * (half + 1)]))
            pb.mm(None, pairs)
            gk = 128 * (8 * tb + k)
            pb.load(x_t[:, :], x_d[gk:gk + 128, :])
            pb.op("vector", lambda e: e.tensor_mul(out=scr[:, :], in0=psD[:, :], in1=s3_bc[:, :]))
            pb.op("vector", lambda e: e.tensor_add(out=scr[:, :], in0=scr[:, :], in1=x_t[:, :]))
            pb.store(y_d[gk:gk + 128, :], scr[:, :])

    pb.finish()
    return pb.nc


def build_l0ffn():
    pb = PB()
    x_d = pb.dram("x", [TOK, D], F32)
    wgt_d = pb.dram("wgt", [D, 8], F32)
    wfct_d = pb.dram("wfct", [8, D, 512], F32)
    wpjt_d = pb.dram("wpjt", [8, 512, D], F32)
    y_d = pb.dram("y", [TOK, D], F32, kind="ExternalOutput")

    WFC = pb.sb("WFC", [128, 32768], BF16)
    WPJ = pb.sb("WPJ", [128, 32768], BF16)
    stag = pb.sb("stag", [128, 4096], F32)
    XT = pb.sb("XT", [128, 4096], BF16)
    HT = pb.sb("HT", [128, 2048], BF16)
    wg_bf = pb.sb("wg_bf", [128, 64], BF16)
    x_t = pb.sb("x_t", [128, D], F32)
    scr = pb.sb("scr", [128, D], F32)
    xn_bf = pb.sb("xn_bf", [128, D], BF16)
    stats = pb.sb("stats", [128, 16], F32)
    Y = [pb.sb("Y", [128, D], F32) for _ in range(4)]
    G4 = pb.sb("G4", [128, 32], F32)
    rt = pb.sb("rt", [128, 8], F32)
    eq1 = pb.sb("eq1", [128, 8], F32)
    eq2 = pb.sb("eq2", [128, 8], F32)
    tmp8 = pb.sb("tmp8", [128, 8], F32)
    g1 = pb.sb("g1", [128, 8], F32)

    psH = pb.ps("psH", [128, 512], F32)
    psY = pb.ps("psY", [128, 1024], F32)
    psG = pb.ps("psG", [128, 8], F32)
    ps_tr = pb.ps("ps_tr", [128, 1024], BF16)

    pb.start()
    _setup_common(pb)
    pb.load(stag[:, 0:64].rearrange("p (c f) -> p c f", c=8, f=8),
            wgt_d.rearrange("(c p) f -> p c f", p=128))
    pb.cast(wg_bf[:, :], stag[:, 0:64])
    for e in range(8):
        pb.load(stag.rearrange("p (c f) -> p c f", c=8, f=512),
                wfct_d[e].rearrange("(c p) f -> p c f", p=128))
        pb.cast(WFC[:, 4096 * e:4096 * (e + 1)], stag[:, :])
        pb.load(stag.rearrange("p (c f) -> p c f", c=4, f=1024),
                wpjt_d[e].rearrange("(c p) f -> p c f", p=128))
        pb.cast(WPJ[:, 4096 * e:4096 * (e + 1)], stag[:, 0:4096])

    for b in range(8):
        for kt in range(4):
            k = 4 * b + kt
            pb.load(x_t[:, :], x_d[128 * k:128 * (k + 1), :])
            pb.ln(x_t[:, :], xn_bf[:, :], stats, scr[:, :], D)
            pb.tr_run([(xn_bf[:, 128 * c:128 * (c + 1)],
                        XT[:, 512 * c + 128 * kt:512 * c + 128 * (kt + 1)])
                       for c in range(8)], ps_tr, pb.ident_bf[:, :])
            pb.mm(psG[:, :], [(XT[:, 512 * c + 128 * kt:512 * c + 128 * (kt + 1)],
                               wg_bf[:, 8 * c:8 * (c + 1)]) for c in range(8)])
            pb.op("vector", lambda e: e.reduce_max(rt[:, 0:1], psG[:, :], axis=AX.X))
            pb.op("vector", lambda e: e.tensor_scalar(out=eq1[:, :], in0=psG[:, :], scalar1=rt[:, 0:1],
                                                      scalar2=None, op0=ALU.is_equal))
            pb.op("scalar", lambda e: e.mul(out=tmp8[:, :], in_=eq1[:, :], mul=-1e30))
            pb.op("vector", lambda e: e.tensor_add(out=tmp8[:, :], in0=psG[:, :], in1=tmp8[:, :]))
            pb.op("vector", lambda e: e.reduce_max(rt[:, 1:2], tmp8[:, :], axis=AX.X))
            pb.op("vector", lambda e: e.tensor_scalar(out=eq2[:, :], in0=tmp8[:, :], scalar1=rt[:, 1:2],
                                                      scalar2=None, op0=ALU.is_equal))
            pb.op("vector", lambda e: e.tensor_sub(out=rt[:, 2:3], in0=rt[:, 0:1], in1=rt[:, 1:2]))
            pb.op("scalar", lambda e: e.activation(out=rt[:, 3:4], in_=rt[:, 2:3], func=AF.Sigmoid))
            pb.op("scalar", lambda e: e.activation(out=rt[:, 4:5], in_=rt[:, 2:3], func=AF.Sigmoid, scale=-1.0))
            pb.op("vector", lambda e: e.tensor_scalar(out=g1[:, :], in0=eq1[:, :], scalar1=rt[:, 3:4],
                                                      scalar2=None, op0=ALU.mult))
            pb.op("vector", lambda e, kt=kt: e.tensor_scalar(out=G4[:, 8 * kt:8 * (kt + 1)], in0=eq2[:, :],
                                                             scalar1=rt[:, 4:5], scalar2=None, op0=ALU.mult))
            pb.op("vector", lambda e, kt=kt: e.tensor_add(out=G4[:, 8 * kt:8 * (kt + 1)],
                                                          in0=G4[:, 8 * kt:8 * (kt + 1)], in1=g1[:, :]))
            pb.cast(Y[kt][:, :], x_t[:, :])
        for e_ in range(8):
            for i in range(4):
                pb.mm(psH[:, :], [(WFC[:, (8 * e_ + c) * 512 + 128 * i:(8 * e_ + c) * 512 + 128 * (i + 1)],
                                   XT[:, 512 * c:512 * (c + 1)]) for c in range(8)])
                pb.op("scalar", lambda e, i=i: e.activation(out=HT[:, 512 * i:512 * (i + 1)],
                                                            in_=psH[:, :], func=AF.Gelu))
            for kt in range(4):
                pairs = []
                for i in range(4):
                    for half in range(2):
                        pairs.append((HT[:, 512 * i + 128 * kt:512 * i + 128 * (kt + 1)],
                                      WPJ[:, (4 * e_ + i) * 1024 + 512 * half:(4 * e_ + i) * 1024 + 512 * (half + 1)],
                                      psY[:, 512 * half:512 * (half + 1)]))
                pb.mm(None, pairs)
                pb.op("vector", lambda e, kt=kt, e_=e_: e.scalar_tensor_tensor(
                    out=Y[kt][:, :], in0=psY[:, :], scalar=G4[:, 8 * kt + e_:8 * kt + e_ + 1],
                    in1=Y[kt][:, :], op0=ALU.mult, op1=ALU.add))
        for kt in range(4):
            k = 4 * b + kt
            pb.store(y_d[128 * k:128 * (k + 1), :], Y[kt][:, :])

    pb.finish()
    return pb.nc


def build_l1ab():
    pb = PB()
    x_d = pb.dram("x", [TOK, D], F32)
    wq_d = pb.dram("wq", [D, D], F32)
    wk_d = pb.dram("wk", [D, D], F32)
    wv_d = pb.dram("wv", [D, D], F32)
    wo_d = pb.dram("wo", [D, D], F32)
    bq_d = pb.dram("bq", [D], F32)
    bk_d = pb.dram("bk", [D], F32)
    bv_d = pb.dram("bv", [D], F32)
    bo_d = pb.dram("bo", [D], F32)
    y_d = pb.dram("y", [TOK, D], F32, kind="ExternalOutput")

    S = 2048
    XT = pb.sb("XT", [128, 8 * S], BF16)
    QT = pb.sb("QT", [128, 8 * S], BF16)
    KT = pb.sb("KT", [128, 8 * S], BF16)
    VA = pb.sb("VA", [128, 16 * 1040], BF16)
    PT = pb.sb("PT", [128, 1024], BF16)
    WB = pb.sb("WB", [128, 8192], BF16)
    stag = pb.sb("stag", [128, 8192], F32)
    x_t = pb.sb("x_t", [128, D], F32)
    scr = pb.sb("scr", [128, D], F32)
    xn_bf = pb.sb("xn_bf", [128, D], BF16)
    stats = pb.sb("stats", [128, 16], F32)
    svst = pb.sb("svst", [32, 128], F32)
    bq_t = pb.sb("bq_t", [128, 8], F32)
    bk_t = pb.sb("bk_t", [128, 8], F32)
    bv_bc = pb.sb("bv_bc", [128, D], F32)
    bo_bc = pb.sb("bo_bc", [128, D], F32)
    ones_bf = pb.sb("ones_bf", [1, 128], BF16)
    dn = pb.sb("dn", [1, 1024], F32)
    dn_bf = pb.sb("dn_bf", [1, 1024], BF16)
    sc64 = pb.sb("sc64", [64, 1024], BF16)

    psS = pb.ps("psS", [128, 1024], F32)
    psN = pb.ps("psN", [65, 1024], F32)
    psP = pb.ps("psP", [128, 1024], F32)
    ps_tr = pb.ps("ps_tr", [128, 1024], BF16)
    ps_sc = pb.ps("ps_sc", [128, 32], F32)

    pb.start()
    _setup_common(pb)
    pb.vec_to_pcols(bq_d, 8, svst, ps_sc, bq_t)
    pb.vec_to_pcols(bk_d, 8, svst, ps_sc, bk_t)
    pb.bc_row(bv_bc, bv_d, D)
    pb.bc_row(bo_bc, bo_d, D)
    pb.op("vector", lambda e: e.memset(ones_bf[:, :], 1.0))

    def load_w(w_d):
        pb.load(stag.rearrange("p (c f) -> p c f", c=8, f=1024),
                w_d.rearrange("(c p) f -> p c f", p=128))
        pb.cast(WB[:, :], stag[:, :])

    for bb in range(2):
        xoff = 2048 * bb
        pb.op("vector", lambda e: e.memset(VA[:, :], 1.0))
        for k in range(16):
            pb.load(x_t[:, :], x_d[xoff + 128 * k:xoff + 128 * (k + 1), :])
            pb.ln(x_t[:, :], xn_bf[:, :], stats, scr[:, :], D)
            pb.tr_run([(xn_bf[:, 128 * c:128 * (c + 1)],
                        XT[:, S * c + 128 * k:S * c + 128 * (k + 1)])
                       for c in range(8)], ps_tr, pb.ident_bf[:, :])
        for (w_d, b_t, DST) in ((wq_d, bq_t, QT), (wk_d, bk_t, KT)):
            load_w(w_d)
            for j in range(8):
                for tq in range(4):
                    pb.mm(psP[:, 0:512],
                          [(WB[:, 1024 * c + 128 * j:1024 * c + 128 * (j + 1)],
                            XT[:, S * c + 512 * tq:S * c + 512 * (tq + 1)]) for c in range(8)])
                    pb.op("vector", lambda e, j=j, tq=tq, DST=DST, b_t=b_t: e.tensor_scalar(
                        out=DST[:, S * j + 512 * tq:S * j + 512 * (tq + 1)], in0=psP[:, 0:512],
                        scalar1=b_t[:, j:j + 1], scalar2=None, op0=ALU.add))
        load_w(wv_d)
        for k in range(16):
            pairs = []
            for c in range(8):
                for half in range(2):
                    pairs.append((XT[:, S * c + 128 * k:S * c + 128 * (k + 1)],
                                  WB[:, 1024 * c + 512 * half:1024 * c + 512 * (half + 1)],
                                  psP[:, 512 * half:512 * (half + 1)]))
            pb.mm(None, pairs)
            vk = VA[:, 1040 * k:1040 * (k + 1)].rearrange("p (h w) -> p h w", h=16, w=65)[:, :, 0:64]
            pb.op("vector", lambda e, vk=vk: e.tensor_tensor(
                out=vk, in0=psP.rearrange("p (h w) -> p h w", h=16, w=64),
                in1=bv_bc.rearrange("p (h w) -> p h w", h=16, w=64), op=ALU.add))
        AT = XT
        for h in range(16):
            hb = (h % 2) * 64
            j = h // 2
            for qh in range(2):
                qo = 1024 * qh
                for kt in range(16):
                    pb.mm(psS[:, :],
                          [(KT[hb:hb + 64, S * j + 128 * kt:S * j + 128 * (kt + 1)],
                            QT[hb:hb + 64, S * j + qo + 512 * tq:S * j + qo + 512 * (tq + 1)],
                            psS[:, 512 * tq:512 * (tq + 1)]) for tq in range(2)])
                    pb.op("scalar", lambda e: e.activation(out=PT[:, :], in_=psS[:, :],
                                                           func=AF.Exp, scale=0.125))
                    pb.mm(None,
                          [(VA[:, 1040 * kt + 65 * h:1040 * kt + 65 * (h + 1)],
                            PT[:, 512 * tq:512 * (tq + 1)],
                            psN[:, 512 * tq:512 * (tq + 1)]) for tq in range(2)],
                          first_start=(kt == 0), last_stop=(kt == 15))
                pb.op("vector", lambda e: e.reciprocal(dn[:, :], psN[64:65, :]))
                pb.cast(dn_bf[:, :], dn[:, :])
                pb.mm(psS[0:64, 0:1024], [(ones_bf[:, 0:64], dn_bf[:, 512 * tq:512 * (tq + 1)],
                                           psS[0:64, 512 * tq:512 * (tq + 1)]) for tq in range(2)])
                pb.cast(sc64[:, :], psN[0:64, :])
                pb.op("vector", lambda e, j=j, hb=hb, qo=qo: e.tensor_tensor(
                    out=AT[hb:hb + 64, S * j + qo:S * j + qo + 1024], in0=sc64[:, :],
                    in1=psS[0:64, 0:1024], op=ALU.mult))
        load_w(wo_d)
        for k in range(16):
            pairs = []
            for c in range(8):
                for half in range(2):
                    pairs.append((AT[:, S * c + 128 * k:S * c + 128 * (k + 1)],
                                  WB[:, 1024 * c + 512 * half:1024 * c + 512 * (half + 1)],
                                  psP[:, 512 * half:512 * (half + 1)]))
            pb.mm(None, pairs)
            pb.load(x_t[:, :], x_d[xoff + 128 * k:xoff + 128 * (k + 1), :])
            pb.op("vector", lambda e: e.tensor_add(out=scr[:, :], in0=psP[:, :], in1=bo_bc[:, :]))
            pb.op("vector", lambda e: e.tensor_add(out=scr[:, :], in0=scr[:, :], in1=x_t[:, :]))
            pb.store(y_d[xoff + 128 * k:xoff + 128 * (k + 1), :], scr[:, :])

    pb.finish()
    return pb.nc


def build_l0aa():
    pb = PB()
    S = 2048
    x_d = pb.dram("x", [TOK, D], F32)
    wdq_d = pb.dram("wdq", [D, 512], F32)
    wuq_d = pb.dram("wuq", [512, D], F32)
    wqr_d = pb.dram("wqr", [512, D], F32)
    wkr_d = pb.dram("wkr", [D, D], F32)
    wdkv_d = pb.dram("wdkv", [D, 512], F32)
    wuk_d = pb.dram("wuk", [512, D], F32)
    wuv_d = pb.dram("wuv", [512, D], F32)
    wo_d = pb.dram("wo", [D, D], F32)
    bdq_d = pb.dram("bdq", [512], F32)
    buq_d = pb.dram("buq", [D], F32)
    bqr_d = pb.dram("bqr", [D], F32)
    bkr_d = pb.dram("bkr", [D], F32)
    bdkv_d = pb.dram("bdkv", [512], F32)
    buk_d = pb.dram("buk", [D], F32)
    buv_d = pb.dram("buv", [D], F32)
    bo_d = pb.dram("bo", [D], F32)
    cos_d = pb.dram("cost", [S, 512], F32)
    sin_d = pb.dram("sint", [S, 512], F32)
    y_d = pb.dram("y", [TOK, D], F32, kind="ExternalOutput")

    XT = pb.sb("XT", [128, 8 * S], BF16)
    CC = pb.sb("CC", [128, 8 * S], BF16)
    QT = pb.sb("QT", [128, 8 * S], BF16)
    KT = pb.sb("KT", [128, 8 * S], BF16)
    VA = pb.sb("VA", [128, 16 * 1040], BF16)
    PT = pb.sb("PT", [128, 1024], BF16)
    WB = pb.sb("WB", [128, 8192], BF16)
    stag = pb.sb("stag", [128, 1024], F32)
    x_t = pb.sb("x_t", [128, D], F32)
    scr = pb.sb("scr", [128, D], F32)
    qro = pb.sb("qro", [128, D], F32)
    cos_tl = pb.sb("cos_tl", [128, 512], F32)
    sin_tl = pb.sb("sin_tl", [128, 512], F32)
    tmp5 = pb.sb("tmp5", [128, 512], F32)
    xn_bf = pb.sb("xn_bf", [128, D], BF16)
    stats = pb.sb("stats", [128, 16], F32)
    svst = stag[0:32, 0:128]
    bdq_t = pb.sb("bdq_t", [128, 4], F32)
    bdkv_t = pb.sb("bdkv_t", [128, 4], F32)
    buq_t = pb.sb("buq_t", [128, 8], F32)
    buk_t = pb.sb("buk_t", [128, 8], F32)
    b_bc = pb.sb("b_bc", [128, D], F32)
    bqr_bc = bkr_bc = buv_bc = bo_bc = b_bc
    ones_bf = pb.sb("ones_bf", [1, 128], BF16)
    dn = stag[0:1, 0:1024]
    dn_bf = XT[0:1, 0:1024]
    sc64 = XT[0:64, 2048:3072]

    psS = pb.ps("psS", [128, 1024], F32)
    psN = pb.ps("psN", [65, 1024], F32)
    psP = pb.ps("psP", [128, 1024], F32)
    ps_tr = pb.ps("ps_tr", [128, 1024], BF16)
    ps_sc = pb.ps("ps_sc", [128, 32], F32)

    pb.start()
    _setup_common(pb)
    pb.vec_to_pcols(bdq_d, 4, svst, ps_sc, bdq_t)
    pb.vec_to_pcols(bdkv_d, 4, svst, ps_sc, bdkv_t)
    pb.vec_to_pcols(buq_d, 8, svst, ps_sc, buq_t)
    pb.vec_to_pcols(buk_d, 8, svst, ps_sc, buk_t)
    pb.op("vector", lambda e: e.memset(ones_bf[:, :], 1.0))

    def load_w(w_d, rows, cols):
        nc_ = rows // 128
        for c in range(nc_):
            pb.load(stag[:, 0:cols], w_d[128 * c:128 * (c + 1), :])
            pb.cast(WB[:, cols * c:cols * (c + 1)], stag[:, 0:cols])

    def rope(ps_src, bias_bc, out_f32):
        pb.op("vector", lambda e: e.tensor_add(out=qro[:, :], in0=ps_src, in1=bias_bc[:, :]))
        o = pb.op
        o("vector", lambda e: e.tensor_mul(out=out_f32[:, 0:D:2], in0=qro[:, 0:D:2], in1=cos_tl[:, :]))
        o("vector", lambda e: e.tensor_mul(out=tmp5[:, :], in0=qro[:, 1:D:2], in1=sin_tl[:, :]))
        o("vector", lambda e: e.tensor_sub(out=out_f32[:, 0:D:2], in0=out_f32[:, 0:D:2], in1=tmp5[:, :]))
        o("vector", lambda e: e.tensor_mul(out=out_f32[:, 1:D:2], in0=qro[:, 1:D:2], in1=cos_tl[:, :]))
        o("vector", lambda e: e.tensor_mul(out=tmp5[:, :], in0=qro[:, 0:D:2], in1=sin_tl[:, :]))
        o("vector", lambda e: e.tensor_add(out=out_f32[:, 1:D:2], in0=out_f32[:, 1:D:2], in1=tmp5[:, :]))

    CQT = CC[:, 0:4 * S]
    CKT = CC[:, 4 * S:8 * S]

    for bb in range(2):
        xoff = 2048 * bb
        pb.op("vector", lambda e: e.memset(VA[:, :], 1.0))
        for k in range(16):
            pb.load(x_t[:, :], x_d[xoff + 128 * k:xoff + 128 * (k + 1), :])
            pb.ln(x_t[:, :], xn_bf[:, :], stats, scr[:, :], D)
            pb.tr_run([(xn_bf[:, 128 * c:128 * (c + 1)],
                        XT[:, S * c + 128 * k:S * c + 128 * (k + 1)])
                       for c in range(8)], ps_tr, pb.ident_bf[:, :])
        for (w_d, b_t, DSTb) in ((wdq_d, bdq_t, CQT), (wdkv_d, bdkv_t, CKT)):
            load_w(w_d, D, 512)
            for j in range(4):
                for tq in range(4):
                    pb.mm(psP[:, 0:512],
                          [(WB[:, 512 * c + 128 * j:512 * c + 128 * (j + 1)],
                            XT[:, S * c + 512 * tq:S * c + 512 * (tq + 1)]) for c in range(8)])
                    pb.op("vector", lambda e, j=j, tq=tq, DSTb=DSTb, b_t=b_t: e.tensor_scalar(
                        out=DSTb[:, S * j + 512 * tq:S * j + 512 * (tq + 1)], in0=psP[:, 0:512],
                        scalar1=b_t[:, j:j + 1], scalar2=None, op0=ALU.add))
        for (w_d, b_t, SRCb, DSTb) in ((wuq_d, buq_t, CQT, QT), (wuk_d, buk_t, CKT, KT)):
            load_w(w_d, 512, D)
            for j in range(8):
                for tq in range(4):
                    pb.mm(psP[:, 0:512],
                          [(WB[:, 1024 * c + 128 * j:1024 * c + 128 * (j + 1)],
                            SRCb[:, S * c + 512 * tq:S * c + 512 * (tq + 1)]) for c in range(4)])
                    pb.op("vector", lambda e, j=j, tq=tq, DSTb=DSTb, b_t=b_t: e.tensor_scalar(
                        out=DSTb[:, S * j + 512 * tq:S * j + 512 * (tq + 1)], in0=psP[:, 0:512],
                        scalar1=b_t[:, j:j + 1], scalar2=None, op0=ALU.add))
        for (w_d, b_d_, SRCb, nsrc, DSTb) in ((wqr_d, bqr_d, CQT, 4, QT),
                                              (wkr_d, bkr_d, XT, 8, KT)):
            pb.bc_row(b_bc, b_d_, D)
            load_w(w_d, 128 * nsrc, D)
            for k in range(16):
                pb.load(cos_tl[:, :], cos_d[128 * k:128 * (k + 1), :])
                pb.load(sin_tl[:, :], sin_d[128 * k:128 * (k + 1), :])
                pairs = []
                for c in range(nsrc):
                    for half in range(2):
                        pairs.append((SRCb[:, S * c + 128 * k:S * c + 128 * (k + 1)],
                                      WB[:, 1024 * c + 512 * half:1024 * c + 512 * (half + 1)],
                                      psP[:, 512 * half:512 * (half + 1)]))
                pb.mm(None, pairs)
                rope(psP[:, :], b_bc, scr)
                pb.cast(xn_bf[:, :], scr[:, :])
                pb.tr_run([(xn_bf[:, 128 * c:128 * (c + 1)],
                            DSTb[:, S * c + 128 * k:S * c + 128 * (k + 1)])
                           for c in range(8)], ps_tr, pb.ident_bf[:, :], evac="add")
        pb.bc_row(b_bc, buv_d, D)
        load_w(wuv_d, 512, D)
        for k in range(16):
            pairs = []
            for c in range(4):
                for half in range(2):
                    pairs.append((CKT[:, S * c + 128 * k:S * c + 128 * (k + 1)],
                                  WB[:, 1024 * c + 512 * half:1024 * c + 512 * (half + 1)],
                                  psP[:, 512 * half:512 * (half + 1)]))
            pb.mm(None, pairs)
            vk = VA[:, 1040 * k:1040 * (k + 1)].rearrange("p (h w) -> p h w", h=16, w=65)[:, :, 0:64]
            pb.op("vector", lambda e, vk=vk: e.tensor_tensor(
                out=vk, in0=psP.rearrange("p (h w) -> p h w", h=16, w=64),
                in1=buv_bc.rearrange("p (h w) -> p h w", h=16, w=64), op=ALU.add))
        AT = CC
        esc = 1.0 / float(np.sqrt(np.float32(16 + 512 + 512)))
        for h in range(16):
            hb = (h % 2) * 64
            j = h // 2
            for qh in range(2):
                qo = 1024 * qh
                for kt in range(16):
                    pb.mm(psS[:, :],
                          [(KT[hb:hb + 64, S * j + 128 * kt:S * j + 128 * (kt + 1)],
                            QT[hb:hb + 64, S * j + qo + 512 * tq:S * j + qo + 512 * (tq + 1)],
                            psS[:, 512 * tq:512 * (tq + 1)]) for tq in range(2)])
                    pb.op("scalar", lambda e: e.activation(out=PT[:, :], in_=psS[:, :],
                                                           func=AF.Exp, scale=esc))
                    pb.mm(None,
                          [(VA[:, 1040 * kt + 65 * h:1040 * kt + 65 * (h + 1)],
                            PT[:, 512 * tq:512 * (tq + 1)],
                            psN[:, 512 * tq:512 * (tq + 1)]) for tq in range(2)],
                          first_start=(kt == 0), last_stop=(kt == 15))
                pb.op("vector", lambda e: e.reciprocal(dn, psN[64:65, :]))
                pb.cast(dn_bf, dn)
                pb.mm(psS[0:64, 0:1024], [(ones_bf[:, 0:64], dn_bf[0:1, 512 * tq:512 * (tq + 1)],
                                           psS[0:64, 512 * tq:512 * (tq + 1)]) for tq in range(2)])
                pb.cast(sc64, psN[0:64, :])
                pb.op("vector", lambda e, j=j, hb=hb, qo=qo: e.tensor_tensor(
                    out=AT[hb:hb + 64, S * j + qo:S * j + qo + 1024], in0=sc64,
                    in1=psS[0:64, 0:1024], op=ALU.mult))
        pb.bc_row(b_bc, bo_d, D)
        load_w(wo_d, D, D)
        for k in range(16):
            pairs = []
            for c in range(8):
                for half in range(2):
                    pairs.append((AT[:, S * c + 128 * k:S * c + 128 * (k + 1)],
                                  WB[:, 1024 * c + 512 * half:1024 * c + 512 * (half + 1)],
                                  psP[:, 512 * half:512 * (half + 1)]))
            pb.mm(None, pairs)
            pb.load(x_t[:, :], x_d[xoff + 128 * k:xoff + 128 * (k + 1), :])
            pb.op("vector", lambda e: e.tensor_add(out=scr[:, :], in0=psP[:, :], in1=bo_bc[:, :]))
            pb.op("vector", lambda e: e.tensor_add(out=scr[:, :], in0=scr[:, :], in1=x_t[:, :]))
            pb.store(y_d[xoff + 128 * k:xoff + 128 * (k + 1), :], scr[:, :])

    pb.finish()
    return pb.nc


def build_l0ab(debug_u2=False):
    pb = PB()
    x_d = pb.dram("x", [TOK, D], F32)
    qwT_d = pb.dram("qwT", [D, 64], F32)
    kwT_d = pb.dram("kwT", [D, 64], F32)
    vwT_d = pb.dram("vwT", [D, D], F32)
    owT_d = pb.dram("owT", [D, D], F32)
    f1wT_d = pb.dram("f1wT", [D, 4096], F32)
    f2wT_d = pb.dram("f2wT", [4096, D], F32)
    ln1w_d = pb.dram("ln1w", [D], F32)
    ln1b_d = pb.dram("ln1b", [D], F32)
    ln2w_d = pb.dram("ln2w", [D], F32)
    ln2b_d = pb.dram("ln2b", [D], F32)
    bq_d = pb.dram("bq2", [64], F32)
    bk_d = pb.dram("bk2", [64], F32)
    bv_d = pb.dram("bv2", [D], F32)
    bo_d = pb.dram("bo2", [D], F32)
    f1b_d = pb.dram("f1b", [4096], F32)
    f2b_d = pb.dram("f2b", [D], F32)
    tril_d = pb.dram("tril", [128, 128], F32)
    y_d = pb.dram("y", [TOK, D], F32, kind="ExternalOutput")
    u2_d = pb.nc.dram_tensor("u2buf", [TOK, D], F32, kind="Internal").ap()

    AR = pb.sb("AR", [128, 49152], BF16)
    VW = AR[:, 0:8192]
    OW = AR[:, 8192:16384]
    XTc = AR[:, 16384:17408]
    Vbf = AR[:, 17408:18432]
    ATc = AR[:, 18432:19456]
    KVb = AR[:, 19456:20480]
    QW = AR[:, 20480:20992]
    KW = AR[:, 20992:21504]
    MISC = AR[:, 21504:22528]
    HT2 = AR[:, 0:16384]
    F2W = AR[:, 16384:49152]
    XN2T = pb.sb("XN2T", [128, 4096], BF16)

    x_t = pb.sb("x_t", [128, D], F32)
    u_t = pb.sb("u_t", [128, D], F32)
    scr = pb.sb("scr", [128, D], F32)
    scr2 = pb.sb("scr2", [128, D], F32)
    xn_bf = pb.sb("xn_bf", [128, D], BF16)
    stats = pb.sb("stats", [128, 16], F32)
    svst = pb.sb("svst", [32, 128], F32)
    stag = pb.sb("stag", [128, 1024], F32)
    ln1w_bc = pb.sb("ln1w_bc", [128, D], F32)
    ln1b_bc = pb.sb("ln1b_bc", [128, D], F32)
    ln2w_bc = pb.sb("ln2w_bc", [128, D], F32)
    ln2b_bc = pb.sb("ln2b_bc", [128, D], F32)
    bv_bc = pb.sb("bv_bc", [128, D], F32)
    bo_bc = pb.sb("bo_bc", [128, D], F32)
    f2b_bc = pb.sb("f2b_bc", [128, D], F32)
    tril_f = pb.sb("tril_f", [128, 128], F32)
    bq_t = pb.sb("bq_t", [64, 1], F32)
    bk_t = pb.sb("bk_t", [64, 1], F32)
    f1b_t = pb.sb("f1b_t", [128, 32], F32)
    qf = pb.sb("qf", [64, 128], F32)
    kf = pb.sb("kf", [64, 128], F32)
    ef = pb.sb("ef", [64, 128], F32)
    amf = pb.sb("amf", [128, 128], F32)
    dni = pb.sb("dni", [128, 4], F32)
    KVs = pb.sb("KVs", [64, 1024], F32)
    Ksf = pb.sb("Ksf", [64, 4], F32)
    Ksb = pb.sb("Ksb", [64, 1], BF16)
    onesc = pb.sb("onesc", [128, 1], BF16)
    eps6 = pb.sb("eps6", [128, 1], F32)
    wjc = pb.sb("wjc", [128, 1024], BF16)
    wjc2 = pb.sb("wjc2", [128, 1024], BF16)
    stagf2 = pb.sb("stagf2", [128, 1024], F32)

    psV = pb.ps("psV", [128, 1024], F32)
    psN2 = pb.ps("psN2", [128, 1024], F32)
    KVps = pb.ps("KVps", [64, 1024], F32)
    psM = pb.ps("psM", [128, 512], F32)
    ps_trb = pb.ps("ps_trb", [128, 1024], BF16)

    pb.start()
    _setup_common(pb)
    for (d, t) in ((ln1w_d, ln1w_bc), (ln1b_d, ln1b_bc), (ln2w_d, ln2w_bc),
                   (ln2b_d, ln2b_bc), (bv_d, bv_bc), (bo_d, bo_bc), (f2b_d, f2b_bc)):
        pb.bc_row(t, d, D)
    pb.load(tril_f[:, :], tril_d[:, :])
    pb.load(bq_t[:, :], bq_d[:, None])
    pb.load(bk_t[:, :], bk_d[:, None])
    pb.vec_to_pcols(f1b_d, 32, svst, psM[:, 0:512], f1b_t)
    pb.op("vector", lambda e: e.memset(onesc[:, :], 1.0))
    pb.op("vector", lambda e: e.memset(eps6[:, :], 1e-6))
    for c in range(8):
        pb.load(stag[:, 0:64], qwT_d[128 * c:128 * (c + 1), :])
        pb.cast(QW[:, 64 * c:64 * (c + 1)], stag[:, 0:64])
        pb.load(stag[:, 0:64], kwT_d[128 * c:128 * (c + 1), :])
        pb.cast(KW[:, 64 * c:64 * (c + 1)], stag[:, 0:64])
        pb.load(stag[:, :], vwT_d[128 * c:128 * (c + 1), :])
        pb.cast(VW[:, 1024 * c:1024 * (c + 1)], stag[:, :])
        pb.load(stag[:, :], owT_d[128 * c:128 * (c + 1), :])
        pb.cast(OW[:, 1024 * c:1024 * (c + 1)], stag[:, :])

    Qbf = MISC[0:64, 0:128]
    Kbf = MISC[0:64, 128:256]
    Amb = MISC[:, 256:384]
    AmT = MISC[:, 384:512]

    for bb in range(2):
        xoff = 2048 * bb
        pb.op("vector", lambda e: e.memset(KVs[:, :], 0.0))
        pb.op("vector", lambda e: e.memset(Ksf[:, :], 0.0))
        for k in range(16):
            pb.load(x_t[:, :], x_d[xoff + 128 * k:xoff + 128 * (k + 1), :])
            pb.ln(x_t[:, :], u_t[:, :], stats, scr[:, :], D)
            pb.ln(u_t[:, :], scr2[:, :], stats, scr[:, :], D)
            pb.op("vector", lambda e: e.tensor_mul(out=scr2[:, :], in0=scr2[:, :], in1=ln1w_bc[:, :]))
            pb.op("vector", lambda e: e.tensor_add(out=scr2[:, :], in0=scr2[:, :], in1=ln1b_bc[:, :]))
            pb.cast(xn_bf[:, :], scr2[:, :])
            pb.tr_run([(xn_bf[:, 128 * c:128 * (c + 1)], XTc[:, 128 * c:128 * (c + 1)])
                       for c in range(8)], ps_trb, pb.ident_bf[:, :])
            for (WW, b_t, dst) in ((QW, bq_t, qf), (KW, bk_t, kf)):
                pb.mm(psM[0:64, 0:128],
                      [(WW[:, 64 * c:64 * (c + 1)], XTc[:, 128 * c:128 * (c + 1)],
                        psM[0:64, 0:128]) for c in range(8)])
                pb.op("vector", lambda e, b_t=b_t, dst=dst: e.tensor_scalar(
                    out=dst[:, :], in0=psM[0:64, 0:128], scalar1=b_t[:, 0:1],
                    scalar2=None, op0=ALU.add))
                pb.op("vector", lambda e, dst=dst: e.tensor_scalar(
                    out=ef[:, :], in0=dst[:, :], scalar1=0.0, scalar2=None, op0=ALU.min))
                pb.op("scalar", lambda e: e.activation(out=ef[:, :], in_=ef[:, :], func=AF.Exp))
                pb.op("scalar", lambda e, dst=dst: e.activation(out=dst[:, :], in_=dst[:, :], func=AF.Relu))
                pb.op("vector", lambda e, dst=dst: e.tensor_add(out=dst[:, :], in0=dst[:, :], in1=ef[:, :]))
            pb.cast(Qbf, qf[:, :])
            pb.cast(Kbf, kf[:, :])
            pairs = []
            for c in range(8):
                for half in range(2):
                    pairs.append((XTc[:, 128 * c:128 * (c + 1)],
                                  VW[:, 1024 * c + 512 * half:1024 * c + 512 * (half + 1)],
                                  psV[:, 512 * half:512 * (half + 1)]))
            pb.mm(None, pairs)
            pb.op("vector", lambda e: e.tensor_add(out=scr2[:, :], in0=psV[:, :], in1=bv_bc[:, :]))
            pb.cast(Vbf[:, :], scr2[:, :])
            pb.mm(psM[:, 0:128], [(Qbf, Kbf, psM[:, 0:128])])
            pb.op("vector", lambda e: e.tensor_mul(out=amf[:, :], in0=psM[:, 0:128], in1=tril_f[:, :]))
            pb.op("vector", lambda e: e.reduce_sum(dni[:, 0:1], amf[:, :], axis=AX.X))
            pb.cast(Amb, amf[:, :])
            pb.tr(ps_trb[:, 0:128], Amb, AmT, pb.ident_bf[:, :])
            pairs = [(AmT, Vbf[:, 0:512], psN2[:, 0:512]),
                     (AmT, Vbf[:, 512:1024], psN2[:, 512:1024])]
            if k > 0:
                pairs += [(Qbf, KVb[0:64, 0:512], psN2[:, 0:512]),
                          (Qbf, KVb[0:64, 512:1024], psN2[:, 512:1024]),
                          (Qbf, Ksb[:, 0:1], psM[:, 128:129])]
            pb.mm(None, pairs)
            KiT_ap = MISC[:, 512:576]
            pb.op("tensor", lambda e: e.transpose(ps_trb[:, 0:64], Kbf, pb.ident_bf[0:64, 0:64]))
            pb.op("scalar", lambda e, KiT_ap=KiT_ap: e.copy(out=KiT_ap, in_=ps_trb[:, 0:64]))
            pb.mm(None, [(KiT_ap, Vbf[:, 0:512], KVps[:, 0:512]),
                         (KiT_ap, Vbf[:, 512:1024], KVps[:, 512:1024]),
                         (KiT_ap, onesc[:, 0:1], psM[0:64, 256:257])])
            pb.op("vector", lambda e: e.tensor_add(out=KVs[:, :], in0=KVs[:, :], in1=KVps[:, :]))
            pb.op("vector", lambda e: e.tensor_add(out=Ksf[:, 0:1], in0=Ksf[:, 0:1], in1=psM[0:64, 256:257]))
            pb.cast(KVb[0:64, :], KVs[:, :])
            pb.cast(Ksb[:, :], Ksf[:, 0:1])
            if k > 0:
                pb.op("vector", lambda e: e.tensor_add(out=dni[:, 1:2], in0=psM[:, 128:129], in1=dni[:, 0:1]))
            else:
                pb.op("scalar", lambda e: e.copy(out=dni[:, 1:2], in_=dni[:, 0:1]))
            pb.op("vector", lambda e: e.tensor_add(out=dni[:, 1:2], in0=dni[:, 1:2], in1=eps6[:, 0:1]))
            pb.op("vector", lambda e: e.reciprocal(dni[:, 2:3], dni[:, 1:2]))
            pb.op("vector", lambda e: e.tensor_scalar(out=scr2[:, :], in0=psN2[:, :],
                                                      scalar1=dni[:, 2:3], scalar2=None, op0=ALU.mult))
            pb.cast(xn_bf[:, :], scr2[:, :])
            pb.tr_run([(xn_bf[:, 128 * c:128 * (c + 1)], ATc[:, 128 * c:128 * (c + 1)])
                       for c in range(8)], ps_trb, pb.ident_bf[:, :])
            pairs = []
            for c in range(8):
                for half in range(2):
                    pairs.append((ATc[:, 128 * c:128 * (c + 1)],
                                  OW[:, 1024 * c + 512 * half:1024 * c + 512 * (half + 1)],
                                  psV[:, 512 * half:512 * (half + 1)]))
            pb.mm(None, pairs)
            pb.op("vector", lambda e: e.tensor_add(out=scr2[:, :], in0=psV[:, :], in1=bo_bc[:, :]))
            pb.op("vector", lambda e: e.tensor_add(out=scr2[:, :], in0=scr2[:, :], in1=u_t[:, :]))
            pb.store(u2_d[xoff + 128 * k:xoff + 128 * (k + 1), :], scr2[:, :])
            if debug_u2:
                pb.store(y_d[xoff + 128 * k:xoff + 128 * (k + 1), :], scr2[:, :])

    if debug_u2:
        pb.finish()
        return pb.nc
    f1v = f1wT_d.rearrange("(c p) f -> p c f", p=128)
    for c in range(32):
        pb.load(stag[:, :], f2wT_d[128 * c:128 * (c + 1), :])
        pb.cast(F2W[:, 1024 * c:1024 * (c + 1)], stag[:, :])
    for blk in range(8):
        for kt in range(4):
            k = 4 * blk + kt
            pb.load(u_t[:, :], u2_d[128 * k:128 * (k + 1), :])
            pb.ln(u_t[:, :], scr2[:, :], stats, scr[:, :], D)
            pb.op("vector", lambda e: e.tensor_mul(out=scr2[:, :], in0=scr2[:, :], in1=ln2w_bc[:, :]))
            pb.op("vector", lambda e: e.tensor_add(out=scr2[:, :], in0=scr2[:, :], in1=ln2b_bc[:, :]))
            pb.cast(xn_bf[:, :], scr2[:, :])
            pb.tr_run([(xn_bf[:, 128 * c:128 * (c + 1)],
                        XN2T[:, 512 * c + 128 * kt:512 * c + 128 * (kt + 1)])
                       for c in range(8)], ps_trb, pb.ident_bf[:, :])
        STGF = [stag, stagf2]
        WJC = [wjc, wjc2]
        pb.load(STGF[0][:, :].rearrange("p (c f) -> p c f", c=8, f=128),
                f1v[:, :, 0:128], after_n=pb.n)
        for j in range(32):
            bi = j % 2
            pb.cast(WJC[bi][:, :], STGF[bi][:, :])
            ck = pb.n
            pb.mm(psM[:, 0:512],
                  [(WJC[bi][:, 128 * c:128 * (c + 1)], XN2T[:, 512 * c:512 * (c + 1)])
                   for c in range(8)])
            pb.op("scalar", lambda e, j=j: e.activation(
                out=HT2[:, 512 * j:512 * (j + 1)], in_=psM[:, 0:512], func=AF.Gelu,
                bias=f1b_t[:, j:j + 1]))
            if j + 1 < 32:
                pb.load(STGF[1 - bi][:, :].rearrange("p (c f) -> p c f", c=8, f=128),
                        f1v[:, :, 128 * (j + 1):128 * (j + 2)], after_n=ck)
        for kt in range(4):
            k = 4 * blk + kt
            pairs = []
            for c in range(32):
                for half in range(2):
                    pairs.append((HT2[:, 512 * c + 128 * kt:512 * c + 128 * (kt + 1)],
                                  F2W[:, 1024 * c + 512 * half:1024 * c + 512 * (half + 1)],
                                  psV[:, 512 * half:512 * (half + 1)]))
            pb.mm(None, pairs)
            pb.load(u_t[:, :], u2_d[128 * k:128 * (k + 1), :])
            pb.load(x_t[:, :], x_d[128 * k:128 * (k + 1), :])
            pb.op("vector", lambda e: e.tensor_add(out=scr[:, :], in0=psV[:, :], in1=f2b_bc[:, :]))
            pb.op("vector", lambda e: e.tensor_add(out=scr[:, :], in0=scr[:, :], in1=u_t[:, :]))
            pb.op("vector", lambda e: e.tensor_add(out=scr[:, :], in0=scr[:, :], in1=x_t[:, :]))
            pb.store(y_d[128 * k:128 * (k + 1), :], scr[:, :])

    pb.finish()
    return pb.nc


def build_l1aa():
    pb = PB()
    x_d = pb.dram("x", [TOK, D], F32)
    winT_d = pb.dram("winT", [D, 3072], F32)
    binp_d = pb.dram("binp", [3072], F32)
    woutT_d = pb.dram("woutT", [D, D], F32)
    bout_d = pb.dram("bout", [D], F32)
    y_d = pb.dram("y", [TOK, D], F32, kind="ExternalOutput")

    WIN = pb.sb("WIN", [128, 24576], BF16)
    OWT = pb.sb("OWT", [128, 8192], BF16)
    QL = pb.sb("QL", [128, 16384], BF16)
    KL = pb.sb("KL", [128, 16384], BF16)
    VL = pb.sb("VL", [128, 16384], BF16)
    binp_bc = pb.sb("binp_bc", [128, 3072], F32)
    bout_bc = pb.sb("bout_bc", [128, D], F32)
    stag = pb.sb("stag", [128, 1024], F32)
    XTc = pb.sb("XTc", [128, 1024], BF16)
    CXT = pb.sb("CXT", [128, 1024], BF16)
    x_t = pb.sb("x_t", [128, D], F32)
    scr = pb.sb("scr", [128, D], F32)
    tmp = pb.sb("tmp", [128, D], F32)
    CX = pb.sb("CX", [128, D], F32)
    xn_bf = pb.sb("xn_bf", [128, D], BF16)
    SC = pb.sb("SC", [128, 256], F32)
    P = pb.sb("P", [128, 256], F32)
    lden = pb.sb("lden", [128, 16], F32)
    rl = pb.sb("rl", [128, 16], F32)
    stats = pb.sb("stats", [128, 16], F32)

    psL = pb.ps("psL", [128, 1536], F32)
    psO = pb.ps("psO", [128, 1024], F32)
    ps_tr = pb.ps("ps_tr", [128, 1024], BF16)

    pb.start()
    _setup_common(pb)
    pb.bc_row(binp_bc, binp_d, 3072)
    pb.bc_row(bout_bc, bout_d, D)
    for c in range(8):
        for q in range(3):
            pb.load(stag[:, :], winT_d[128 * c:128 * (c + 1), 1024 * q:1024 * (q + 1)])
            pb.cast(WIN[:, 3072 * c + 1024 * q:3072 * c + 1024 * (q + 1)], stag[:, :])
        pb.load(stag[:, :], woutT_d[128 * c:128 * (c + 1), :])
        pb.cast(OWT[:, 1024 * c:1024 * (c + 1)], stag[:, :])

    qpat = lambda ap, off: ap.rearrange("p (h z j) -> p h z j", h=8, z=3, j=64)[:, :, off, :]

    for hf in range(2):
        for s in range(16):
            i = 2 * s + hf
            pb.load(x_t[:, :], x_d[128 * i:128 * (i + 1), :])
            pb.ln(x_t[:, :], xn_bf[:, :], stats, scr[:, :], D)
            pb.tr_run([(xn_bf[:, 128 * c:128 * (c + 1)], XTc[:, 128 * c:128 * (c + 1)])
                       for c in range(8)], ps_tr, pb.ident_bf[:, :])
            for fh in range(2):
                pairs = []
                for c in range(8):
                    for q in range(3):
                        pairs.append((XTc[:, 128 * c:128 * (c + 1)],
                                      WIN[:, 3072 * c + 1536 * fh + 512 * q:3072 * c + 1536 * fh + 512 * (q + 1)],
                                      psL[:, 512 * q:512 * (q + 1)]))
                pb.mm(None, pairs)
                for (off, DST) in ((0, QL), (1, KL), (2, VL)):
                    pb.op("vector", lambda e, off=off, DST=DST, s=s, fh=fh: e.tensor_tensor(
                        out=DST[:, 1024 * s + 512 * fh:1024 * s + 512 * fh + 512].rearrange(
                            "p (h j) -> p h j", h=8, j=64),
                        in0=qpat(psL, off),
                        in1=qpat(binp_bc[:, 1536 * fh:1536 * (fh + 1)], off), op=ALU.add))
        for s in range(16):
            i = 2 * s + hf
            for t in range(16):
                pb.op("vector", lambda e, s=s, t=t: e.tensor_mul(
                    out=tmp[:, :], in0=QL[:, 1024 * s:1024 * (s + 1)],
                    in1=KL[:, 1024 * t:1024 * (t + 1)]))
                pb.op("vector", lambda e, t=t: e.reduce_sum(
                    SC[:, t:256:16], tmp.rearrange("p (h o) -> p h o", h=16, o=64), axis=AX.X))
            pb.op("scalar", lambda e: e.activation(out=P[:, :], in_=SC[:, :], func=AF.Exp, scale=0.125))
            pb.op("vector", lambda e: e.reduce_sum(
                lden[:, :], P.rearrange("p (h t) -> p h t", h=16, t=16), axis=AX.X))
            pb.op("vector", lambda e: e.reciprocal(rl[:, :], lden[:, :]))
            for t in range(16):
                pbc = P[:, t:256:16].rearrange("p (h o) -> p h o", h=16, o=1).broadcast_to([128, 16, 64])
                dst = CX if t == 0 else tmp
                pb.op("vector", lambda e, t=t, pbc=pbc, dst=dst: e.tensor_tensor(
                    out=dst.rearrange("p (h o) -> p h o", h=16, o=64), in0=pbc,
                    in1=VL[:, 1024 * t:1024 * (t + 1)].rearrange("p (h o) -> p h o", h=16, o=64),
                    op=ALU.mult))
                if t > 0:
                    pb.op("vector", lambda e: e.tensor_add(out=CX[:, :], in0=CX[:, :], in1=tmp[:, :]))
            rbc = rl.rearrange("p (h o) -> p h o", h=16, o=1).broadcast_to([128, 16, 64])
            pb.op("vector", lambda e, rbc=rbc: e.tensor_tensor(
                out=CX.rearrange("p (h o) -> p h o", h=16, o=64),
                in0=CX.rearrange("p (h o) -> p h o", h=16, o=64), in1=rbc, op=ALU.mult))
            pb.cast(xn_bf[:, :], CX[:, :])
            pb.tr_run([(xn_bf[:, 128 * c:128 * (c + 1)], CXT[:, 128 * c:128 * (c + 1)])
                       for c in range(8)], ps_tr, pb.ident_bf[:, :])
            pairs = []
            for c in range(8):
                for half in range(2):
                    pairs.append((CXT[:, 128 * c:128 * (c + 1)],
                                  OWT[:, 1024 * c + 512 * half:1024 * c + 512 * (half + 1)],
                                  psO[:, 512 * half:512 * (half + 1)]))
            pb.mm(None, pairs)
            pb.load(x_t[:, :], x_d[128 * i:128 * (i + 1), :])
            pb.op("vector", lambda e: e.tensor_add(out=scr[:, :], in0=psO[:, :], in1=bout_bc[:, :]))
            pb.op("vector", lambda e: e.tensor_add(out=scr[:, :], in0=scr[:, :], in1=x_t[:, :]))
            pb.store(y_d[128 * i:128 * (i + 1), :], scr[:, :])

    pb.finish()
    return pb.nc


def run_l1aa(x, inp):
    nc = _get("l1aa", build_l1aa)
    shared = {
        "winT": _t(inp["L1_attn_a_in_proj_weight"]),
        "binp": np.asarray(inp["L1_attn_a_in_proj_bias"], np.float32),
        "woutT": _t(inp["L1_attn_a_out_proj_weight"]),
        "bout": np.asarray(inp["L1_attn_a_out_proj_bias"], np.float32),
        "ident": IDENT,
    }
    shards = _shard_seq(x)
    in_maps = [dict(shared, x=s) for s in shards]
    return _unshard_seq(_run(nc, in_maps))


def run_l0aa(x, inp):
    nc = _get("l0aa", build_l0aa)
    pos = np.arange(2048, dtype=np.float32)[:, None]
    ids = np.arange(512, dtype=np.float32)
    theta = (10000.0 ** (-2.0 * ids)) / 512.0
    r = pos * theta[None, :]
    shared = {
        "wdq": np.asarray(inp["L0_attn_a_Wd_q_w"], np.float32),
        "wuq": np.asarray(inp["L0_attn_a_Wu_q_w"], np.float32),
        "wqr": np.asarray(inp["L0_attn_a_Wq_r_w"], np.float32),
        "wkr": np.asarray(inp["L0_attn_a_Wk_r_w"], np.float32),
        "wdkv": np.asarray(inp["L0_attn_a_Wd_kv_w"], np.float32),
        "wuk": np.asarray(inp["L0_attn_a_Wu_k_w"], np.float32),
        "wuv": np.asarray(inp["L0_attn_a_Wu_v_w"], np.float32),
        "wo": np.asarray(inp["L0_attn_a_Wo_w"], np.float32),
        "bdq": np.asarray(inp["L0_attn_a_Wd_q_b"], np.float32),
        "buq": np.asarray(inp["L0_attn_a_Wu_q_b"], np.float32),
        "bqr": np.asarray(inp["L0_attn_a_Wq_r_b"], np.float32),
        "bkr": np.asarray(inp["L0_attn_a_Wk_r_b"], np.float32),
        "bdkv": np.asarray(inp["L0_attn_a_Wd_kv_b"], np.float32),
        "buk": np.asarray(inp["L0_attn_a_Wu_k_b"], np.float32),
        "buv": np.asarray(inp["L0_attn_a_Wu_v_b"], np.float32),
        "bo": np.asarray(inp["L0_attn_a_Wo_b"], np.float32),
        "cost": np.cos(r).astype(np.float32),
        "sint": np.sin(r).astype(np.float32),
        "ident": IDENT,
    }
    shards = _shard_batch(x)
    in_maps = [dict(shared, x=s) for s in shards]
    return _unshard_batch(_run(nc, in_maps))


def run_l0ab(x, inp):
    nc = _get("l0ab", build_l0ab)
    shared = {
        "qwT": _t(inp["L0_attn_b_q_w"]),
        "kwT": _t(inp["L0_attn_b_k_w"]),
        "vwT": _t(inp["L0_attn_b_v_w"]),
        "owT": _t(inp["L0_attn_b_o_w"]),
        "f1wT": _t(inp["L0_attn_b_f1_w"]),
        "f2wT": _t(inp["L0_attn_b_f2_w"]),
        "ln1w": np.asarray(inp["L0_attn_b_ln1_w"], np.float32),
        "ln1b": np.asarray(inp["L0_attn_b_ln1_b"], np.float32),
        "ln2w": np.asarray(inp["L0_attn_b_ln2_w"], np.float32),
        "ln2b": np.asarray(inp["L0_attn_b_ln2_b"], np.float32),
        "bq2": np.asarray(inp["L0_attn_b_q_b"], np.float32),
        "bk2": np.asarray(inp["L0_attn_b_k_b"], np.float32),
        "bv2": np.asarray(inp["L0_attn_b_v_b"], np.float32),
        "bo2": np.asarray(inp["L0_attn_b_o_b"], np.float32),
        "f1b": np.asarray(inp["L0_attn_b_f1_b"], np.float32),
        "f2b": np.asarray(inp["L0_attn_b_f2_b"], np.float32),
        "tril": np.tril(np.ones((128, 128), np.float32)),
        "ident": IDENT,
    }
    shards = _shard_batch(x)
    in_maps = [dict(shared, x=s) for s in shards]
    return _unshard_batch(_run(nc, in_maps))


def run_l0ffn(x, inp):
    nc = _get("l0ffn", build_l0ffn)
    wfc = np.asarray(inp["L0_ffn_w_fc"], np.float32)
    wpj = np.asarray(inp["L0_ffn_w_proj"], np.float32)
    shared = {
        "wgt": _t(inp["L0_ffn_w_gate"]),
        "wfct": np.ascontiguousarray(wfc.transpose(0, 2, 1)),
        "wpjt": np.ascontiguousarray(wpj.transpose(0, 2, 1)),
        "ident": IDENT,
    }
    shards = _shard_batch(x)
    in_maps = [dict(shared, x=s) for s in shards]
    return _unshard_batch(_run(nc, in_maps))


def run_l1ab(x, inp):
    nc = _get("l1ab", build_l1ab)
    shared = {
        "wq": np.asarray(inp["L1_attn_b_Wq"], np.float32),
        "wk": np.asarray(inp["L1_attn_b_Wk"], np.float32),
        "wv": np.asarray(inp["L1_attn_b_Wv"], np.float32),
        "wo": np.asarray(inp["L1_attn_b_Wo"], np.float32),
        "bq": np.asarray(inp["L1_attn_b_bq"], np.float32),
        "bk": np.asarray(inp["L1_attn_b_bk"], np.float32),
        "bv": np.asarray(inp["L1_attn_b_bv"], np.float32),
        "bo": np.asarray(inp["L1_attn_b_bo"], np.float32),
        "ident": IDENT,
    }
    shards = _shard_batch(x)
    in_maps = [dict(shared, x=s) for s in shards]
    return _unshard_batch(_run(nc, in_maps))


_BUILT = {}


def _get(name, builder):
    if name not in _BUILT:
        _BUILT[name] = builder()
    return _BUILT[name]


def _shard_batch(x):
    return [np.ascontiguousarray(s.reshape(TOK, D)) for s in np.split(x, NCORES, axis=0)]


def _unshard_batch(ys):
    return np.concatenate([y.reshape(2, 2048, D) for y in ys], axis=0)


def _shard_seq(x):
    return [np.ascontiguousarray(s.reshape(TOK, D)) for s in np.split(x, NCORES, axis=1)]


def _unshard_seq(ys):
    return np.concatenate([y.reshape(16, 256, D) for y in ys], axis=1)


def _run(nc, in_maps):
    res = run_bass_kernel_spmd(nc, in_maps, core_ids=list(range(NCORES)))
    return [r["y"] for r in res.results]


def _t(a):
    return np.ascontiguousarray(np.asarray(a, np.float32).T)


def run_l1ffn(x, inp):
    nc = _get("l1ffn", build_l1ffn)
    shared = {
        "w1t": _t(inp["L1_ffn_w1_w"]), "w2t": _t(inp["L1_ffn_w2_w"]),
        "h1t": _t(inp["L1_ffn_h1_w"]), "h2t": _t(inp["L1_ffn_h2_w"]),
        "w3t": _t(inp["L1_ffn_w3_w"]),
        "s1": np.asarray(inp["L1_ffn_w1_s"], np.float32),
        "s2": np.asarray(inp["L1_ffn_w2_s"], np.float32),
        "hs1": np.asarray(inp["L1_ffn_h1_s"], np.float32),
        "hs2": np.asarray(inp["L1_ffn_h2_s"], np.float32),
        "s3": np.asarray(inp["L1_ffn_w3_s"], np.float32),
        "ident": IDENT,
    }
    shards = _shard_batch(x)
    in_maps = [dict(shared, x=s) for s in shards]
    return _unshard_batch(_run(nc, in_maps))


def kernel(**inp):
    x = np.asarray(inp["L0_attn_a_x"], np.float32)
    x = run_l0aa(x, inp)
    x = run_l0ffn(x, inp)
    x = run_l0ab(x, inp)
    x = run_l0ffn(x, inp)
    x = run_l1aa(x, inp)
    x = run_l1ffn(x, inp)
    x = run_l1ab(x, inp)
    x = run_l1ffn(x, inp)
    return x


# version 07 (final)
# speedup vs baseline: 1.2074x; 1.1003x over previous
import numpy as np
import concourse.bass as bass
import concourse.mybir as mybir
from concourse.bass_utils import run_bass_kernel_spmd
from contextlib import ExitStack

F32 = mybir.dt.float32
BF16 = mybir.dt.bfloat16
AF = mybir.ActivationFunctionType
ALU = mybir.AluOpType
AX = mybir.AxisListType

NCORES = 8
D = 1024
TOK = 4096
NT = TOK // 128
LN_EPS = 1e-5

IDENT = np.eye(128, dtype=np.float32)


class PB:

    def __init__(self):
        self.nc = bass.Bass("TRN2", target_bir_lowering=False)
        self.es = ExitStack()
        self._names = {}
        self.block = None
        self.g = None
        self.n = 0

    def _u(self, p):
        i = self._names.get(p, 0)
        self._names[p] = i + 1
        return f"{p}_{i}"

    def dram(self, name, shape, dtype=F32, kind="ExternalInput"):
        return self.nc.dram_tensor(name, shape, dtype, kind=kind).ap()

    def sb(self, prefix, shape, dtype=F32):
        return self.es.enter_context(self.nc.sbuf_tensor(self._u(prefix), shape, dtype))

    def ps(self, prefix, shape, dtype=F32):
        return self.es.enter_context(self.nc.psum_tensor(self._u(prefix), shape, dtype))

    def start(self):
        self.g = self.es.enter_context(self.nc.semaphore("g"))
        self.L = self.es.enter_context(self.nc.semaphore("L"))
        self.S = self.es.enter_context(self.nc.semaphore("S"))
        self.l = 0
        self.s = 0
        self.block = self.es.enter_context(self.nc.Block())

    def op(self, eng, f, inc=1, swait=None):
        need = self.n
        lneed = self.l
        sect = getattr(self.block, eng)

        def body(e):
            e.wait_ge(self.g, need)
            if lneed:
                e.wait_ge(self.L, lneed)
            if swait:
                e.wait_ge(self.S, swait)
            f(e).then_inc(self.g, inc)

        sect(body)
        self.n += inc

    def group(self, eng, fs):
        if not fs:
            return
        need = self.n
        lneed = self.l
        sect = getattr(self.block, eng)

        def body(e):
            e.wait_ge(self.g, need)
            if lneed:
                e.wait_ge(self.L, lneed)
            last = None
            for f in fs:
                last = f(e)
            last.then_inc(self.g, 1)

        sect(body)
        self.n += 1

    def load(self, dst, src, after_n=None):
        need = self.n if after_n is None else after_n
        lneed_prev = 0
        sect = self.block.sync

        def body(e):
            e.wait_ge(self.g, need)
            e.dma_start(out=dst, in_=src).then_inc(self.L, 16)

        sect(body)
        self.l += 16

    def store(self, dst, src):
        self.op("gpsimd", lambda e: e.dma_start(out=dst, in_=src), inc=16)

    def store_async(self, dst, src):
        need = self.n
        lneed = self.l

        def body(e):
            e.wait_ge(self.g, need)
            if lneed:
                e.wait_ge(self.L, lneed)
            e.dma_start(out=dst, in_=src).then_inc(self.S, 16)

        self.block.gpsimd(body)
        self.s += 16

    def drain_stores(self):
        sv = self.s

        def body(e):
            if sv:
                e.wait_ge(self.S, sv)
            e.nop().then_inc(self.g, 1)

        self.block.gpsimd(body)
        self.n += 1

    def mm(self, psum_ap, pairs, first_start=True, last_stop=True):
        fs = []
        keys = []
        for i, pr in enumerate(pairs):
            out = pr[2] if len(pr) > 2 and pr[2] is not None else psum_ap
            keys.append(str(out))
        seen = set()
        last = {}
        for i, k in enumerate(keys):
            last[k] = i
        for i, pr in enumerate(pairs):
            lhsT, rhs = pr[0], pr[1]
            out = pr[2] if len(pr) > 2 and pr[2] is not None else psum_ap
            st = first_start and (keys[i] not in seen)
            seen.add(keys[i])
            sp = last_stop and (last[keys[i]] == i)
            fs.append(lambda e, a=out, l=lhsT, r=rhs, s1=st, s2=sp:
                      e.matmul(a, l, r, start=s1, stop=s2))
        self.group("tensor", fs)

    def cast(self, dst, src):
        self.op("scalar", lambda e: e.copy(out=dst, in_=src))

    def tr(self, ps_tr, src_bf, dst, ident_bf):
        self.op("tensor", lambda e: e.transpose(ps_tr, src_bf, ident_bf))
        self.op("scalar", lambda e: e.copy(out=dst, in_=ps_tr))

    def tr_run(self, pairs, ps8, ident_bf, evac="copy"):
        self.group("tensor", [
            (lambda e, s=s_, i=i_: e.transpose(ps8[:, 128 * i:128 * (i + 1)], s, ident_bf))
            for i_, (s_, d_) in enumerate(pairs)])
        if evac == "copy":
            self.group("scalar", [
                (lambda e, d=d_, i=i_: e.copy(out=d, in_=ps8[:, 128 * i:128 * (i + 1)]))
                for i_, (s_, d_) in enumerate(pairs)])
        else:
            self.group("vector", [
                (lambda e, d=d_, i=i_: e.tensor_tensor(
                    out=d, in0=ps8[:, 128 * i:128 * (i + 1)], in1=d, op=ALU.add))
                for i_, (s_, d_) in enumerate(pairs)])

    def ln(self, x_ap, out_ap, stats, scr, Dd, eps=LN_EPS):
        o = self.op
        o("scalar", lambda e: e.activation(out=scr, in_=x_ap, func=AF.Square,
                                           accum_out=stats[:, 1:2]))
        o("vector", lambda e: e.reduce_sum(stats[:, 0:1], x_ap, axis=AX.X))
        o("scalar", lambda e: e.mul(out=stats[:, 2:3], in_=stats[:, 0:1], mul=1.0 / Dd))
        o("scalar", lambda e: e.mul(out=stats[:, 4:5], in_=stats[:, 1:2], mul=1.0 / Dd))
        o("vector", lambda e: e.tensor_mul(out=stats[:, 3:4], in0=stats[:, 2:3], in1=stats[:, 2:3]))
        o("vector", lambda e: e.tensor_sub(out=stats[:, 5:6], in0=stats[:, 4:5], in1=stats[:, 3:4]))
        o("scalar", lambda e: e.activation(out=stats[:, 6:7], in_=stats[:, 5:6], func=AF.Ln,
                                           bias=self.eps_t[:, 0:1]))
        o("scalar", lambda e: e.activation(out=stats[:, 7:8], in_=stats[:, 6:7], func=AF.Exp, scale=-0.5))
        o("scalar", lambda e: e.mul(out=stats[:, 8:9], in_=stats[:, 2:3], mul=-1.0))
        o("vector", lambda e: e.tensor_scalar(out=out_ap, in0=x_ap, scalar1=stats[:, 8:9],
                                              scalar2=stats[:, 7:8], op0=ALU.add, op1=ALU.mult))

    def vec_to_pcols(self, dram_vec, n, stage, ps_sc, out_tile):
        self.load(stage[0:n, :], dram_vec.rearrange("(j p) -> j p", p=128))
        self.op("tensor", lambda e: e.transpose(ps_sc[:, 0:n], stage[0:n, :], self.ident_f[0:n, 0:n]))
        self.op("scalar", lambda e: e.copy(out=out_tile[:, 0:n], in_=ps_sc[:, 0:n]))

    def bc_row(self, tile, dram_vec, Dd):
        self.load(tile[:, 0:Dd], dram_vec[None, :].broadcast_to([128, Dd]))

    def finish(self):
        nv, lv, sv = self.n, self.l, self.s

        def body(e):
            e.wait_ge(self.g, nv)
            if lv:
                e.wait_ge(self.L, lv)
            if sv:
                e.wait_ge(self.S, sv)
            e.nop()

        self.block.sync(body)
        self.es.close()
        return self.nc


def _setup_common(pb):
    ident_d = pb.dram("ident", [128, 128], F32)
    pb.ident_f = pb.sb("ident_f", [128, 128], F32)
    pb.ident_bf = pb.sb("ident_bf", [128, 128], BF16)
    pb.load(pb.ident_f[:, :], ident_d[:, :])
    pb.cast(pb.ident_bf[:, :], pb.ident_f[:, :])
    pb.eps_t = pb.sb("eps_t", [128, 2], F32)
    pb.op("vector", lambda e: e.memset(pb.eps_t[:, 0:1], LN_EPS))


def build_l1ffn():
    pb = PB()
    x_d = pb.dram("x", [TOK, D], F32)
    w1t_d = pb.dram("w1t", [D, 4096], F32)
    w2t_d = pb.dram("w2t", [D, 4096], F32)
    h1t_d = pb.dram("h1t", [4096, 4096], F32)
    h2t_d = pb.dram("h2t", [4096, 4096], F32)
    w3t_d = pb.dram("w3t", [4096, D], F32)
    s1_d = pb.dram("s1", [4096], F32)
    s2_d = pb.dram("s2", [4096], F32)
    hs1_d = pb.dram("hs1", [4096], F32)
    hs2_d = pb.dram("hs2", [4096], F32)
    s3_d = pb.dram("s3", [D], F32)
    y_d = pb.dram("y", [TOK, D], F32, kind="ExternalOutput")
    H0_d = pb.nc.dram_tensor("H0buf", [4096, TOK], BF16, kind="Internal").ap()
    H1_d = pb.nc.dram_tensor("H1buf", [4096, TOK], BF16, kind="Internal").ap()
    H2_d = pb.nc.dram_tensor("H2buf", [4096, TOK], BF16, kind="Internal").ap()

    BUF_A = pb.sb("bufA", [128, 32768], BF16)
    BUF_B = pb.sb("bufB", [128, 32768], BF16)
    stagw = pb.sb("stagw", [128, 4096], F32)
    stagw2 = pb.sb("stagw2", [128, 4096], F32)
    wc_bf = pb.sb("wc_bf", [128, 4096], BF16)
    wc_bf2 = pb.sb("wc_bf2", [128, 4096], BF16)
    stag2 = pb.sb("stag2", [128, 1024], F32)
    h0_bf2 = pb.sb("h0_bf2", [128, 512], BF16)
    u1_bf2 = pb.sb("u1_bf2", [128, 512], BF16)
    x_t = pb.sb("x_t", [128, D], F32)
    scr = pb.sb("scr", [128, D], F32)
    scr5 = pb.sb("scr5", [128, 512], F32)
    xn_bf = pb.sb("xn_bf", [128, D], BF16)
    u1_bf = pb.sb("u1_bf", [128, 512], BF16)
    h0_bf = pb.sb("h0_bf", [128, 512], BF16)
    stats = pb.sb("stats", [128, 16], F32)
    svst = pb.sb("svst", [32, 128], F32)
    s1_t = pb.sb("s1_t", [128, 32], F32)
    s2_t = pb.sb("s2_t", [128, 32], F32)
    hs1_t = pb.sb("hs1_t", [128, 32], F32)
    hs2_t = pb.sb("hs2_t", [128, 32], F32)
    s3_bc = pb.sb("s3_bc", [128, D], F32)

    psA = pb.ps("psA", [128, 512], F32)
    psB = pb.ps("psB", [128, 512], F32)
    psD = pb.ps("psD", [128, 1024], F32)
    ps_tr = pb.ps("ps_tr", [128, 1024], BF16)
    ps_sc = pb.ps("ps_sc", [128, 32], F32)

    pb.start()
    _setup_common(pb)
    pb.vec_to_pcols(s1_d, 32, svst, ps_sc, s1_t)
    pb.vec_to_pcols(s2_d, 32, svst, ps_sc, s2_t)
    pb.vec_to_pcols(hs1_d, 32, svst, ps_sc, hs1_t)
    pb.vec_to_pcols(hs2_d, 32, svst, ps_sc, hs2_t)
    pb.bc_row(s3_bc, s3_d, D)

    def xnT(c):
        return BUF_A[:, 4096 * c:4096 * (c + 1)]

    for k in range(NT):
        pb.load(x_t[:, :], x_d[128 * k:128 * (k + 1), :])
        pb.ln(x_t[:, :], xn_bf[:, :], stats, scr[:, :], D)
        pb.tr_run([(xn_bf[:, 128 * c:128 * (c + 1)], xnT(c)[:, 128 * k:128 * (k + 1)])
                   for c in range(8)], ps_tr, pb.ident_bf[:, :])

    w1v = w1t_d.rearrange("(c p) f -> p c f", p=128)
    w2v = w2t_d.rearrange("(c p) f -> p c f", p=128)
    STG = [stagw, stagw2]
    WCB = [wc_bf, wc_bf2]
    H0R = [h0_bf, h0_bf2]

    def ph1_loads(j, bi, after_n):
        pb.load(STG[bi][:, 0:1024].rearrange("p (c f) -> p c f", c=8, f=128),
                w1v[:, :, 128 * j:128 * (j + 1)], after_n=after_n)
        pb.load(STG[bi][:, 1024:2048].rearrange("p (c f) -> p c f", c=8, f=128),
                w2v[:, :, 128 * j:128 * (j + 1)], after_n=after_n)

    ph1_loads(0, 0, pb.n)
    for j in range(32):
        bi = j % 2
        pb.cast(WCB[bi][:, 0:2048], STG[bi][:, 0:2048])
        ck = pb.n
        for tb in range(8):
            ts = slice(512 * tb, 512 * (tb + 1))
            pb.mm(psA[:, :], [(WCB[bi][:, 128 * c:128 * (c + 1)], xnT(c)[:, ts]) for c in range(8)])
            pb.mm(psB[:, :], [(WCB[bi][:, 1024 + 128 * c:1024 + 128 * (c + 1)], xnT(c)[:, ts]) for c in range(8)])
            pb.op("scalar", lambda e, j=j: e.activation(out=u1_bf[:, :], in_=psA[:, :],
                                                        func=AF.Silu, scale=s1_t[:, j:j + 1]))
            pb.op("scalar", lambda e, j=j: e.activation(out=scr5[:, :], in_=psB[:, :],
                                                        func=AF.Copy, scale=s2_t[:, j:j + 1]))
            hb = H0R[tb % 2]
            pb.op("vector", lambda e, hb=hb: e.tensor_mul(out=hb[:, :], in0=scr5[:, :], in1=u1_bf[:, :]),
                  swait=max(0, pb.s - 16))
            pb.store_async(H0_d[128 * j:128 * (j + 1), ts], hb[:, :])
        if j + 1 < 32:
            ph1_loads(j + 1, 1 - bi, ck)

    U1R = [u1_bf, u1_bf2]
    for (ht_d, hs_t, src_d, dst_d) in ((h1t_d, hs1_t, H0_d, H1_d),
                                       (h2t_d, hs2_t, H1_d, H2_d)):
        hv = ht_d.rearrange("(c p) f -> p c f", p=128)

        def w_loads(j, bi, after_n, hv=hv):
            for q in range(4):
                pb.load(STG[bi][:, 1024 * q:1024 * (q + 1)].rearrange("p (c f) -> p c f", c=8, f=128),
                        hv[:, 8 * q:8 * (q + 1), 128 * j:128 * (j + 1)], after_n=after_n)

        for tb in range(2):
            tbs = slice(2048 * tb, 2048 * (tb + 1))
            for c in range(32):
                dst = (BUF_A if c < 16 else BUF_B)
                cc = c % 16
                pb.load(dst[:, 2048 * cc:2048 * (cc + 1)], src_d[128 * c:128 * (c + 1), tbs])
            w_loads(0, 0, pb.n)
            for j in range(32):
                bi = j % 2
                pb.cast(WCB[bi][:, :], STG[bi][:, :])
                ck = pb.n
                for h in range(4):
                    hsl = slice(512 * h, 512 * (h + 1))
                    pairs = []
                    for c in range(32):
                        srcb = (BUF_A if c < 16 else BUF_B)
                        cc = c % 16
                        pairs.append((WCB[bi][:, 128 * c:128 * (c + 1)],
                                      srcb[:, 2048 * cc:2048 * cc + 2048][:, hsl]))
                    pb.mm(psA[:, :], pairs)
                    ub = U1R[h % 2]
                    pb.op("scalar", lambda e, j=j, hs_t=hs_t, ub=ub: e.activation(
                        out=ub[:, :], in_=psA[:, :], func=AF.Gelu, scale=hs_t[:, j:j + 1]),
                        swait=max(0, pb.s - 16))
                    pb.store_async(dst_d[128 * j:128 * (j + 1), tbs][:, hsl], ub[:, :])
                if j + 1 < 32:
                    w_loads(j + 1, 1 - bi, ck)

    w3v = w3t_d.rearrange("(c p) f -> p c f", p=128)
    for tb in range(4):
        tbs = slice(1024 * tb, 1024 * (tb + 1))
        for c in range(32):
            pb.load(BUF_A[:, 1024 * c:1024 * (c + 1)], H2_d[128 * c:128 * (c + 1), tbs])
        for c in range(32):
            pb.load(stag2[:, :], w3v[:, c, :])
            pb.cast(BUF_B[:, 1024 * c:1024 * (c + 1)], stag2[:, :])
        for k in range(8):
            ksl = slice(128 * k, 128 * (k + 1))
            pairs = []
            for c in range(32):
                for half in range(2):
                    pairs.append((BUF_A[:, 1024 * c:1024 * (c + 1)][:, ksl],
                                  BUF_B[:, 1024 * c + 512 * half:1024 * c + 512 * (half + 1)],
                                  psD[:, 512 * half:512 * (half + 1)]))
            pb.mm(None, pairs)
            gk = 128 * (8 * tb + k)
            pb.load(x_t[:, :], x_d[gk:gk + 128, :])
            pb.op("vector", lambda e: e.tensor_mul(out=scr[:, :], in0=psD[:, :], in1=s3_bc[:, :]))
            pb.op("vector", lambda e: e.tensor_add(out=scr[:, :], in0=scr[:, :], in1=x_t[:, :]))
            pb.store(y_d[gk:gk + 128, :], scr[:, :])

    pb.finish()
    return pb.nc


def build_l0ffn():
    pb = PB()
    x_d = pb.dram("x", [TOK, D], F32)
    wgt_d = pb.dram("wgt", [D, 8], F32)
    wfct_d = pb.dram("wfct", [8, D, 512], F32)
    wpjt_d = pb.dram("wpjt", [8, 512, D], F32)
    y_d = pb.dram("y", [TOK, D], F32, kind="ExternalOutput")

    WFC = pb.sb("WFC", [128, 32768], BF16)
    WPJ = pb.sb("WPJ", [128, 32768], BF16)
    stag = pb.sb("stag", [128, 4096], F32)
    XT = pb.sb("XT", [128, 4096], BF16)
    HT = pb.sb("HT", [128, 2048], BF16)
    wg_bf = pb.sb("wg_bf", [128, 64], BF16)
    x_t = pb.sb("x_t", [128, D], F32)
    scr = pb.sb("scr", [128, D], F32)
    xn_bf = pb.sb("xn_bf", [128, D], BF16)
    stats = pb.sb("stats", [128, 16], F32)
    Y = [pb.sb("Y", [128, D], F32) for _ in range(4)]
    G4 = pb.sb("G4", [128, 32], F32)
    rt = pb.sb("rt", [128, 8], F32)
    eq1 = pb.sb("eq1", [128, 8], F32)
    eq2 = pb.sb("eq2", [128, 8], F32)
    tmp8 = pb.sb("tmp8", [128, 8], F32)
    g1 = pb.sb("g1", [128, 8], F32)

    psH = pb.ps("psH", [128, 512], F32)
    psY = pb.ps("psY", [128, 1024], F32)
    psG = pb.ps("psG", [128, 8], F32)
    ps_tr = pb.ps("ps_tr", [128, 1024], BF16)

    pb.start()
    _setup_common(pb)
    pb.load(stag[:, 0:64].rearrange("p (c f) -> p c f", c=8, f=8),
            wgt_d.rearrange("(c p) f -> p c f", p=128))
    pb.cast(wg_bf[:, :], stag[:, 0:64])
    for e in range(8):
        pb.load(stag.rearrange("p (c f) -> p c f", c=8, f=512),
                wfct_d[e].rearrange("(c p) f -> p c f", p=128))
        pb.cast(WFC[:, 4096 * e:4096 * (e + 1)], stag[:, :])
        pb.load(stag.rearrange("p (c f) -> p c f", c=4, f=1024),
                wpjt_d[e].rearrange("(c p) f -> p c f", p=128))
        pb.cast(WPJ[:, 4096 * e:4096 * (e + 1)], stag[:, 0:4096])

    for b in range(8):
        for kt in range(4):
            k = 4 * b + kt
            pb.load(x_t[:, :], x_d[128 * k:128 * (k + 1), :])
            pb.ln(x_t[:, :], xn_bf[:, :], stats, scr[:, :], D)
            pb.tr_run([(xn_bf[:, 128 * c:128 * (c + 1)],
                        XT[:, 512 * c + 128 * kt:512 * c + 128 * (kt + 1)])
                       for c in range(8)], ps_tr, pb.ident_bf[:, :])
            pb.mm(psG[:, :], [(XT[:, 512 * c + 128 * kt:512 * c + 128 * (kt + 1)],
                               wg_bf[:, 8 * c:8 * (c + 1)]) for c in range(8)])
            pb.op("vector", lambda e: e.reduce_max(rt[:, 0:1], psG[:, :], axis=AX.X))
            pb.op("vector", lambda e: e.tensor_scalar(out=eq1[:, :], in0=psG[:, :], scalar1=rt[:, 0:1],
                                                      scalar2=None, op0=ALU.is_equal))
            pb.op("scalar", lambda e: e.mul(out=tmp8[:, :], in_=eq1[:, :], mul=-1e30))
            pb.op("vector", lambda e: e.tensor_add(out=tmp8[:, :], in0=psG[:, :], in1=tmp8[:, :]))
            pb.op("vector", lambda e: e.reduce_max(rt[:, 1:2], tmp8[:, :], axis=AX.X))
            pb.op("vector", lambda e: e.tensor_scalar(out=eq2[:, :], in0=tmp8[:, :], scalar1=rt[:, 1:2],
                                                      scalar2=None, op0=ALU.is_equal))
            pb.op("vector", lambda e: e.tensor_sub(out=rt[:, 2:3], in0=rt[:, 0:1], in1=rt[:, 1:2]))
            pb.op("scalar", lambda e: e.activation(out=rt[:, 3:4], in_=rt[:, 2:3], func=AF.Sigmoid))
            pb.op("scalar", lambda e: e.activation(out=rt[:, 4:5], in_=rt[:, 2:3], func=AF.Sigmoid, scale=-1.0))
            pb.op("vector", lambda e: e.tensor_scalar(out=g1[:, :], in0=eq1[:, :], scalar1=rt[:, 3:4],
                                                      scalar2=None, op0=ALU.mult))
            pb.op("vector", lambda e, kt=kt: e.tensor_scalar(out=G4[:, 8 * kt:8 * (kt + 1)], in0=eq2[:, :],
                                                             scalar1=rt[:, 4:5], scalar2=None, op0=ALU.mult))
            pb.op("vector", lambda e, kt=kt: e.tensor_add(out=G4[:, 8 * kt:8 * (kt + 1)],
                                                          in0=G4[:, 8 * kt:8 * (kt + 1)], in1=g1[:, :]))
            pb.cast(Y[kt][:, :], x_t[:, :])
        for e_ in range(8):
            for i in range(4):
                pb.mm(psH[:, :], [(WFC[:, (8 * e_ + c) * 512 + 128 * i:(8 * e_ + c) * 512 + 128 * (i + 1)],
                                   XT[:, 512 * c:512 * (c + 1)]) for c in range(8)])
                pb.op("scalar", lambda e, i=i: e.activation(out=HT[:, 512 * i:512 * (i + 1)],
                                                            in_=psH[:, :], func=AF.Gelu))
            for kt in range(4):
                pairs = []
                for i in range(4):
                    for half in range(2):
                        pairs.append((HT[:, 512 * i + 128 * kt:512 * i + 128 * (kt + 1)],
                                      WPJ[:, (4 * e_ + i) * 1024 + 512 * half:(4 * e_ + i) * 1024 + 512 * (half + 1)],
                                      psY[:, 512 * half:512 * (half + 1)]))
                pb.mm(None, pairs)
                pb.op("vector", lambda e, kt=kt, e_=e_: e.scalar_tensor_tensor(
                    out=Y[kt][:, :], in0=psY[:, :], scalar=G4[:, 8 * kt + e_:8 * kt + e_ + 1],
                    in1=Y[kt][:, :], op0=ALU.mult, op1=ALU.add))
        for kt in range(4):
            k = 4 * b + kt
            pb.store(y_d[128 * k:128 * (k + 1), :], Y[kt][:, :])

    pb.finish()
    return pb.nc


def build_l1ab():
    pb = PB()
    x_d = pb.dram("x", [TOK, D], F32)
    wq_d = pb.dram("wq", [D, D], F32)
    wk_d = pb.dram("wk", [D, D], F32)
    wv_d = pb.dram("wv", [D, D], F32)
    wo_d = pb.dram("wo", [D, D], F32)
    bq_d = pb.dram("bq", [D], F32)
    bk_d = pb.dram("bk", [D], F32)
    bv_d = pb.dram("bv", [D], F32)
    bo_d = pb.dram("bo", [D], F32)
    y_d = pb.dram("y", [TOK, D], F32, kind="ExternalOutput")

    S = 2048
    XT = pb.sb("XT", [128, 8 * S], BF16)
    QT = pb.sb("QT", [128, 8 * S], BF16)
    KT = pb.sb("KT", [128, 8 * S], BF16)
    VA = pb.sb("VA", [128, 16 * 1040], BF16)
    PT = pb.sb("PT", [128, 1024], BF16)
    WB = pb.sb("WB", [128, 8192], BF16)
    stag = pb.sb("stag", [128, 8192], F32)
    x_t = pb.sb("x_t", [128, D], F32)
    scr = pb.sb("scr", [128, D], F32)
    xn_bf = pb.sb("xn_bf", [128, D], BF16)
    stats = pb.sb("stats", [128, 16], F32)
    svst = pb.sb("svst", [32, 128], F32)
    bq_t = pb.sb("bq_t", [128, 8], F32)
    bk_t = pb.sb("bk_t", [128, 8], F32)
    bv_bc = pb.sb("bv_bc", [128, D], F32)
    bo_bc = pb.sb("bo_bc", [128, D], F32)
    ones_bf = pb.sb("ones_bf", [1, 128], BF16)
    dn = pb.sb("dn", [1, 1024], F32)
    dn_bf = pb.sb("dn_bf", [1, 1024], BF16)
    sc64 = pb.sb("sc64", [64, 1024], BF16)

    psS = pb.ps("psS", [128, 1024], F32)
    psN = pb.ps("psN", [65, 1024], F32)
    psP = pb.ps("psP", [128, 1024], F32)
    ps_tr = pb.ps("ps_tr", [128, 1024], BF16)
    ps_sc = pb.ps("ps_sc", [128, 32], F32)

    pb.start()
    _setup_common(pb)
    pb.vec_to_pcols(bq_d, 8, svst, ps_sc, bq_t)
    pb.vec_to_pcols(bk_d, 8, svst, ps_sc, bk_t)
    pb.bc_row(bv_bc, bv_d, D)
    pb.bc_row(bo_bc, bo_d, D)
    pb.op("vector", lambda e: e.memset(ones_bf[:, :], 1.0))

    def load_w(w_d):
        pb.load(stag.rearrange("p (c f) -> p c f", c=8, f=1024),
                w_d.rearrange("(c p) f -> p c f", p=128))
        pb.cast(WB[:, :], stag[:, :])

    for bb in range(2):
        xoff = 2048 * bb
        pb.op("vector", lambda e: e.memset(VA[:, :], 1.0))
        for k in range(16):
            pb.load(x_t[:, :], x_d[xoff + 128 * k:xoff + 128 * (k + 1), :])
            pb.ln(x_t[:, :], xn_bf[:, :], stats, scr[:, :], D)
            pb.tr_run([(xn_bf[:, 128 * c:128 * (c + 1)],
                        XT[:, S * c + 128 * k:S * c + 128 * (k + 1)])
                       for c in range(8)], ps_tr, pb.ident_bf[:, :])
        for (w_d, b_t, DST) in ((wq_d, bq_t, QT), (wk_d, bk_t, KT)):
            load_w(w_d)
            for j in range(8):
                for tq in range(4):
                    pb.mm(psP[:, 0:512],
                          [(WB[:, 1024 * c + 128 * j:1024 * c + 128 * (j + 1)],
                            XT[:, S * c + 512 * tq:S * c + 512 * (tq + 1)]) for c in range(8)])
                    pb.op("vector", lambda e, j=j, tq=tq, DST=DST, b_t=b_t: e.tensor_scalar(
                        out=DST[:, S * j + 512 * tq:S * j + 512 * (tq + 1)], in0=psP[:, 0:512],
                        scalar1=b_t[:, j:j + 1], scalar2=None, op0=ALU.add))
        load_w(wv_d)
        for k in range(16):
            pairs = []
            for c in range(8):
                for half in range(2):
                    pairs.append((XT[:, S * c + 128 * k:S * c + 128 * (k + 1)],
                                  WB[:, 1024 * c + 512 * half:1024 * c + 512 * (half + 1)],
                                  psP[:, 512 * half:512 * (half + 1)]))
            pb.mm(None, pairs)
            vk = VA[:, 1040 * k:1040 * (k + 1)].rearrange("p (h w) -> p h w", h=16, w=65)[:, :, 0:64]
            pb.op("vector", lambda e, vk=vk: e.tensor_tensor(
                out=vk, in0=psP.rearrange("p (h w) -> p h w", h=16, w=64),
                in1=bv_bc.rearrange("p (h w) -> p h w", h=16, w=64), op=ALU.add))
        AT = XT
        for h in range(16):
            hb = (h % 2) * 64
            j = h // 2
            for qh in range(2):
                qo = 1024 * qh
                for kt in range(16):
                    pb.mm(psS[:, :],
                          [(KT[hb:hb + 64, S * j + 128 * kt:S * j + 128 * (kt + 1)],
                            QT[hb:hb + 64, S * j + qo + 512 * tq:S * j + qo + 512 * (tq + 1)],
                            psS[:, 512 * tq:512 * (tq + 1)]) for tq in range(2)])
                    pb.op("scalar", lambda e: e.activation(out=PT[:, :], in_=psS[:, :],
                                                           func=AF.Exp, scale=0.125))
                    pb.mm(None,
                          [(VA[:, 1040 * kt + 65 * h:1040 * kt + 65 * (h + 1)],
                            PT[:, 512 * tq:512 * (tq + 1)],
                            psN[:, 512 * tq:512 * (tq + 1)]) for tq in range(2)],
                          first_start=(kt == 0), last_stop=(kt == 15))
                pb.op("vector", lambda e: e.reciprocal(dn[:, :], psN[64:65, :]))
                pb.cast(dn_bf[:, :], dn[:, :])
                pb.mm(psS[0:64, 0:1024], [(ones_bf[:, 0:64], dn_bf[:, 512 * tq:512 * (tq + 1)],
                                           psS[0:64, 512 * tq:512 * (tq + 1)]) for tq in range(2)])
                pb.cast(sc64[:, :], psN[0:64, :])
                pb.op("vector", lambda e, j=j, hb=hb, qo=qo: e.tensor_tensor(
                    out=AT[hb:hb + 64, S * j + qo:S * j + qo + 1024], in0=sc64[:, :],
                    in1=psS[0:64, 0:1024], op=ALU.mult))
        load_w(wo_d)
        for k in range(16):
            pairs = []
            for c in range(8):
                for half in range(2):
                    pairs.append((AT[:, S * c + 128 * k:S * c + 128 * (k + 1)],
                                  WB[:, 1024 * c + 512 * half:1024 * c + 512 * (half + 1)],
                                  psP[:, 512 * half:512 * (half + 1)]))
            pb.mm(None, pairs)
            pb.load(x_t[:, :], x_d[xoff + 128 * k:xoff + 128 * (k + 1), :])
            pb.op("vector", lambda e: e.tensor_add(out=scr[:, :], in0=psP[:, :], in1=bo_bc[:, :]))
            pb.op("vector", lambda e: e.tensor_add(out=scr[:, :], in0=scr[:, :], in1=x_t[:, :]))
            pb.store(y_d[xoff + 128 * k:xoff + 128 * (k + 1), :], scr[:, :])

    pb.finish()
    return pb.nc


def build_l0aa():
    pb = PB()
    S = 2048
    x_d = pb.dram("x", [TOK, D], F32)
    wdq_d = pb.dram("wdq", [D, 512], F32)
    wuq_d = pb.dram("wuq", [512, D], F32)
    wqr_d = pb.dram("wqr", [512, D], F32)
    wkr_d = pb.dram("wkr", [D, D], F32)
    wdkv_d = pb.dram("wdkv", [D, 512], F32)
    wuk_d = pb.dram("wuk", [512, D], F32)
    wuv_d = pb.dram("wuv", [512, D], F32)
    wo_d = pb.dram("wo", [D, D], F32)
    bdq_d = pb.dram("bdq", [512], F32)
    buq_d = pb.dram("buq", [D], F32)
    bqr_d = pb.dram("bqr", [D], F32)
    bkr_d = pb.dram("bkr", [D], F32)
    bdkv_d = pb.dram("bdkv", [512], F32)
    buk_d = pb.dram("buk", [D], F32)
    buv_d = pb.dram("buv", [D], F32)
    bo_d = pb.dram("bo", [D], F32)
    cos_d = pb.dram("cost", [S, 512], F32)
    sin_d = pb.dram("sint", [S, 512], F32)
    y_d = pb.dram("y", [TOK, D], F32, kind="ExternalOutput")

    XT = pb.sb("XT", [128, 8 * S], BF16)
    CC = pb.sb("CC", [128, 8 * S], BF16)
    QT = pb.sb("QT", [128, 8 * S], BF16)
    KT = pb.sb("KT", [128, 8 * S], BF16)
    VA = pb.sb("VA", [128, 16 * 1040], BF16)
    PT = pb.sb("PT", [128, 1024], BF16)
    WB = pb.sb("WB", [128, 8192], BF16)
    stag = pb.sb("stag", [128, 1024], F32)
    x_t = pb.sb("x_t", [128, D], F32)
    scr = pb.sb("scr", [128, D], F32)
    qro = pb.sb("qro", [128, D], F32)
    cos_tl = pb.sb("cos_tl", [128, 512], F32)
    sin_tl = pb.sb("sin_tl", [128, 512], F32)
    tmp5 = pb.sb("tmp5", [128, 512], F32)
    xn_bf = pb.sb("xn_bf", [128, D], BF16)
    stats = pb.sb("stats", [128, 16], F32)
    svst = stag[0:32, 0:128]
    bdq_t = pb.sb("bdq_t", [128, 4], F32)
    bdkv_t = pb.sb("bdkv_t", [128, 4], F32)
    buq_t = pb.sb("buq_t", [128, 8], F32)
    buk_t = pb.sb("buk_t", [128, 8], F32)
    b_bc = pb.sb("b_bc", [128, D], F32)
    bqr_bc = bkr_bc = buv_bc = bo_bc = b_bc
    ones_bf = pb.sb("ones_bf", [1, 128], BF16)
    dn = stag[0:1, 0:1024]
    dn_bf = XT[0:1, 0:1024]
    sc64 = XT[0:64, 2048:3072]

    psS = pb.ps("psS", [128, 1024], F32)
    psN = pb.ps("psN", [65, 1024], F32)
    psP = pb.ps("psP", [128, 1024], F32)
    ps_tr = pb.ps("ps_tr", [128, 1024], BF16)
    ps_sc = pb.ps("ps_sc", [128, 32], F32)

    pb.start()
    _setup_common(pb)
    pb.vec_to_pcols(bdq_d, 4, svst, ps_sc, bdq_t)
    pb.vec_to_pcols(bdkv_d, 4, svst, ps_sc, bdkv_t)
    pb.vec_to_pcols(buq_d, 8, svst, ps_sc, buq_t)
    pb.vec_to_pcols(buk_d, 8, svst, ps_sc, buk_t)
    pb.op("vector", lambda e: e.memset(ones_bf[:, :], 1.0))

    def load_w(w_d, rows, cols):
        nc_ = rows // 128
        for c in range(nc_):
            pb.load(stag[:, 0:cols], w_d[128 * c:128 * (c + 1), :])
            pb.cast(WB[:, cols * c:cols * (c + 1)], stag[:, 0:cols])

    def rope(ps_src, bias_bc, out_f32):
        pb.op("vector", lambda e: e.tensor_add(out=qro[:, :], in0=ps_src, in1=bias_bc[:, :]))
        o = pb.op
        o("vector", lambda e: e.tensor_mul(out=out_f32[:, 0:D:2], in0=qro[:, 0:D:2], in1=cos_tl[:, :]))
        o("vector", lambda e: e.tensor_mul(out=tmp5[:, :], in0=qro[:, 1:D:2], in1=sin_tl[:, :]))
        o("vector", lambda e: e.tensor_sub(out=out_f32[:, 0:D:2], in0=out_f32[:, 0:D:2], in1=tmp5[:, :]))
        o("vector", lambda e: e.tensor_mul(out=out_f32[:, 1:D:2], in0=qro[:, 1:D:2], in1=cos_tl[:, :]))
        o("vector", lambda e: e.tensor_mul(out=tmp5[:, :], in0=qro[:, 0:D:2], in1=sin_tl[:, :]))
        o("vector", lambda e: e.tensor_add(out=out_f32[:, 1:D:2], in0=out_f32[:, 1:D:2], in1=tmp5[:, :]))

    CQT = CC[:, 0:4 * S]
    CKT = CC[:, 4 * S:8 * S]

    for bb in range(2):
        xoff = 2048 * bb
        pb.op("vector", lambda e: e.memset(VA[:, :], 1.0))
        for k in range(16):
            pb.load(x_t[:, :], x_d[xoff + 128 * k:xoff + 128 * (k + 1), :])
            pb.ln(x_t[:, :], xn_bf[:, :], stats, scr[:, :], D)
            pb.tr_run([(xn_bf[:, 128 * c:128 * (c + 1)],
                        XT[:, S * c + 128 * k:S * c + 128 * (k + 1)])
                       for c in range(8)], ps_tr, pb.ident_bf[:, :])
        for (w_d, b_t, DSTb) in ((wdq_d, bdq_t, CQT), (wdkv_d, bdkv_t, CKT)):
            load_w(w_d, D, 512)
            for j in range(4):
                for tq in range(4):
                    pb.mm(psP[:, 0:512],
                          [(WB[:, 512 * c + 128 * j:512 * c + 128 * (j + 1)],
                            XT[:, S * c + 512 * tq:S * c + 512 * (tq + 1)]) for c in range(8)])
                    pb.op("vector", lambda e, j=j, tq=tq, DSTb=DSTb, b_t=b_t: e.tensor_scalar(
                        out=DSTb[:, S * j + 512 * tq:S * j + 512 * (tq + 1)], in0=psP[:, 0:512],
                        scalar1=b_t[:, j:j + 1], scalar2=None, op0=ALU.add))
        for (w_d, b_t, SRCb, DSTb) in ((wuq_d, buq_t, CQT, QT), (wuk_d, buk_t, CKT, KT)):
            load_w(w_d, 512, D)
            for j in range(8):
                for tq in range(4):
                    pb.mm(psP[:, 0:512],
                          [(WB[:, 1024 * c + 128 * j:1024 * c + 128 * (j + 1)],
                            SRCb[:, S * c + 512 * tq:S * c + 512 * (tq + 1)]) for c in range(4)])
                    pb.op("vector", lambda e, j=j, tq=tq, DSTb=DSTb, b_t=b_t: e.tensor_scalar(
                        out=DSTb[:, S * j + 512 * tq:S * j + 512 * (tq + 1)], in0=psP[:, 0:512],
                        scalar1=b_t[:, j:j + 1], scalar2=None, op0=ALU.add))
        for (w_d, b_d_, SRCb, nsrc, DSTb) in ((wqr_d, bqr_d, CQT, 4, QT),
                                              (wkr_d, bkr_d, XT, 8, KT)):
            pb.bc_row(b_bc, b_d_, D)
            load_w(w_d, 128 * nsrc, D)
            for k in range(16):
                pb.load(cos_tl[:, :], cos_d[128 * k:128 * (k + 1), :])
                pb.load(sin_tl[:, :], sin_d[128 * k:128 * (k + 1), :])
                pairs = []
                for c in range(nsrc):
                    for half in range(2):
                        pairs.append((SRCb[:, S * c + 128 * k:S * c + 128 * (k + 1)],
                                      WB[:, 1024 * c + 512 * half:1024 * c + 512 * (half + 1)],
                                      psP[:, 512 * half:512 * (half + 1)]))
                pb.mm(None, pairs)
                rope(psP[:, :], b_bc, scr)
                pb.cast(xn_bf[:, :], scr[:, :])
                pb.tr_run([(xn_bf[:, 128 * c:128 * (c + 1)],
                            DSTb[:, S * c + 128 * k:S * c + 128 * (k + 1)])
                           for c in range(8)], ps_tr, pb.ident_bf[:, :], evac="add")
        pb.bc_row(b_bc, buv_d, D)
        load_w(wuv_d, 512, D)
        for k in range(16):
            pairs = []
            for c in range(4):
                for half in range(2):
                    pairs.append((CKT[:, S * c + 128 * k:S * c + 128 * (k + 1)],
                                  WB[:, 1024 * c + 512 * half:1024 * c + 512 * (half + 1)],
                                  psP[:, 512 * half:512 * (half + 1)]))
            pb.mm(None, pairs)
            vk = VA[:, 1040 * k:1040 * (k + 1)].rearrange("p (h w) -> p h w", h=16, w=65)[:, :, 0:64]
            pb.op("vector", lambda e, vk=vk: e.tensor_tensor(
                out=vk, in0=psP.rearrange("p (h w) -> p h w", h=16, w=64),
                in1=buv_bc.rearrange("p (h w) -> p h w", h=16, w=64), op=ALU.add))
        AT = CC
        esc = 1.0 / float(np.sqrt(np.float32(16 + 512 + 512)))
        for h in range(16):
            hb = (h % 2) * 64
            j = h // 2
            for qh in range(2):
                qo = 1024 * qh
                for kt in range(16):
                    pb.mm(psS[:, :],
                          [(KT[hb:hb + 64, S * j + 128 * kt:S * j + 128 * (kt + 1)],
                            QT[hb:hb + 64, S * j + qo + 512 * tq:S * j + qo + 512 * (tq + 1)],
                            psS[:, 512 * tq:512 * (tq + 1)]) for tq in range(2)])
                    pb.op("scalar", lambda e: e.activation(out=PT[:, :], in_=psS[:, :],
                                                           func=AF.Exp, scale=esc))
                    pb.mm(None,
                          [(VA[:, 1040 * kt + 65 * h:1040 * kt + 65 * (h + 1)],
                            PT[:, 512 * tq:512 * (tq + 1)],
                            psN[:, 512 * tq:512 * (tq + 1)]) for tq in range(2)],
                          first_start=(kt == 0), last_stop=(kt == 15))
                pb.op("vector", lambda e: e.reciprocal(dn, psN[64:65, :]))
                pb.cast(dn_bf, dn)
                pb.mm(psS[0:64, 0:1024], [(ones_bf[:, 0:64], dn_bf[0:1, 512 * tq:512 * (tq + 1)],
                                           psS[0:64, 512 * tq:512 * (tq + 1)]) for tq in range(2)])
                pb.cast(sc64, psN[0:64, :])
                pb.op("vector", lambda e, j=j, hb=hb, qo=qo: e.tensor_tensor(
                    out=AT[hb:hb + 64, S * j + qo:S * j + qo + 1024], in0=sc64,
                    in1=psS[0:64, 0:1024], op=ALU.mult))
        pb.bc_row(b_bc, bo_d, D)
        load_w(wo_d, D, D)
        for k in range(16):
            pairs = []
            for c in range(8):
                for half in range(2):
                    pairs.append((AT[:, S * c + 128 * k:S * c + 128 * (k + 1)],
                                  WB[:, 1024 * c + 512 * half:1024 * c + 512 * (half + 1)],
                                  psP[:, 512 * half:512 * (half + 1)]))
            pb.mm(None, pairs)
            pb.load(x_t[:, :], x_d[xoff + 128 * k:xoff + 128 * (k + 1), :])
            pb.op("vector", lambda e: e.tensor_add(out=scr[:, :], in0=psP[:, :], in1=bo_bc[:, :]))
            pb.op("vector", lambda e: e.tensor_add(out=scr[:, :], in0=scr[:, :], in1=x_t[:, :]))
            pb.store(y_d[xoff + 128 * k:xoff + 128 * (k + 1), :], scr[:, :])

    pb.finish()
    return pb.nc


def build_l0ab(debug_u2=False):
    pb = PB()
    x_d = pb.dram("x", [TOK, D], F32)
    qwT_d = pb.dram("qwT", [D, 64], F32)
    kwT_d = pb.dram("kwT", [D, 64], F32)
    vwT_d = pb.dram("vwT", [D, D], F32)
    owT_d = pb.dram("owT", [D, D], F32)
    f1wT_d = pb.dram("f1wT", [D, 4096], F32)
    f2wT_d = pb.dram("f2wT", [4096, D], F32)
    ln1w_d = pb.dram("ln1w", [D], F32)
    ln1b_d = pb.dram("ln1b", [D], F32)
    ln2w_d = pb.dram("ln2w", [D], F32)
    ln2b_d = pb.dram("ln2b", [D], F32)
    bq_d = pb.dram("bq2", [64], F32)
    bk_d = pb.dram("bk2", [64], F32)
    bv_d = pb.dram("bv2", [D], F32)
    bo_d = pb.dram("bo2", [D], F32)
    f1b_d = pb.dram("f1b", [4096], F32)
    f2b_d = pb.dram("f2b", [D], F32)
    tril_d = pb.dram("tril", [128, 128], F32)
    y_d = pb.dram("y", [TOK, D], F32, kind="ExternalOutput")
    u2_d = pb.nc.dram_tensor("u2buf", [TOK, D], F32, kind="Internal").ap()

    AR = pb.sb("AR", [128, 49152], BF16)
    VW = AR[:, 0:8192]
    OW = AR[:, 8192:16384]
    XTc = AR[:, 16384:17408]
    Vbf = AR[:, 17408:18432]
    ATc = AR[:, 18432:19456]
    KVb = AR[:, 19456:20480]
    QW = AR[:, 20480:20992]
    KW = AR[:, 20992:21504]
    MISC = AR[:, 21504:22528]
    HT2 = AR[:, 0:16384]
    F2W = AR[:, 16384:49152]
    XN2T = pb.sb("XN2T", [128, 4096], BF16)

    x_t = pb.sb("x_t", [128, D], F32)
    u_t = pb.sb("u_t", [128, D], F32)
    scr = pb.sb("scr", [128, D], F32)
    scr2 = pb.sb("scr2", [128, D], F32)
    xn_bf = pb.sb("xn_bf", [128, D], BF16)
    stats = pb.sb("stats", [128, 16], F32)
    svst = pb.sb("svst", [32, 128], F32)
    stag = pb.sb("stag", [128, 1024], F32)
    ln1w_bc = pb.sb("ln1w_bc", [128, D], F32)
    ln1b_bc = pb.sb("ln1b_bc", [128, D], F32)
    ln2w_bc = pb.sb("ln2w_bc", [128, D], F32)
    ln2b_bc = pb.sb("ln2b_bc", [128, D], F32)
    bv_bc = pb.sb("bv_bc", [128, D], F32)
    bo_bc = pb.sb("bo_bc", [128, D], F32)
    f2b_bc = pb.sb("f2b_bc", [128, D], F32)
    tril_f = pb.sb("tril_f", [128, 128], F32)
    bq_t = pb.sb("bq_t", [64, 1], F32)
    bk_t = pb.sb("bk_t", [64, 1], F32)
    f1b_t = pb.sb("f1b_t", [128, 32], F32)
    qf = pb.sb("qf", [64, 128], F32)
    kf = pb.sb("kf", [64, 128], F32)
    ef = pb.sb("ef", [64, 128], F32)
    amf = pb.sb("amf", [128, 128], F32)
    dni = pb.sb("dni", [128, 4], F32)
    KVs = pb.sb("KVs", [64, 1024], F32)
    Ksf = pb.sb("Ksf", [64, 4], F32)
    Ksb = pb.sb("Ksb", [64, 1], BF16)
    onesc = pb.sb("onesc", [128, 1], BF16)
    eps6 = pb.sb("eps6", [128, 1], F32)
    wjc = pb.sb("wjc", [128, 1024], BF16)
    wjc2 = pb.sb("wjc2", [128, 1024], BF16)
    stagf2 = pb.sb("stagf2", [128, 1024], F32)

    psV = pb.ps("psV", [128, 1024], F32)
    psN2 = pb.ps("psN2", [128, 1024], F32)
    KVps = pb.ps("KVps", [64, 1024], F32)
    psM = pb.ps("psM", [128, 512], F32)
    ps_trb = pb.ps("ps_trb", [128, 1024], BF16)

    pb.start()
    _setup_common(pb)
    for (d, t) in ((ln1w_d, ln1w_bc), (ln1b_d, ln1b_bc), (ln2w_d, ln2w_bc),
                   (ln2b_d, ln2b_bc), (bv_d, bv_bc), (bo_d, bo_bc), (f2b_d, f2b_bc)):
        pb.bc_row(t, d, D)
    pb.load(tril_f[:, :], tril_d[:, :])
    pb.load(bq_t[:, :], bq_d[:, None])
    pb.load(bk_t[:, :], bk_d[:, None])
    pb.vec_to_pcols(f1b_d, 32, svst, psM[:, 0:512], f1b_t)
    pb.op("vector", lambda e: e.memset(onesc[:, :], 1.0))
    pb.op("vector", lambda e: e.memset(eps6[:, :], 1e-6))
    for c in range(8):
        pb.load(stag[:, 0:64], qwT_d[128 * c:128 * (c + 1), :])
        pb.cast(QW[:, 64 * c:64 * (c + 1)], stag[:, 0:64])
        pb.load(stag[:, 0:64], kwT_d[128 * c:128 * (c + 1), :])
        pb.cast(KW[:, 64 * c:64 * (c + 1)], stag[:, 0:64])
        pb.load(stag[:, :], vwT_d[128 * c:128 * (c + 1), :])
        pb.cast(VW[:, 1024 * c:1024 * (c + 1)], stag[:, :])
        pb.load(stag[:, :], owT_d[128 * c:128 * (c + 1), :])
        pb.cast(OW[:, 1024 * c:1024 * (c + 1)], stag[:, :])

    Qbf = MISC[0:64, 0:128]
    Kbf = MISC[0:64, 128:256]
    Amb = MISC[:, 256:384]
    AmT = MISC[:, 384:512]

    for bb in range(2):
        xoff = 2048 * bb
        pb.op("vector", lambda e: e.memset(KVs[:, :], 0.0))
        pb.op("vector", lambda e: e.memset(Ksf[:, :], 0.0))
        for k in range(16):
            pb.load(x_t[:, :], x_d[xoff + 128 * k:xoff + 128 * (k + 1), :])
            pb.ln(x_t[:, :], u_t[:, :], stats, scr[:, :], D)
            pb.ln(u_t[:, :], scr2[:, :], stats, scr[:, :], D)
            pb.op("vector", lambda e: e.tensor_mul(out=scr2[:, :], in0=scr2[:, :], in1=ln1w_bc[:, :]))
            pb.op("vector", lambda e: e.tensor_add(out=scr2[:, :], in0=scr2[:, :], in1=ln1b_bc[:, :]))
            pb.cast(xn_bf[:, :], scr2[:, :])
            pb.tr_run([(xn_bf[:, 128 * c:128 * (c + 1)], XTc[:, 128 * c:128 * (c + 1)])
                       for c in range(8)], ps_trb, pb.ident_bf[:, :])
            for (WW, b_t, dst) in ((QW, bq_t, qf), (KW, bk_t, kf)):
                pb.mm(psM[0:64, 0:128],
                      [(WW[:, 64 * c:64 * (c + 1)], XTc[:, 128 * c:128 * (c + 1)],
                        psM[0:64, 0:128]) for c in range(8)])
                pb.op("vector", lambda e, b_t=b_t, dst=dst: e.tensor_scalar(
                    out=dst[:, :], in0=psM[0:64, 0:128], scalar1=b_t[:, 0:1],
                    scalar2=None, op0=ALU.add))
                pb.op("vector", lambda e, dst=dst: e.tensor_scalar(
                    out=ef[:, :], in0=dst[:, :], scalar1=0.0, scalar2=None, op0=ALU.min))
                pb.op("scalar", lambda e: e.activation(out=ef[:, :], in_=ef[:, :], func=AF.Exp))
                pb.op("scalar", lambda e, dst=dst: e.activation(out=dst[:, :], in_=dst[:, :], func=AF.Relu))
                pb.op("vector", lambda e, dst=dst: e.tensor_add(out=dst[:, :], in0=dst[:, :], in1=ef[:, :]))
            pb.cast(Qbf, qf[:, :])
            pb.cast(Kbf, kf[:, :])
            pairs = []
            for c in range(8):
                for half in range(2):
                    pairs.append((XTc[:, 128 * c:128 * (c + 1)],
                                  VW[:, 1024 * c + 512 * half:1024 * c + 512 * (half + 1)],
                                  psV[:, 512 * half:512 * (half + 1)]))
            pb.mm(None, pairs)
            pb.op("vector", lambda e: e.tensor_add(out=scr2[:, :], in0=psV[:, :], in1=bv_bc[:, :]))
            pb.cast(Vbf[:, :], scr2[:, :])
            pb.mm(psM[:, 0:128], [(Qbf, Kbf, psM[:, 0:128])])
            pb.op("vector", lambda e: e.tensor_mul(out=amf[:, :], in0=psM[:, 0:128], in1=tril_f[:, :]))
            pb.op("vector", lambda e: e.reduce_sum(dni[:, 0:1], amf[:, :], axis=AX.X))
            pb.cast(Amb, amf[:, :])
            pb.tr(ps_trb[:, 0:128], Amb, AmT, pb.ident_bf[:, :])
            pairs = [(AmT, Vbf[:, 0:512], psN2[:, 0:512]),
                     (AmT, Vbf[:, 512:1024], psN2[:, 512:1024])]
            if k > 0:
                pairs += [(Qbf, KVb[0:64, 0:512], psN2[:, 0:512]),
                          (Qbf, KVb[0:64, 512:1024], psN2[:, 512:1024]),
                          (Qbf, Ksb[:, 0:1], psM[:, 128:129])]
            pb.mm(None, pairs)
            KiT_ap = MISC[:, 512:576]
            pb.op("tensor", lambda e: e.transpose(ps_trb[:, 0:64], Kbf, pb.ident_bf[0:64, 0:64]))
            pb.op("scalar", lambda e, KiT_ap=KiT_ap: e.copy(out=KiT_ap, in_=ps_trb[:, 0:64]))
            pb.mm(None, [(KiT_ap, Vbf[:, 0:512], KVps[:, 0:512]),
                         (KiT_ap, Vbf[:, 512:1024], KVps[:, 512:1024]),
                         (KiT_ap, onesc[:, 0:1], psM[0:64, 256:257])])
            pb.op("vector", lambda e: e.tensor_add(out=KVs[:, :], in0=KVs[:, :], in1=KVps[:, :]))
            pb.op("vector", lambda e: e.tensor_add(out=Ksf[:, 0:1], in0=Ksf[:, 0:1], in1=psM[0:64, 256:257]))
            pb.cast(KVb[0:64, :], KVs[:, :])
            pb.cast(Ksb[:, :], Ksf[:, 0:1])
            if k > 0:
                pb.op("vector", lambda e: e.tensor_add(out=dni[:, 1:2], in0=psM[:, 128:129], in1=dni[:, 0:1]))
            else:
                pb.op("scalar", lambda e: e.copy(out=dni[:, 1:2], in_=dni[:, 0:1]))
            pb.op("vector", lambda e: e.tensor_add(out=dni[:, 1:2], in0=dni[:, 1:2], in1=eps6[:, 0:1]))
            pb.op("vector", lambda e: e.reciprocal(dni[:, 2:3], dni[:, 1:2]))
            pb.op("vector", lambda e: e.tensor_scalar(out=scr2[:, :], in0=psN2[:, :],
                                                      scalar1=dni[:, 2:3], scalar2=None, op0=ALU.mult))
            pb.cast(xn_bf[:, :], scr2[:, :])
            pb.tr_run([(xn_bf[:, 128 * c:128 * (c + 1)], ATc[:, 128 * c:128 * (c + 1)])
                       for c in range(8)], ps_trb, pb.ident_bf[:, :])
            pairs = []
            for c in range(8):
                for half in range(2):
                    pairs.append((ATc[:, 128 * c:128 * (c + 1)],
                                  OW[:, 1024 * c + 512 * half:1024 * c + 512 * (half + 1)],
                                  psV[:, 512 * half:512 * (half + 1)]))
            pb.mm(None, pairs)
            pb.op("vector", lambda e: e.tensor_add(out=scr2[:, :], in0=psV[:, :], in1=bo_bc[:, :]))
            pb.op("vector", lambda e: e.tensor_add(out=scr2[:, :], in0=scr2[:, :], in1=u_t[:, :]))
            pb.store(u2_d[xoff + 128 * k:xoff + 128 * (k + 1), :], scr2[:, :])
            if debug_u2:
                pb.store(y_d[xoff + 128 * k:xoff + 128 * (k + 1), :], scr2[:, :])

    if debug_u2:
        pb.finish()
        return pb.nc
    f1v = f1wT_d.rearrange("(c p) f -> p c f", p=128)
    for c in range(32):
        pb.load(stag[:, :], f2wT_d[128 * c:128 * (c + 1), :])
        pb.cast(F2W[:, 1024 * c:1024 * (c + 1)], stag[:, :])
    for blk in range(8):
        for kt in range(4):
            k = 4 * blk + kt
            pb.load(u_t[:, :], u2_d[128 * k:128 * (k + 1), :])
            pb.ln(u_t[:, :], scr2[:, :], stats, scr[:, :], D)
            pb.op("vector", lambda e: e.tensor_mul(out=scr2[:, :], in0=scr2[:, :], in1=ln2w_bc[:, :]))
            pb.op("vector", lambda e: e.tensor_add(out=scr2[:, :], in0=scr2[:, :], in1=ln2b_bc[:, :]))
            pb.cast(xn_bf[:, :], scr2[:, :])
            pb.tr_run([(xn_bf[:, 128 * c:128 * (c + 1)],
                        XN2T[:, 512 * c + 128 * kt:512 * c + 128 * (kt + 1)])
                       for c in range(8)], ps_trb, pb.ident_bf[:, :])
        STGF = [stag, stagf2]
        WJC = [wjc, wjc2]
        pb.load(STGF[0][:, :].rearrange("p (c f) -> p c f", c=8, f=128),
                f1v[:, :, 0:128], after_n=pb.n)
        for j in range(32):
            bi = j % 2
            pb.cast(WJC[bi][:, :], STGF[bi][:, :])
            ck = pb.n
            pb.mm(psM[:, 0:512],
                  [(WJC[bi][:, 128 * c:128 * (c + 1)], XN2T[:, 512 * c:512 * (c + 1)])
                   for c in range(8)])
            pb.op("scalar", lambda e, j=j: e.activation(
                out=HT2[:, 512 * j:512 * (j + 1)], in_=psM[:, 0:512], func=AF.Gelu,
                bias=f1b_t[:, j:j + 1]))
            if j + 1 < 32:
                pb.load(STGF[1 - bi][:, :].rearrange("p (c f) -> p c f", c=8, f=128),
                        f1v[:, :, 128 * (j + 1):128 * (j + 2)], after_n=ck)
        for kt in range(4):
            k = 4 * blk + kt
            pairs = []
            for c in range(32):
                for half in range(2):
                    pairs.append((HT2[:, 512 * c + 128 * kt:512 * c + 128 * (kt + 1)],
                                  F2W[:, 1024 * c + 512 * half:1024 * c + 512 * (half + 1)],
                                  psV[:, 512 * half:512 * (half + 1)]))
            pb.mm(None, pairs)
            pb.load(u_t[:, :], u2_d[128 * k:128 * (k + 1), :])
            pb.load(x_t[:, :], x_d[128 * k:128 * (k + 1), :])
            pb.op("vector", lambda e: e.tensor_add(out=scr[:, :], in0=psV[:, :], in1=f2b_bc[:, :]))
            pb.op("vector", lambda e: e.tensor_add(out=scr[:, :], in0=scr[:, :], in1=u_t[:, :]))
            pb.op("vector", lambda e: e.tensor_add(out=scr[:, :], in0=scr[:, :], in1=x_t[:, :]))
            pb.store(y_d[128 * k:128 * (k + 1), :], scr[:, :])

    pb.finish()
    return pb.nc


def build_l1aa():
    pb = PB()
    x_d = pb.dram("x", [TOK, D], F32)
    winT_d = pb.dram("winT", [D, 3072], F32)
    binp_d = pb.dram("binp", [3072], F32)
    woutT_d = pb.dram("woutT", [D, D], F32)
    bout_d = pb.dram("bout", [D], F32)
    y_d = pb.dram("y", [TOK, D], F32, kind="ExternalOutput")

    WIN = pb.sb("WIN", [128, 24576], BF16)
    OWT = pb.sb("OWT", [128, 8192], BF16)
    QL = pb.sb("QL", [128, 16384], BF16)
    KL = pb.sb("KL", [128, 16384], BF16)
    VL = pb.sb("VL", [128, 16384], BF16)
    binp_bc = pb.sb("binp_bc", [128, 3072], F32)
    bout_bc = pb.sb("bout_bc", [128, D], F32)
    stag = pb.sb("stag", [128, 1024], F32)
    XTc = pb.sb("XTc", [128, 1024], BF16)
    CXT = pb.sb("CXT", [128, 1024], BF16)
    x_t = pb.sb("x_t", [128, D], F32)
    scr = pb.sb("scr", [128, D], F32)
    tmp = pb.sb("tmp", [128, D], F32)
    CX = pb.sb("CX", [128, D], F32)
    xn_bf = pb.sb("xn_bf", [128, D], BF16)
    SC = pb.sb("SC", [128, 256], F32)
    P = pb.sb("P", [128, 256], F32)
    lden = pb.sb("lden", [128, 16], F32)
    rl = pb.sb("rl", [128, 16], F32)
    stats = pb.sb("stats", [128, 16], F32)

    psL = pb.ps("psL", [128, 1536], F32)
    psO = pb.ps("psO", [128, 1024], F32)
    ps_tr = pb.ps("ps_tr", [128, 1024], BF16)

    pb.start()
    _setup_common(pb)
    pb.bc_row(binp_bc, binp_d, 3072)
    pb.bc_row(bout_bc, bout_d, D)
    for c in range(8):
        for q in range(3):
            pb.load(stag[:, :], winT_d[128 * c:128 * (c + 1), 1024 * q:1024 * (q + 1)])
            pb.cast(WIN[:, 3072 * c + 1024 * q:3072 * c + 1024 * (q + 1)], stag[:, :])
        pb.load(stag[:, :], woutT_d[128 * c:128 * (c + 1), :])
        pb.cast(OWT[:, 1024 * c:1024 * (c + 1)], stag[:, :])

    qpat = lambda ap, off: ap.rearrange("p (h z j) -> p h z j", h=8, z=3, j=64)[:, :, off, :]

    for hf in range(2):
        for s in range(16):
            i = 2 * s + hf
            pb.load(x_t[:, :], x_d[128 * i:128 * (i + 1), :])
            pb.ln(x_t[:, :], xn_bf[:, :], stats, scr[:, :], D)
            pb.tr_run([(xn_bf[:, 128 * c:128 * (c + 1)], XTc[:, 128 * c:128 * (c + 1)])
                       for c in range(8)], ps_tr, pb.ident_bf[:, :])
            for fh in range(2):
                pairs = []
                for c in range(8):
                    for q in range(3):
                        pairs.append((XTc[:, 128 * c:128 * (c + 1)],
                                      WIN[:, 3072 * c + 1536 * fh + 512 * q:3072 * c + 1536 * fh + 512 * (q + 1)],
                                      psL[:, 512 * q:512 * (q + 1)]))
                pb.mm(None, pairs)
                for (off, DST) in ((0, QL), (1, KL), (2, VL)):
                    pb.op("vector", lambda e, off=off, DST=DST, s=s, fh=fh: e.tensor_tensor(
                        out=DST[:, 1024 * s + 512 * fh:1024 * s + 512 * fh + 512].rearrange(
                            "p (h j) -> p h j", h=8, j=64),
                        in0=qpat(psL, off),
                        in1=qpat(binp_bc[:, 1536 * fh:1536 * (fh + 1)], off), op=ALU.add))
        for s in range(16):
            i = 2 * s + hf
            sc_fs = []
            for t in range(16):
                sc_fs.append(lambda e, s=s, t=t: e.tensor_mul(
                    out=tmp[:, :], in0=QL[:, 1024 * s:1024 * (s + 1)],
                    in1=KL[:, 1024 * t:1024 * (t + 1)]))
                sc_fs.append(lambda e, t=t: e.reduce_sum(
                    SC[:, t:256:16], tmp.rearrange("p (h o) -> p h o", h=16, o=64), axis=AX.X))
            pb.group("vector", sc_fs)
            pb.op("scalar", lambda e: e.activation(out=P[:, :], in_=SC[:, :], func=AF.Exp, scale=0.125))
            pb.op("vector", lambda e: e.reduce_sum(
                lden[:, :], P.rearrange("p (h t) -> p h t", h=16, t=16), axis=AX.X))
            pb.op("vector", lambda e: e.reciprocal(rl[:, :], lden[:, :]))
            cx_fs = []
            for t in range(16):
                pbc = P[:, t:256:16].rearrange("p (h o) -> p h o", h=16, o=1).broadcast_to([128, 16, 64])
                dst = CX if t == 0 else tmp
                cx_fs.append(lambda e, t=t, pbc=pbc, dst=dst: e.tensor_tensor(
                    out=dst.rearrange("p (h o) -> p h o", h=16, o=64), in0=pbc,
                    in1=VL[:, 1024 * t:1024 * (t + 1)].rearrange("p (h o) -> p h o", h=16, o=64),
                    op=ALU.mult))
                if t > 0:
                    cx_fs.append(lambda e: e.tensor_add(out=CX[:, :], in0=CX[:, :], in1=tmp[:, :]))
            rbc = rl.rearrange("p (h o) -> p h o", h=16, o=1).broadcast_to([128, 16, 64])
            cx_fs.append(lambda e, rbc=rbc: e.tensor_tensor(
                out=CX.rearrange("p (h o) -> p h o", h=16, o=64),
                in0=CX.rearrange("p (h o) -> p h o", h=16, o=64), in1=rbc, op=ALU.mult))
            pb.group("vector", cx_fs)
            pb.cast(xn_bf[:, :], CX[:, :])
            pb.tr_run([(xn_bf[:, 128 * c:128 * (c + 1)], CXT[:, 128 * c:128 * (c + 1)])
                       for c in range(8)], ps_tr, pb.ident_bf[:, :])
            pairs = []
            for c in range(8):
                for half in range(2):
                    pairs.append((CXT[:, 128 * c:128 * (c + 1)],
                                  OWT[:, 1024 * c + 512 * half:1024 * c + 512 * (half + 1)],
                                  psO[:, 512 * half:512 * (half + 1)]))
            pb.mm(None, pairs)
            pb.load(x_t[:, :], x_d[128 * i:128 * (i + 1), :])
            pb.op("vector", lambda e: e.tensor_add(out=scr[:, :], in0=psO[:, :], in1=bout_bc[:, :]))
            pb.op("vector", lambda e: e.tensor_add(out=scr[:, :], in0=scr[:, :], in1=x_t[:, :]))
            pb.store(y_d[128 * i:128 * (i + 1), :], scr[:, :])

    pb.finish()
    return pb.nc


def run_l1aa(x, inp):
    nc = _get("l1aa", build_l1aa)
    shared = {
        "winT": _t(inp["L1_attn_a_in_proj_weight"]),
        "binp": np.asarray(inp["L1_attn_a_in_proj_bias"], np.float32),
        "woutT": _t(inp["L1_attn_a_out_proj_weight"]),
        "bout": np.asarray(inp["L1_attn_a_out_proj_bias"], np.float32),
        "ident": IDENT,
    }
    shards = _shard_seq(x)
    in_maps = [dict(shared, x=s) for s in shards]
    return _unshard_seq(_run(nc, in_maps))


def run_l0aa(x, inp):
    nc = _get("l0aa", build_l0aa)
    pos = np.arange(2048, dtype=np.float32)[:, None]
    ids = np.arange(512, dtype=np.float32)
    theta = (10000.0 ** (-2.0 * ids)) / 512.0
    r = pos * theta[None, :]
    shared = {
        "wdq": np.asarray(inp["L0_attn_a_Wd_q_w"], np.float32),
        "wuq": np.asarray(inp["L0_attn_a_Wu_q_w"], np.float32),
        "wqr": np.asarray(inp["L0_attn_a_Wq_r_w"], np.float32),
        "wkr": np.asarray(inp["L0_attn_a_Wk_r_w"], np.float32),
        "wdkv": np.asarray(inp["L0_attn_a_Wd_kv_w"], np.float32),
        "wuk": np.asarray(inp["L0_attn_a_Wu_k_w"], np.float32),
        "wuv": np.asarray(inp["L0_attn_a_Wu_v_w"], np.float32),
        "wo": np.asarray(inp["L0_attn_a_Wo_w"], np.float32),
        "bdq": np.asarray(inp["L0_attn_a_Wd_q_b"], np.float32),
        "buq": np.asarray(inp["L0_attn_a_Wu_q_b"], np.float32),
        "bqr": np.asarray(inp["L0_attn_a_Wq_r_b"], np.float32),
        "bkr": np.asarray(inp["L0_attn_a_Wk_r_b"], np.float32),
        "bdkv": np.asarray(inp["L0_attn_a_Wd_kv_b"], np.float32),
        "buk": np.asarray(inp["L0_attn_a_Wu_k_b"], np.float32),
        "buv": np.asarray(inp["L0_attn_a_Wu_v_b"], np.float32),
        "bo": np.asarray(inp["L0_attn_a_Wo_b"], np.float32),
        "cost": np.cos(r).astype(np.float32),
        "sint": np.sin(r).astype(np.float32),
        "ident": IDENT,
    }
    shards = _shard_batch(x)
    in_maps = [dict(shared, x=s) for s in shards]
    return _unshard_batch(_run(nc, in_maps))


def run_l0ab(x, inp):
    nc = _get("l0ab", build_l0ab)
    shared = {
        "qwT": _t(inp["L0_attn_b_q_w"]),
        "kwT": _t(inp["L0_attn_b_k_w"]),
        "vwT": _t(inp["L0_attn_b_v_w"]),
        "owT": _t(inp["L0_attn_b_o_w"]),
        "f1wT": _t(inp["L0_attn_b_f1_w"]),
        "f2wT": _t(inp["L0_attn_b_f2_w"]),
        "ln1w": np.asarray(inp["L0_attn_b_ln1_w"], np.float32),
        "ln1b": np.asarray(inp["L0_attn_b_ln1_b"], np.float32),
        "ln2w": np.asarray(inp["L0_attn_b_ln2_w"], np.float32),
        "ln2b": np.asarray(inp["L0_attn_b_ln2_b"], np.float32),
        "bq2": np.asarray(inp["L0_attn_b_q_b"], np.float32),
        "bk2": np.asarray(inp["L0_attn_b_k_b"], np.float32),
        "bv2": np.asarray(inp["L0_attn_b_v_b"], np.float32),
        "bo2": np.asarray(inp["L0_attn_b_o_b"], np.float32),
        "f1b": np.asarray(inp["L0_attn_b_f1_b"], np.float32),
        "f2b": np.asarray(inp["L0_attn_b_f2_b"], np.float32),
        "tril": np.tril(np.ones((128, 128), np.float32)),
        "ident": IDENT,
    }
    shards = _shard_batch(x)
    in_maps = [dict(shared, x=s) for s in shards]
    return _unshard_batch(_run(nc, in_maps))


def run_l0ffn(x, inp):
    nc = _get("l0ffn", build_l0ffn)
    wfc = np.asarray(inp["L0_ffn_w_fc"], np.float32)
    wpj = np.asarray(inp["L0_ffn_w_proj"], np.float32)
    shared = {
        "wgt": _t(inp["L0_ffn_w_gate"]),
        "wfct": np.ascontiguousarray(wfc.transpose(0, 2, 1)),
        "wpjt": np.ascontiguousarray(wpj.transpose(0, 2, 1)),
        "ident": IDENT,
    }
    shards = _shard_batch(x)
    in_maps = [dict(shared, x=s) for s in shards]
    return _unshard_batch(_run(nc, in_maps))


def run_l1ab(x, inp):
    nc = _get("l1ab", build_l1ab)
    shared = {
        "wq": np.asarray(inp["L1_attn_b_Wq"], np.float32),
        "wk": np.asarray(inp["L1_attn_b_Wk"], np.float32),
        "wv": np.asarray(inp["L1_attn_b_Wv"], np.float32),
        "wo": np.asarray(inp["L1_attn_b_Wo"], np.float32),
        "bq": np.asarray(inp["L1_attn_b_bq"], np.float32),
        "bk": np.asarray(inp["L1_attn_b_bk"], np.float32),
        "bv": np.asarray(inp["L1_attn_b_bv"], np.float32),
        "bo": np.asarray(inp["L1_attn_b_bo"], np.float32),
        "ident": IDENT,
    }
    shards = _shard_batch(x)
    in_maps = [dict(shared, x=s) for s in shards]
    return _unshard_batch(_run(nc, in_maps))


_BUILT = {}


def _get(name, builder):
    if name not in _BUILT:
        _BUILT[name] = builder()
    return _BUILT[name]


def _shard_batch(x):
    return [np.ascontiguousarray(s.reshape(TOK, D)) for s in np.split(x, NCORES, axis=0)]


def _unshard_batch(ys):
    return np.concatenate([y.reshape(2, 2048, D) for y in ys], axis=0)


def _shard_seq(x):
    return [np.ascontiguousarray(s.reshape(TOK, D)) for s in np.split(x, NCORES, axis=1)]


def _unshard_seq(ys):
    return np.concatenate([y.reshape(16, 256, D) for y in ys], axis=1)


def _run(nc, in_maps):
    res = run_bass_kernel_spmd(nc, in_maps, core_ids=list(range(NCORES)))
    return [r["y"] for r in res.results]


def _t(a):
    return np.ascontiguousarray(np.asarray(a, np.float32).T)


def run_l1ffn(x, inp):
    nc = _get("l1ffn", build_l1ffn)
    shared = {
        "w1t": _t(inp["L1_ffn_w1_w"]), "w2t": _t(inp["L1_ffn_w2_w"]),
        "h1t": _t(inp["L1_ffn_h1_w"]), "h2t": _t(inp["L1_ffn_h2_w"]),
        "w3t": _t(inp["L1_ffn_w3_w"]),
        "s1": np.asarray(inp["L1_ffn_w1_s"], np.float32),
        "s2": np.asarray(inp["L1_ffn_w2_s"], np.float32),
        "hs1": np.asarray(inp["L1_ffn_h1_s"], np.float32),
        "hs2": np.asarray(inp["L1_ffn_h2_s"], np.float32),
        "s3": np.asarray(inp["L1_ffn_w3_s"], np.float32),
        "ident": IDENT,
    }
    shards = _shard_batch(x)
    in_maps = [dict(shared, x=s) for s in shards]
    return _unshard_batch(_run(nc, in_maps))


def kernel(**inp):
    x = np.asarray(inp["L0_attn_a_x"], np.float32)
    x = run_l0aa(x, inp)
    x = run_l0ffn(x, inp)
    x = run_l0ab(x, inp)
    x = run_l0ffn(x, inp)
    x = run_l1aa(x, inp)
    x = run_l1ffn(x, inp)
    x = run_l1ab(x, inp)
    x = run_l1ffn(x, inp)
    return x
